# version 28
# baseline (speedup 1.0000x reference)
import numpy as np
import ml_dtypes
from contextlib import ExitStack

import concourse.bass as bass
import concourse.mybir as mybir
from concourse.bass_utils import run_bass_kernel_spmd

F32 = mybir.dt.float32
BF16 = mybir.dt.bfloat16
AF = mybir.ActivationFunctionType
ALU = mybir.AluOpType
AX = mybir.AxisListType

D = 1024
T = 2048
C8 = 8
C16 = 16
NT = 512
NTILES = T // NT
EPS = 1e-5
PAD = 4
XW = T + PAD

BF = ml_dtypes.bfloat16


def _prep_lhsT(w):
    IN, OUT = w.shape
    return np.ascontiguousarray(w.reshape(IN // 128, 128, OUT).transpose(1, 0, 2).reshape(128, -1))


def _prep_out_blocked(w, blk=128):
    IN, OUT = w.shape
    return np.ascontiguousarray(
        w.reshape(IN // 128, 128, OUT // blk, blk).transpose(2, 1, 0, 3).reshape(OUT // blk, 128, -1))


def _cols(v):
    return np.ascontiguousarray(v.reshape(-1, 128).T)


def host_prep(I):
    p = {}
    p["wr"] = _prep_lhsT(np.ascontiguousarray(I["L0_ffn_W_r"].T))
    p["wk"] = _prep_lhsT(np.ascontiguousarray(I["L0_ffn_W_k"].T))
    p["wv"] = _prep_lhsT(np.ascontiguousarray(I["L0_ffn_W_v"].T))
    p["wq"] = _prep_out_blocked(I["L0_attn_wq"])
    p["wkk"] = _prep_out_blocked(I["L0_attn_wk"])
    p["wvv"] = _prep_out_blocked(I["L0_attn_wv"])
    p["wo"] = _prep_lhsT(I["L0_attn_wo"])
    p["peT"] = np.ascontiguousarray(I["L0_attn_pe"][0, :T, :].T)
    p["win"] = _prep_out_blocked(I["L1_attn_W_in"])
    p["wbc"] = _prep_out_blocked(I["L1_attn_W_bc"])
    p["wout"] = _prep_out_blocked(I["L1_attn_W_out"])
    p["wconv"] = np.ascontiguousarray(
        I["L1_attn_conv_w"][:, 0, :].reshape(16, 128, 4).transpose(1, 0, 2).reshape(128, 64))
    p["bw"] = _prep_lhsT(np.ascontiguousarray(I["L1_ffn_base_w"].T))
    p["rm"] = _prep_lhsT(np.ascontiguousarray(
        np.concatenate([I["L1_ffn_router_w"].T, I["L1_ffn_lora_A"]], axis=1)))
    p["bflat"] = np.ascontiguousarray(I["L1_ffn_lora_B"].reshape(64, 1024))
    cols = [
        _cols(I["L0_ffn_mu_r"]),
        _cols(I["L0_ffn_mu_k"]),
        _cols(I["L0_attn_bq"]),
        _cols(I["L0_attn_bk"]),
        _cols(I["L0_attn_bo"]),
        _cols(I["L0_attn_gamma"]),
        _cols(I["L0_attn_beta"]),
        np.ascontiguousarray(I["L1_attn_b_in"][:2048].reshape(16, 128).T),
        np.ascontiguousarray(I["L1_attn_b_in"][2048:].reshape(16, 128).T),
        np.ascontiguousarray(I["L1_attn_conv_b"].reshape(16, 128).T),
        np.ascontiguousarray(I["L1_attn_log_decay"].reshape(16, 128).T),
        np.ascontiguousarray(I["L1_attn_b_bc"][:2048].reshape(16, 128).T),
        np.ascontiguousarray(I["L1_attn_b_bc"][2048:].reshape(16, 128).T),
        _cols(I["L1_attn_b_out"]),
        _cols(I["L1_ffn_base_b"]),
    ]
    p["smalls"] = np.ascontiguousarray(np.concatenate(cols, axis=1))
    rows = [
        I["L0_attn_bv"],
        np.concatenate([I["L1_ffn_router_b"], np.zeros(8, np.float32)]),
    ]
    p["rows"] = np.ascontiguousarray(np.concatenate(rows)[None, :])
    p["ident"] = np.eye(128, dtype=np.float32)
    return p


SM = dict(mu_r=(0, 8), mu_k=(8, 16), bq=(16, 24), bk=(24, 32), bo=(32, 40),
          gamma=(40, 48), beta=(48, 56), b_in_u=(56, 72), b_in_g=(72, 88),
          conv_b=(88, 104), log_decay=(104, 120), b_bc_b=(120, 136),
          b_bc_c=(136, 152), b_out=(152, 160), base_b=(160, 168))
NSMALL = 168


class Sem:
    def __init__(self, h):
        self.h = h
        self.v = 0

    def inc(self, n=1):
        self.v += n
        return self.v


class Ctx:

    def __init__(self, nc, es):
        self.nc = nc
        self.es = es
        self.sems = {}
        for name in ("wl", "xl", "pe", "ac", "ve", "gp", "st", "pc"):
            self.sems[name] = Sem(es.enter_context(nc.semaphore(name)))
        self.psA = es.enter_context(nc.psum_tensor("psA", [128, 2048], F32))
        self.psB = es.enter_context(nc.psum_tensor("psB", [128, 1536], F32))
        self.bank = [self.psA[:, 512 * i:512 * (i + 1)] for i in range(4)] + \
                    [self.psB[:, 512 * i:512 * (i + 1)] for i in range(3)]
        self.bankb = es.enter_context(nc.psum_tensor("bankb", [128, 1024], BF16))

    def sb(self, name, shape, dtype):
        return self.es.enter_context(self.nc.sbuf_tensor(name, shape, dtype))


def ceil_div(a, b):
    return -(-a // b)


class E:

    ENGINES = ("tensor", "vector", "scalar", "sync", "gpsimd")

    def __init__(self, ctx, block):
        self.ctx = ctx
        self.block = block
        self.q = {e: [] for e in self.ENGINES}

    def add(self, eng, f):
        self.q[eng].append(f)

    def flush(self):
        for eng in self.ENGINES:
            steps = self.q[eng]
            if not steps:
                continue
            reg = getattr(self.block, eng)

            def body(engine, steps=steps):
                for s in steps:
                    s(engine)
            reg(body)
        self.q = {e: [] for e in self.ENGINES}


def _wait(S, waits):
    def f(engine, waits=tuple(waits)):
        for sname, val in waits:
            engine.wait_ge(S[sname].h, val)
    return f


def mm_group(ctx, em, bank_ap, pairs, waits=(), sem="pe", n_inc=1):
    S = ctx.sems

    def f(t, bank_ap=bank_ap, pairs=tuple(pairs), waits=tuple(waits)):
        for sname, val in waits:
            t.wait_ge(S[sname].h, val)
        n = len(pairs)
        inst = None
        for i, (l, r) in enumerate(pairs):
            inst = t.matmul(bank_ap, l, r, start=(i == 0), stop=(i == n - 1))
        inst.then_inc(S[sem].h, n_inc)
    em.add("tensor", f)
    return S[sem].inc(n_inc)


def emit_ln_cast(ctx, em, NTl, src_fn, src_ready=None, cast_guard=None):
    S = ctx.sems
    xc = ctx.xcastT

    def cast(v):
        for sname, val in (src_ready or ()):
            v.wait_ge(S[sname].h, val)
        for sname, val in (cast_guard or ()):
            v.wait_ge(S[sname].h, val)
        inst = None
        for c in range(8):
            inst = v.tensor_copy(xc[:, c * 512:c * 512 + NTl], src_fn(c))
        inst.then_inc(S["ve"].h, 1)
    em.add("vector", cast)
    return S["ve"].inc()


def emit_ln_tile(ctx, em, k, NTl, src_fn, mode, dst_fn=None, extra_fn=None,
                 gb=None, row_off=0, bank_sxx_free=None, bank_ab_free=None,
                 src_ready=None, rowsA=None, rowsB=None, apply_waits=None,
                 extra_ready=None, cast_eng="vector", cast_guard=None,
                 precast=None):
    S = ctx.sems
    nc = ctx.nc
    B0, B1, B2, B3 = ctx.bank[0], ctx.bank[1], ctx.bank[2], ctx.bank[3]
    xc, xq = ctx.xcastT, ctx.xsqT
    stat = ctx.statS

    csem = "ve" if cast_eng == "vector" else "gp"
    if precast is not None:
        csem, v_cast = "ve", precast
    else:
        def cast(v, k=k):
            for sname, val in (src_ready or ()):
                v.wait_ge(S[sname].h, val)
            for sname, val in (cast_guard or ()):
                v.wait_ge(S[sname].h, val)
            inst = None
            for c in range(8):
                inst = v.tensor_copy(xc[:, c * 512:c * 512 + NTl], src_fn(c))
            inst.then_inc(S[csem].h, 1)
        em.add(cast_eng, cast)
        v_cast = S[csem].inc()

    def sq(s, k=k):
        for sname, val in (src_ready or ()):
            s.wait_ge(S[sname].h, val)
        inst = None
        for c in range(8):
            inst = s.activation(xq[:, c * 512:c * 512 + NTl], src_fn(c), AF.Square)
        inst.then_inc(S["ac"].h, 1)
    em.add("scalar", sq)
    a_sq = S["ac"].inc()

    waits = [(csem, v_cast), ("ac", a_sq)]
    if bank_sxx_free is not None:
        waits.append(bank_sxx_free)
    pairs0 = [(ctx.ones_pcol_bf[:], xc[:, c * 512:c * 512 + NTl]) for c in range(8)]
    pairs1 = [(ctx.ones_pcol_bf[:], xq[:, c * 512:c * 512 + NTl]) for c in range(8)]
    mm_group(ctx, em, B0[0:1, 0:NTl], pairs0, waits=waits)
    p_stats = mm_group(ctx, em, B1[0:1, 0:NTl], pairs1)

    s0 = stat[0:1, 0:NTl]
    s1 = stat[0:1, 512:512 + NTl]
    s2 = stat[0:1, 1024:1024 + NTl]
    Arow = rowsA if rowsA is not None else ctx.rowA[0:1, row_off:row_off + NTl]

    def statmath_v(v):
        v.wait_ge(S["pe"].h, p_stats)
        v.tensor_scalar_mul(s0, B0[0:1, 0:NTl], -1.0 / 1024.0)
        v.tensor_scalar_mul(s1, B1[0:1, 0:NTl], 1.0 / 1024.0)
        v.tensor_tensor(s2, s0, s0, ALU.mult)
        v.tensor_sub(s2, s1, s2).then_inc(S["ve"].h, 1)
    em.add("vector", statmath_v)
    v_var = S["ve"].inc()

    waits = [("ve", v_var)]
    if bank_ab_free is not None:
        waits.append(bank_ab_free)
    mm_group(ctx, em, B3[:, 0:NTl], [(ctx.ones_row_f[:], s0)], waits=waits)

    def statmath_s(s):
        s.wait_ge(S["ve"].h, v_var)
        s.activation(s1, s2, AF.Ln, bias=ctx.epscol[0:1, 0:1])
        s.activation(Arow, s1, AF.Exp, scale=-0.5).then_inc(S["ac"].h, 1)
    em.add("scalar", statmath_s)
    a_A = S["ac"].inc()

    p_bcast = mm_group(ctx, em, B2[:, 0:NTl], [(ctx.ones_row_f[:], Arow)],
                       waits=[("ac", a_A)])

    tmp = ctx.lnTmp

    def apply(v):
        v.wait_ge(S["pe"].h, p_bcast)
        for sname, val in (apply_waits or ()):
            v.wait_ge(S[sname].h, val)
        for sname, val in (extra_ready or ()):
            v.wait_ge(S[sname].h, val)
        inst = None
        for c in range(8):
            v.tensor_tensor(tmp[:, 0:NTl], src_fn(c), B3[:, 0:NTl], ALU.add)
            if mode == "bf16":
                inst = v.tensor_tensor(dst_fn(c), tmp[:, 0:NTl], B2[:, 0:NTl], ALU.mult)
            elif mode == "pe":
                v.tensor_tensor(tmp[:, 0:NTl], tmp[:, 0:NTl], B2[:, 0:NTl], ALU.mult)
                inst = v.tensor_tensor(dst_fn(c), tmp[:, 0:NTl], extra_fn(c), ALU.add)
            elif mode == "gb_resid":
                v.tensor_tensor(tmp[:, 0:NTl], tmp[:, 0:NTl], B2[:, 0:NTl], ALU.mult)
                gcol, bcol = gb
                v.tensor_scalar(tmp[:, 0:NTl], tmp[:, 0:NTl], gcol(c), bcol(c),
                                op0=ALU.mult, op1=ALU.add)
                inst = v.tensor_tensor(dst_fn(c), dst_fn(c), tmp[:, 0:NTl], ALU.add)
        inst.then_inc(S["ve"].h, 1)
    em.add("vector", apply)
    v_apply = S["ve"].inc()

    return {"stats": p_stats, "bcast": p_bcast, "apply": v_apply,
            "cast": v_cast, "sq": a_sq, "var": v_var}


LN_CAST_ENG = {}


def emit_l0ffn(ctx, block, scope, uid):
    nc, S = ctx.nc, ctx.sems
    NTl, ntl = 512, 4
    XS = NTl + 4
    sb = lambda n, sh, dt: scope.enter_context(nc.sbuf_tensor(f"{n}_{uid}", sh, dt))
    xlnT = [sb("xln0", [128, 8 * XS], BF16)]
    xlnT.append(xlnT[0])
    sv = sb("sv", [128, 8], BF16)
    wr = sb("wr", [128, 8 * 1024], BF16)
    wk = sb("wk", [128, 8 * 1024], BF16)
    wv = sb("wv", [128, 8 * 1024], BF16)
    xr = [sb("xr0", [128, 8 * NTl], BF16), sb("xr1", [128, 8 * NTl], BF16)]
    xk = [sb("xk0", [128, 8 * NTl], BF16), sb("xk1", [128, 8 * NTl], BF16)]
    rbuf = sb("r0", [128, 8 * NTl], BF16)
    kact = sb("ka0", [128, 8 * NTl], BF16)
    mtmp = sb("mtmp", [128, NTl], F32)
    gtmp = sb("gtmp", [128, NTl], BF16)
    otmp = mtmp

    em = E(ctx, block)

    def wload(g):
        g.dma_start(out=wr[:], in_=ctx.dram["wr"][:, :]).then_inc(S["wl"].h, 16)
        g.dma_start(out=wk[:], in_=ctx.dram["wk"][:, :]).then_inc(S["wl"].h, 16)
        g.dma_start(out=wv[:], in_=ctx.dram["wv"][:, :]).then_inc(S["wl"].h, 16)
    em.add("gpsimd", wload)
    wl_done = S["wl"].inc(48)

    def emit_precast():
        rr = list(ctx.resid_ready)

        def precast(g, rr=rr):
            for sname, val in rr:
                g.wait_ge(S[sname].h, val)
            g.wait_ge(S["wl"].h, wl_done)
            for nm16, nm, n in (("win16", "win", 32), ("wbc16", "wbc", 32),
                                ("wout16", "wout", 8)):
                for i in range(n):
                    g.dma_start(out=ctx.dram[nm16][i, :, :],
                                in_=ctx.dram[nm][i, :, :]).then_inc(S["pc"].h, 16)
        em.add("gpsimd", precast)
        S["pc"].inc(72 * 16)

    mu_r = ctx.smalls[:, SM["mu_r"][0]:SM["mu_r"][1]]
    mu_k = ctx.smalls[:, SM["mu_k"][0]:SM["mu_k"][1]]
    omu_r = ctx.smalls2[:, 0:8]
    omu_k = ctx.smalls2[:, 8:16]

    bank_state = {4: None, 5: None}
    ln_prev = [None]
    bcopy_done = {}
    mix_done = {}
    stage_end = {}
    evac_end = {}

    def emit_mix(k):
        par = k % 2

        apv = ln_prev[0]["apply"]

        def bcopy(g, k=k, apv=apv):
            g.wait_ge(S["ve"].h, apv)
            if k == 0:
                g.memset(xlnT[0][:, 3:8 * XS:XS], 0.0)
            else:
                g.tensor_copy(xlnT[0][:, 3:8 * XS:XS], sv[:])
            g.drain().then_inc(S["gp"].h, 1)
        em.add("gpsimd", bcopy)
        bcopy_done[k] = S["gp"].inc()

        def mix(v, k=k, par=par, apv=apv):
            v.wait_ge(S["gp"].h, bcopy_done[k])
            if k >= 2 and stage_end.get(k - 2):
                v.wait_ge(S["pe"].h, stage_end[k - 2]["k"])
            inst = None
            for c in range(8):
                cur = xlnT[par][:, c * XS + 4: c * XS + 4 + NTl]
                prv = xlnT[par][:, c * XS + 3: c * XS + 3 + NTl]
                v.tensor_scalar_mul(gtmp[:], cur, mu_r[:, c:c + 1])
                v.scalar_tensor_tensor(xr[par][:, c * NTl:(c + 1) * NTl], prv,
                                       omu_r[:, c:c + 1], gtmp[:], op0=ALU.mult, op1=ALU.add)
                v.tensor_scalar_mul(gtmp[:], cur, mu_k[:, c:c + 1])
                inst = v.scalar_tensor_tensor(xk[par][:, c * NTl:(c + 1) * NTl], prv,
                                              omu_k[:, c:c + 1], gtmp[:], op0=ALU.mult, op1=ALU.add)
            v.tensor_copy(sv[:], xlnT[0][:, XS - 1:8 * XS:XS])
            inst = v.drain()
            inst.then_inc(S["ve"].h, 1)
        em.add("vector", mix)
        mix_done[k] = S["ve"].inc()

    def emit_stages(k):
        par = k % 2
        ends = {}
        for m in range(8):
            bidx = 4 + (m % 2)
            waits = [("ve", mix_done[k]), ("wl", wl_done)]
            if bank_state[bidx]:
                waits.append(bank_state[bidx])
            pairs = [(wr[:, c * 1024 + m * 128: c * 1024 + (m + 1) * 128],
                      xr[par][:, c * NTl:(c + 1) * NTl]) for c in range(8)]
            pv = mm_group(ctx, em, ctx.bank[bidx][:, 0:NTl], pairs, waits=waits)

            def ev(s, m=m, k=k, bidx=bidx, pv=pv):
                s.wait_ge(S["pe"].h, pv)
                if m == 0 and k >= 1 and (k - 1) in evac_end:
                    s.wait_ge(S["ve"].h, evac_end[k - 1])
                s.activation(rbuf[:, m * NTl:(m + 1) * NTl],
                             ctx.bank[bidx][:, 0:NTl], AF.Sigmoid).then_inc(S["ac"].h, 1)
            em.add("scalar", ev)
            bank_state[bidx] = ("ac", S["ac"].inc())
        ends["r"] = S["pe"].v
        for m in range(8):
            bidx = 4 + (m % 2)
            waits = [("ve", mix_done[k])]
            if bank_state[bidx]:
                waits.append(bank_state[bidx])
            pairs = [(wk[:, c * 1024 + m * 128: c * 1024 + (m + 1) * 128],
                      xk[par][:, c * NTl:(c + 1) * NTl]) for c in range(8)]
            pv = mm_group(ctx, em, ctx.bank[bidx][:, 0:NTl], pairs, waits=waits)

            def ev(s, m=m, bidx=bidx, pv=pv):
                s.wait_ge(S["pe"].h, pv)
                s.activation(ctx.bank[bidx][:, 0:NTl], ctx.bank[bidx][:, 0:NTl], AF.Relu)
                s.activation(kact[:, m * NTl:(m + 1) * NTl], ctx.bank[bidx][:, 0:NTl],
                             AF.Square).then_inc(S["ac"].h, 1)
            em.add("scalar", ev)
            bank_state[bidx] = ("ac", S["ac"].inc())
        ends["k"] = S["pe"].v
        ac_kdone = S["ac"].v
        for m in range(8):
            bidx = 4 + (m % 2)
            waits = [("ac", ac_kdone)]
            if bank_state[bidx]:
                waits.append(bank_state[bidx])
            pairs = [(wv[:, c * 1024 + m * 128: c * 1024 + (m + 1) * 128],
                      kact[:, c * NTl:(c + 1) * NTl]) for c in range(8)]
            pv = mm_group(ctx, em, ctx.bank[bidx][:, 0:NTl], pairs, waits=waits)

            def ev(v, m=m, k=k, bidx=bidx, pv=pv, par=par):
                v.wait_ge(S["pe"].h, pv)
                v.tensor_tensor(otmp[:], ctx.bank[bidx][:, 0:NTl],
                                rbuf[:, m * NTl:(m + 1) * NTl], ALU.mult)
                rv = ctx.resid[:, m * T + k * NTl: m * T + k * NTl + NTl]
                v.scalar_tensor_tensor(rv, otmp[:], 0.5, rv,
                                       op0=ALU.mult, op1=ALU.add).then_inc(S["ve"].h, 1)
            em.add("vector", ev)
            bank_state[bidx] = ("ve", S["ve"].inc())
        ends["v"] = S["pe"].v
        evac_end[k] = S["ve"].v
        stage_end[k] = ends

    def src_of(k):
        return lambda c, k=k: ctx.resid[:, c * T + k * NTl: c * T + (k + 1) * NTl]

    def src_ready_of(k):
        tr = getattr(ctx, "resid_tile_ready", None)
        return tr[k] if tr else ctx.resid_ready

    cast_vals = {0: emit_ln_cast(ctx, em, NTl, src_of(0), src_ready=src_ready_of(0))}
    for k in range(ntl):
        par = k % 2
        lnp = ln_prev[0]
        res = emit_ln_tile(
            ctx, em, k, NTl,
            src_fn=src_of(k),
            mode="bf16",
            dst_fn=lambda c, k=k, par=par: xlnT[par][:, c * XS + 4: c * XS + 4 + NTl],
            row_off=(k % 2) * 512,
            bank_sxx_free=(("ve", lnp["var"]) if lnp else None),
            bank_ab_free=(("ve", lnp["apply"]) if lnp else None),
            src_ready=src_ready_of(k), precast=cast_vals[k],
            apply_waits=([("ve", mix_done[k - 1])] if (k - 1) in mix_done else None))
        ln_prev[0] = res
        emit_mix(k)
        if uid == 0 and k == ntl - 1:
            emit_precast()
        if k + 1 < ntl:
            cast_vals[k + 1] = emit_ln_cast(ctx, em, NTl, src_of(k + 1),
                                            src_ready=src_ready_of(k + 1),
                                            cast_guard=[("pe", res["stats"])])
        if k >= 1:
            emit_stages(k - 1)
    emit_stages(ntl - 1)
    em.flush()
    ctx.resid_ready = [("ve", S["ve"].v)]
    ctx.resid_chunk_ready = None
    ctx.resid_tile_ready = None


DRAM_DECLS = [
    ("wr", [128, 8 * 1024]), ("wk", [128, 8 * 1024]), ("wv", [128, 8 * 1024]),
    ("wq", [8, 128, 8 * 128]), ("wkk", [8, 128, 8 * 128]), ("wvv", [8, 128, 8 * 128]),
    ("wo", [128, 8 * 1024]), ("peT", [1024, 2048]),
    ("win", [32, 128, 8 * 128]), ("wbc", [32, 128, 16 * 128]), ("wout", [8, 128, 16 * 128]),
    ("wconv", [128, 64]),
    ("bw", [128, 8 * 1024]), ("rm", [128, 8 * 16]), ("bflat", [64, 1024]),
    ("smalls", [128, NSMALL]), ("rows", [1, 1040]), ("ident", [128, 128]),
]


def emit_setup(ctx, block):
    nc, S = ctx.nc, ctx.sems
    em = E(ctx, block)

    def loads(sy):
        sy.dma_start(out=ctx.smalls[:], in_=ctx.dram["smalls"][:, :]).then_inc(S["xl"].h, 16)
        sy.dma_start(out=ctx.wconv[:], in_=ctx.dram["wconv"][:, :]).then_inc(S["xl"].h, 16)
    em.add("sync", loads)
    xl_v = S["xl"].inc(32)

    wl_tgt = S["wl"].v + 32

    def gloads(g, wl_tgt=wl_tgt):
        g.dma_start(out=ctx.rows_bf[:], in_=ctx.dram["rows"][:, :]).then_inc(S["wl"].h, 16)
        g.dma_start(out=ctx.idn[:], in_=ctx.dram["ident"][:, :]).then_inc(S["wl"].h, 16)
        g.memset(ctx.ones_pcol_bf[:], 1.0)
        g.memset(ctx.ones_row_f[:], 1.0)
        g.memset(ctx.ones_row_bf[:], 1.0)
        g.wait_ge(S["wl"].h, wl_tgt)
    em.add("gpsimd", gloads)
    S["wl"].inc(32)

    def vsetup(v, xl_v=xl_v):
        v.wait_ge(S["xl"].h, xl_v)
        mu_r = ctx.smalls[:, SM["mu_r"][0]:SM["mu_r"][1]]
        mu_k = ctx.smalls[:, SM["mu_k"][0]:SM["mu_k"][1]]
        v.tensor_scalar(ctx.smalls2[:, 0:8], mu_r, -1.0, 1.0, op0=ALU.mult, op1=ALU.add)
        v.tensor_scalar(ctx.smalls2[:, 8:16], mu_k, -1.0, 1.0, op0=ALU.mult, op1=ALU.add)
        v.memset(ctx.epscol[:], EPS)
        for j in range(16):
            v.reduce_sum(ctx.smalls2[:, 16 + j:17 + j], ctx.wconv[:, 4 * j:4 * j + 4], axis=AX.X)
        v.drain()
        binu_ = ctx.smalls[:, SM["b_in_u"][0]:SM["b_in_u"][1]]
        cvb_ = ctx.smalls[:, SM["conv_b"][0]:SM["conv_b"][1]]
        for j in range(16):
            v.scalar_tensor_tensor(ctx.smalls2[:, 32 + j:33 + j], binu_[:, j:j + 1],
                                   ctx.smalls2[:, 16 + j:17 + j], cvb_[:, j:j + 1],
                                   op0=ALU.mult, op1=ALU.add)
        v.memset(ctx.zcol[:], 0.0)
        v.memset(ctx.ln8col[:], -2.0794415416798357)
    em.add("vector", vsetup)

    def ssetup(s, xl_v=xl_v):
        s.wait_ge(S["xl"].h, xl_v)
        ld = ctx.smalls[:, SM["log_decay"][0]:SM["log_decay"][1]]
        s.activation(ctx.adecay[:], ld, AF.Sigmoid, scale=-1.0)
    em.add("scalar", ssetup)
    em.flush()


def emit_load(ctx, block, seq):
    S = ctx.sems
    em = E(ctx, block)
    st_base = S["st"].v - 128 if S["st"].v else 0
    first = S["st"].v == 0

    def loads(sy, st_base=st_base, first=first):
        for kt in range(4):
            for c in range(8):
                if not first and kt == 0:
                    sy.wait_ge(S["st"].h, st_base + 16 * (c + 1))
                sy.dma_start(
                    out=ctx.resid[:, c * T + kt * 512: c * T + (kt + 1) * 512],
                    in_=ctx.dram["xT"][seq, c * 128:(c + 1) * 128,
                                       kt * 512:(kt + 1) * 512]).then_inc(S["xl"].h, 16)
    em.add("sync", loads)
    xl0 = S["xl"].v
    ctx.resid_tile_ready = [[("xl", xl0 + 128 * (kt + 1))] for kt in range(4)]
    ctx.resid_ready = [("xl", S["xl"].inc(512))]
    ctx.resid_chunk_ready = None
    em.flush()


def emit_store(ctx, block, seq, last):
    S = ctx.sems
    em = E(ctx, block)
    waits = list(ctx.resid_ready)
    chunk_ready = getattr(ctx, "resid_chunk_ready", None)
    st_final = S["st"].v + 128

    def stores(sy, st_final=st_final, chunk_ready=chunk_ready):
        for c in range(8):
            if chunk_ready is not None:
                sname, val = chunk_ready[c]
                sy.wait_ge(S[sname].h, val)
            elif c == 0:
                for sname, val in waits:
                    sy.wait_ge(S[sname].h, val)
            sy.dma_start(out=ctx.dram["outT"][seq, c * 128:(c + 1) * 128, :],
                         in_=ctx.resid[:, c * T:(c + 1) * T]).then_inc(S["st"].h, 16)
        if last:
            sy.wait_ge(S["st"].h, st_final)
    em.add("sync", stores)
    S["st"].inc(128)
    em.flush()


def build_program(subs):
    nc = bass.Bass("TRN2", target_bir_lowering=False)
    dram = {"xT": nc.dram_tensor("xT", [2, 1024, 2048], F32, kind="ExternalInput").ap(),
            "outT": nc.dram_tensor("outT", [2, 1024, 2048], F32, kind="ExternalOutput").ap()}
    for name, shape in DRAM_DECLS:
        dram[name] = nc.dram_tensor(name, shape, F32, kind="ExternalInput").ap()
    for name, shape in (("win16", [32, 128, 8 * 128]), ("wbc16", [32, 128, 16 * 128]),
                        ("wout16", [8, 128, 16 * 128])):
        dram[name] = nc.dram_tensor(name, shape, BF16, kind="Internal").ap()
    with ExitStack() as es:
        ctx = Ctx(nc, es)
        ctx.dram = dram
        ctx.resid = ctx.sb("resid", [128, 8 * T], F32)
        ctx.xcastT = ctx.sb("xcastT", [128, 8 * 512], BF16)
        ctx.xsqT = ctx.sb("xsqT", [128, 8 * 512], BF16)
        ctx.statS = ctx.sb("statS", [1, 1536], F32)
        ctx.rowA = ctx.sb("rowA", [1, 1024], F32)
        ctx.rowB = ctx.sb("rowB", [1, 1024], F32)
        ctx.lnTmp = ctx.sb("lnTmp", [128, 512], F32)
        ctx.smalls = ctx.sb("sb_smalls", [128, NSMALL], F32)
        ctx.smalls2 = ctx.sb("smalls2", [128, 48], F32)
        ctx.rows_bf = ctx.sb("rows_bf", [1, 1040], BF16)
        ctx.ones_pcol_bf = ctx.sb("ones_pcol_bf", [128, 1], BF16)
        ctx.ones_row_f = ctx.sb("ones_row_f", [1, 128], F32)
        ctx.ones_row_bf = ctx.sb("ones_row_bf", [1, 128], BF16)
        ctx.idn = ctx.sb("idn", [128, 128], BF16)
        ctx.wconv = ctx.sb("sb_wconv", [128, 64], F32)
        ctx.adecay = ctx.sb("adecay", [128, 16], F32)
        ctx.epscol = ctx.sb("epscol", [128, 1], F32)
        ctx.zcol = ctx.sb("zcol", [128, 1], F32)
        ctx.ln8col = ctx.sb("ln8col", [128, 1], F32)
        ctx.carry = ctx.sb("carry", [128, 16], F32)
        ctx.tails = ctx.sb("tails", [128, 96], F32)

        with nc.Block("setup") as b:
            emit_setup(ctx, b)
        uid = 0
        for seq in range(2):
            with nc.Block(f"load{seq}") as b:
                emit_load(ctx, b, seq)
            for sname in subs:
                with nc.Block(f"{sname}_{uid}") as b:
                    with ExitStack() as scope:
                        EMITTERS[sname](ctx, b, scope, uid)
                    uid += 1
            with nc.Block(f"store{seq}") as b:
                emit_store(ctx, b, seq, last=(seq == 1))
    return nc


EMITTERS = {}

SUBS = ["l0ffn", "l0attn", "l0ffn", "l1ffn", "l1attn", "l1ffn"]
FUSED = True


def kernel(**inputs):
    prep = host_prep(inputs)
    x = inputs["L0_attn_x"]
    xTs = [np.ascontiguousarray(x[2 * c:2 * c + 2].transpose(0, 2, 1)) for c in range(8)]
    groups = [SUBS] if FUSED else [[s] for s in SUBS]
    cur = xTs
    for g in groups:
        nc = build_program(g)
        in_maps = [dict(prep, xT=cur[c]) for c in range(8)]
        res = run_bass_kernel_spmd(nc, in_maps, core_ids=list(range(8)))
        cur = [np.asarray(res.results[c]["outT"]) for c in range(8)]
    out = np.empty((16, 2048, 1024), np.float32)
    for c in range(8):
        out[2 * c:2 * c + 2] = cur[c].transpose(0, 2, 1)
    return out


def emit_l1ffn(ctx, block, scope, uid):
    nc, S = ctx.nc, ctx.sems
    NTl, ntl = 512, 4
    sb = lambda n, sh, dt: scope.enter_context(nc.sbuf_tensor(f"{n}_{uid}", sh, dt))
    xlnT = [sb("xln0", [128, 8 * NTl], BF16), sb("xln1", [128, 8 * NTl], BF16)]
    bw = sb("bw", [128, 8 * 1024], BF16)
    rm = sb("rm", [128, 8 * 16], BF16)
    bflat = sb("bflat", [64, 1024], BF16)
    rmf = sb("rmf", [128, 64], F32)
    mx = sb("mx", [128, 16], F32)
    gat = sb("gat", [128, 32], F32)
    gbf = sb("gbf", [128, 32], BF16)
    gatf = sb("gatf", [128, 32], F32)
    G = sb("G", [128, 64], BF16)
    gt = sb("gt", [64, 512], BF16)
    lora = sb("lora", [128, 8 * NTl], F32)
    bbuf = sb("bbuf", [128, 8 * NTl], F32)

    em = E(ctx, block)

    def wload(g):
        g.dma_start(out=bw[:], in_=ctx.dram["bw"][:, :]).then_inc(S["wl"].h, 16)
        g.dma_start(out=rm[:], in_=ctx.dram["rm"][:, :]).then_inc(S["wl"].h, 16)
        g.dma_start(out=bflat[:], in_=ctx.dram["bflat"][:, :]).then_inc(S["wl"].h, 16)
    em.add("gpsimd", wload)
    wl_done = S["wl"].inc(48)

    bb = ctx.smalls[:, SM["base_b"][0]:SM["base_b"][1]]
    B6 = ctx.bank[6]
    ln_prev = [None]
    stage_end = {}
    ve_end = {}

    def emit_stage(k):
        par = k % 2
        lnv = ln_apply[k]
        v1s = {}
        for tt in range(4):
            bidx = 4 + (tt % 2)
            waits = [("ve", lnv), ("wl", wl_done)]
            if bank_state[bidx]:
                waits.append(bank_state[bidx])
            pairs = [(xlnT[par][:, c * NTl + tt * 128: c * NTl + (tt + 1) * 128],
                      rm[:, c * 16:(c + 1) * 16]) for c in range(8)]
            pairs.append((ctx.ones_row_bf[:], ctx.rows_bf[0:1, 1024:1040]))
            pv = mm_group(ctx, em, ctx.bank[bidx][:, 0:16], pairs, waits=waits)

            def small(v, tt=tt, bidx=bidx, pv=pv):
                v.wait_ge(S["pe"].h, pv)
                v.tensor_copy(rmf[:, 16 * tt:16 * tt + 16], ctx.bank[bidx][:, 0:16])
                v.reduce_max(mx[:, 4 * tt:4 * tt + 1], ctx.bank[bidx][:, 0:8], axis=AX.X)
                v.drain()
                v.tensor_scalar_mul(mx[:, 4 * tt + 1:4 * tt + 2],
                                    mx[:, 4 * tt:4 * tt + 1], -1.0)
                v.drain().then_inc(S["ve"].h, 1)
            em.add("vector", small)
            v1s[tt] = S["ve"].inc()
            bank_state[bidx] = ("ve", v1s[tt])

        a2s = {}
        for m in range(8):
            bidx = 4 + (m % 2)
            waits = [("ve", lnv)]
            if bank_state[bidx]:
                waits.append(bank_state[bidx])
            pairs = [(bw[:, c * 1024 + m * 128: c * 1024 + (m + 1) * 128],
                      xlnT[par][:, c * NTl:(c + 1) * NTl]) for c in range(8)]
            pv = mm_group(ctx, em, ctx.bank[bidx][:, 0:NTl], pairs, waits=waits)

            def bev(s, m=m, bidx=bidx, pv=pv, rp=r1_of_m.get(m)):
                s.wait_ge(S["pe"].h, pv)
                if rp is not None:
                    s.wait_ge(S["ve"].h, rp)
                s.activation(bbuf[:, m * NTl:(m + 1) * NTl], ctx.bank[bidx][:, 0:NTl],
                             AF.Identity, bias=bb[:, m:m + 1]).then_inc(S["ac"].h, 1)
            em.add("scalar", bev)
            a2s[m] = S["ac"].inc()
            bank_state[bidx] = ("ac", a2s[m])

        def sexp(s):
            for tt in range(4):
                s.wait_ge(S["ve"].h, v1s[tt])
            inst = None
            for tt in range(4):
                inst = s.activation(gat[:, 8 * tt:8 * tt + 8],
                                    rmf[:, 16 * tt:16 * tt + 8], AF.Exp,
                                    bias=mx[:, 4 * tt + 1:4 * tt + 2])
            inst.then_inc(S["ac"].h, 1)
        em.add("scalar", sexp)
        a1 = S["ac"].inc()

        def vsum(v, a1=a1):
            v.wait_ge(S["ac"].h, a1)
            for tt in range(4):
                v.reduce_sum(mx[:, 4 * tt + 2:4 * tt + 3],
                             gat[:, 8 * tt:8 * tt + 8], axis=AX.X)
            v.drain()
            v.reciprocal(mx[:, 3:16:4], mx[:, 2:16:4])
            v.drain().then_inc(S["ve"].h, 1)
        em.add("vector", vsum)
        vsum_v = S["ve"].inc()

        gt_ready = []
        for tt in range(4):
            def small2(v, tt=tt, vsum_v=vsum_v):
                v.wait_ge(S["ve"].h, vsum_v)
                v.tensor_scalar_mul(gbf[:, 8 * tt:8 * tt + 8], gat[:, 8 * tt:8 * tt + 8],
                                    mx[:, 4 * tt + 3:4 * tt + 4])
                v.drain()
                v.tensor_copy(gatf[:, 8 * tt:8 * tt + 8], gbf[:, 8 * tt:8 * tt + 8])
                v.drain()
                inst = None
                for e in range(8):
                    inst = v.tensor_scalar_mul(G[:, 8 * e:8 * e + 8],
                                               rmf[:, 16 * tt + 8:16 * tt + 16],
                                               gatf[:, 8 * tt + e:8 * tt + e + 1])
                v.drain().then_inc(S["ve"].h, 1)
            em.add("vector", small2)
            v2 = S["ve"].inc()

            bf_guard = tuple(bankb_free[0]) if (tt == 0 and bankb_free[0]) else None

            def tp(t, tt=tt, v2=v2, bf_guard=bf_guard):
                t.wait_ge(S["ve"].h, v2)
                if bf_guard:
                    t.wait_ge(S[bf_guard[0]].h, bf_guard[1])
                t.transpose(ctx.bankb[0:64, tt * 128:(tt + 1) * 128], G[:], ctx.idn[:]
                            ).then_inc(S["pe"].h, 1)
            em.add("tensor", tp)
            p2 = S["pe"].inc()

            def gtc(v, tt=tt, p2=p2):
                v.wait_ge(S["pe"].h, p2)
                v.tensor_copy(gt[:, tt * 128:(tt + 1) * 128],
                              ctx.bankb[0:64, tt * 128:(tt + 1) * 128]).then_inc(S["ve"].h, 1)
            em.add("vector", gtc)
            gt_ready.append(S["ve"].inc())
        bankb_free[0] = ("ve", gt_ready[-1])

        lora_ready = []
        rg = rups_done.get(k - 1)
        for m in range(8):
            waits = [("ve", gt_ready[-1])]
            if bank_state[6]:
                waits.append(bank_state[6])
            pv = mm_group(ctx, em, B6[:, 0:NTl],
                          [(bflat[0:64, m * 128:(m + 1) * 128], gt[0:64, 0:NTl])],
                          waits=waits)

            def lev(v, m=m, pv=pv, rg=rg):
                v.wait_ge(S["pe"].h, pv)
                if m == 0 and rg is not None:
                    v.wait_ge(S["gp"].h, rg)
                v.tensor_copy(lora[:, m * NTl:(m + 1) * NTl], B6[:, 0:NTl]).then_inc(S["ve"].h, 1)
            em.add("vector", lev)
            bank_state[6] = ("ve", S["ve"].inc())
            lora_ready.append(bank_state[6][1])

        for m in range(8):
            def rup1(v, m=m, a2=a2s[m], lr=lora_ready[m]):
                v.wait_ge(S["ac"].h, a2)
                v.wait_ge(S["ve"].h, lr)
                lm = lora[:, m * NTl:(m + 1) * NTl]
                v.scalar_tensor_tensor(lm, bbuf[:, m * NTl:(m + 1) * NTl], 0.5, lm,
                                       op0=ALU.mult, op1=ALU.add).then_inc(S["ve"].h, 1)
            em.add("vector", rup1)
            r1 = S["ve"].inc()
            r1_of_m[m] = r1

            def rup(g, m=m, k=k, r1=r1):
                g.wait_ge(S["ve"].h, r1)
                lm = lora[:, m * NTl:(m + 1) * NTl]
                rv = ctx.resid[:, m * T + k * NTl: m * T + (k + 1) * NTl]
                g.tensor_tensor(rv, rv, lm, ALU.add).then_inc(S["gp"].h, 1)
            em.add("gpsimd", rup)
            S["gp"].inc()
            if k == ntl - 1:
                chunk_gp[m] = S["gp"].v
        rups_done[k] = S["gp"].v
        stage_end[k] = S["pe"].v

    bank_state = {4: None, 5: None, 6: None}
    bankb_free = [None]
    rups_done = {}
    r1_of_m = {}
    chunk_gp = {}
    ln_apply = {}
    def src_of(k):
        return lambda c, k=k: ctx.resid[:, c * T + k * NTl: c * T + (k + 1) * NTl]

    cast_vals = {0: emit_ln_cast(ctx, em, NTl, src_of(0), src_ready=ctx.resid_ready)}
    for k in range(ntl):
        lnp = ln_prev[0]
        aw = [("pe", stage_end[k - 2])] if (k - 2) in stage_end else None
        res = emit_ln_tile(
            ctx, em, k, NTl,
            src_fn=src_of(k),
            mode="bf16",
            dst_fn=lambda c, k=k: xlnT[k % 2][:, c * NTl:(c + 1) * NTl],
            row_off=(k % 2) * 512,
            bank_sxx_free=(("ve", lnp["var"]) if lnp else None),
            bank_ab_free=(("ve", lnp["apply"]) if lnp else None),
            src_ready=ctx.resid_ready, precast=cast_vals[k], apply_waits=aw)
        ln_prev[0] = res
        ln_apply[k] = res["apply"]
        if k + 1 < ntl:
            cast_vals[k + 1] = emit_ln_cast(ctx, em, NTl, src_of(k + 1),
                                            src_ready=ctx.resid_ready,
                                            cast_guard=[("pe", res["stats"])])
        if k >= 1:
            emit_stage(k - 1)
    emit_stage(ntl - 1)
    em.flush()
    ctx.resid_ready = [("gp", S["gp"].v)]
    ctx.resid_chunk_ready = [("gp", chunk_gp[m]) for m in range(8)]
    ctx.resid_tile_ready = None


def emit_l0attn(ctx, block, scope, uid):
    nc, S = ctx.nc, ctx.sems
    NTl = 512
    sb = lambda n, sh, dt: scope.enter_context(nc.sbuf_tensor(f"{n}_{uid}", sh, dt))
    xattn = sb("xattn", [128, 8 * 2048], BF16)
    xpe = sb("xpe", [128, 8 * 2048], BF16)
    em = E(ctx, block)
    bq = ctx.smalls[:, SM["bq"][0]:SM["bq"][1]]
    bk = ctx.smalls[:, SM["bk"][0]:SM["bk"][1]]
    bo = ctx.smalls[:, SM["bo"][0]:SM["bo"][1]]
    gam = ctx.smalls[:, SM["gamma"][0]:SM["gamma"][1]]
    bet = ctx.smalls[:, SM["beta"][0]:SM["beta"][1]]
    B0, B1, B2, B3, B6 = ctx.bank[0], ctx.bank[1], ctx.bank[2], ctx.bank[3], ctx.bank[6]

    with ExitStack() as sA:
        sba = lambda n, sh, dt: sA.enter_context(nc.sbuf_tensor(f"{n}_{uid}", sh, dt))
        wqb = sba("wqb", [128, 1024], BF16)
        wkb = sba("wkb", [128, 1024], BF16)
        wvb = sba("wvb", [128, 1024], BF16)
        ksb = [sba("ksb0", [128, 2048], BF16), sba("ksb1", [128, 2048], BF16)]
        vsb = [sba("vsb0", [128, 16 * 130], BF16), sba("vsb1", [128, 16 * 130], BF16)]

        B4, B5, B6 = ctx.bank[4], ctx.bank[5], ctx.bank[6]
        streg = [ctx.psA[:, 0:1024], ctx.psA[:, 1024:2048]]
        bank_state = {6: None}
        streg_state = {0: None, 1: None}
        pv_free = {0: None, 1: None}
        wl_ready = {}
        proj_end = {}
        qkv_done = {}
        mb_done = {}

        def wload(p):
            guard = proj_end.get(p - 1)

            def f(g, p=p, guard=guard):
                if guard is not None:
                    g.wait_ge(S["pe"].h, guard)
                for nm, buf in (("wq", wqb), ("wkk", wkb), ("wvv", wvb)):
                    g.dma_start(out=buf[:, 0:1024],
                                in_=ctx.dram[nm][p, :, :]).then_inc(S["wl"].h, 16)
            em.add("gpsimd", f)
            wl_ready[p] = S["wl"].inc(48)

        def vones(v):
            for par in range(2):
                v.memset(vsb[par][:, 64:16 * 130:130], 1.0)
                v.memset(vsb[par][:, 129:16 * 130:130], 1.0)
        em.add("vector", vones)
        wload(0)

        def qk_step0(wbuf, dst, bias, qt, wait_v):
            waits = [("ve", wait_v), ("wl", wl_ready[0])]
            if bank_state[6]:
                waits.append(bank_state[6])
            pairs = [(wbuf[:, c * 128:(c + 1) * 128],
                      xpe[:, c * 2048 + qt * NTl: c * 2048 + (qt + 1) * NTl])
                     for c in range(8)]
            pv = mm_group(ctx, em, B6[:, 0:NTl], pairs, waits=waits)
            proj_end[0] = pv

            def ev(v, dst=dst, bias=bias, qt=qt, pv=pv):
                v.wait_ge(S["pe"].h, pv)
                v.tensor_scalar_add(dst[:, qt * NTl:(qt + 1) * NTl],
                                    B6[:, 0:NTl], bias[:, 0:1]).then_inc(S["ve"].h, 1)
            em.add("vector", ev)
            bank_state[6] = ("ve", S["ve"].inc())
            qkv_done[0] = bank_state[6][1]

        def v_step0(tt, wait_v):
            waits = [("ve", wait_v), ("wl", wl_ready[0])]
            if bank_state[6]:
                waits.append(bank_state[6])
            pairs = [(xpe[:, c * 2048 + tt * 128: c * 2048 + (tt + 1) * 128],
                      wvb[:, c * 128:(c + 1) * 128])
                     for c in range(8)]
            pairs.append((ctx.ones_row_bf[:], ctx.rows_bf[0:1, 0:128]))
            pv = mm_group(ctx, em, B6[:, 0:128], pairs, waits=waits)
            proj_end[0] = pv

            def vev(v, tt=tt, pv=pv):
                v.wait_ge(S["pe"].h, pv)
                v.tensor_copy(vsb[0][:, tt * 130:tt * 130 + 64], B6[:, 0:64])
                v.tensor_copy(vsb[0][:, tt * 130 + 65:tt * 130 + 129],
                              B6[:, 64:128]).then_inc(S["ve"].h, 1)
            em.add("vector", vev)
            bank_state[6] = ("ve", S["ve"].inc())
            qkv_done[0] = bank_state[6][1]

        ln_prev = [None]
        pe_ready = {}
        with ExitStack() as sA1:
            pet = sA1.enter_context(nc.sbuf_tensor(f"pet_{uid}", [128, 8 * NTl], BF16))
            for k in range(4):
                ap_prev = ln_prev[0]["apply"] if ln_prev[0] else None

                def peload(g, k=k, ap_prev=ap_prev):
                    if ap_prev is not None:
                        g.wait_ge(S["ve"].h, ap_prev)
                    for c in range(8):
                        g.dma_start(out=pet[:, c * NTl:(c + 1) * NTl],
                                    in_=ctx.dram["peT"][c * 128:(c + 1) * 128,
                                                        k * NTl:(k + 1) * NTl]).then_inc(S["wl"].h, 16)
                em.add("gpsimd", peload)
                pe_ready[k] = S["wl"].inc(128)
                lnp = ln_prev[0]
                res = emit_ln_tile(
                    ctx, em, k, NTl,
                    src_fn=lambda c, k=k: ctx.resid[:, c * T + k * NTl: c * T + (k + 1) * NTl],
                    mode="pe",
                    dst_fn=lambda c, k=k: xpe[:, c * 2048 + k * NTl: c * 2048 + (k + 1) * NTl],
                    extra_fn=lambda c: pet[:, c * NTl:(c + 1) * NTl],
                    row_off=(k % 2) * 512,
                    bank_sxx_free=(("ve", lnp["var"]) if lnp else None),
                    bank_ab_free=(("ve", lnp["apply"]) if lnp else None),
                    src_ready=ctx.resid_ready,
                    extra_ready=[("wl", pe_ready[k])], cast_eng="vector",
                    cast_guard=([("pe", lnp["stats"])] if lnp else None))
                ln_prev[0] = res
                qk_step0(wkb, ksb[0], bk, k, res["apply"])
                for tt in range(4 * k, 4 * k + 4):
                    v_step0(tt, res["apply"])
        xpe_done = ln_prev[0]["apply"]
        qsb = [sba("qsb0", [128, 2048], BF16), sba("qsb1", [128, 2048], BF16)]
        pch = sba("pch", [128, 4 * 1024], BF16)
        nrm = sba("nrm", [1, 64], F32)
        mbias = sba("mbias", [128, 4], F32)
        out65x = sba("o65a", [65, 512], F32)
        out65 = [out65x, out65x]
        for qt in range(4):
            qk_step0(wqb, qsb[0], bq, qt, xpe_done)

        def make_proj_steps(p):
            par = p % 2
            steps = []
            for wbuf, dst, bias in ((wqb, qsb[par], bq), (wkb, ksb[par], bk)):
                for qt in range(4):
                    def step(wbuf=wbuf, dst=dst, bias=bias, qt=qt, p=p, par=par):
                        waits = [("ve", xpe_done), ("wl", wl_ready[p])]
                        if bank_state[6]:
                            waits.append(bank_state[6])
                        pairs = [(wbuf[:, c * 128:(c + 1) * 128],
                                  xpe[:, c * 2048 + qt * NTl: c * 2048 + (qt + 1) * NTl])
                                 for c in range(8)]
                        pv = mm_group(ctx, em, B6[:, 0:NTl], pairs, waits=waits)
                        proj_end[p] = pv

                        def ev(v, dst=dst, bias=bias, p=p, qt=qt, pv=pv):
                            v.wait_ge(S["pe"].h, pv)
                            v.tensor_scalar_add(dst[:, qt * NTl:(qt + 1) * NTl],
                                                B6[:, 0:NTl],
                                                bias[:, p:p + 1]).then_inc(S["ve"].h, 1)
                        em.add("vector", ev)
                        bank_state[6] = ("ve", S["ve"].inc())
                        qkv_done[p] = bank_state[6][1]
                    steps.append(step)
            for tt in range(16):
                def step(tt=tt, p=p, par=par):
                    waits = [("ve", xpe_done), ("wl", wl_ready[p])]
                    if bank_state[6]:
                        waits.append(bank_state[6])
                    pairs = [(xpe[:, c * 2048 + tt * 128: c * 2048 + (tt + 1) * 128],
                              wvb[:, c * 128:(c + 1) * 128])
                             for c in range(8)]
                    pairs.append((ctx.ones_row_bf[:], ctx.rows_bf[0:1, p * 128:(p + 1) * 128]))
                    pv = mm_group(ctx, em, B6[:, 0:128], pairs, waits=waits)
                    proj_end[p] = pv

                    def vev(v, tt=tt, par=par, pv=pv):
                        v.wait_ge(S["pe"].h, pv)
                        v.tensor_copy(vsb[par][:, tt * 130:tt * 130 + 64], B6[:, 0:64])
                        v.tensor_copy(vsb[par][:, tt * 130 + 65:tt * 130 + 129],
                                      B6[:, 64:128]).then_inc(S["ve"].h, 1)
                    em.add("vector", vev)
                    bank_state[6] = ("ve", S["ve"].inc())
                    qkv_done[p] = bank_state[6][1]
                steps.append(step)
            return steps

        def emit_norm(p):
            par = p % 2

            def sq(v, p=p, par=par, qd=qkv_done[p]):
                v.wait_ge(S["ve"].h, qd)
                v.tensor_tensor(ctx.xcastT[:, 0:2048], qsb[par][:], qsb[par][:], ALU.mult)
                v.tensor_tensor(ctx.xcastT[:, 2048:4096], ksb[par][:], ksb[par][:],
                                ALU.mult).then_inc(S["ve"].h, 1)
            em.add("vector", sq)
            sq_done = S["ve"].inc()
            for h2 in range(2):
                for i, base in enumerate((0, 2048)):
                    for qt in range(4):
                        waits = [("ve", sq_done)]
                        if bank_state[6]:
                            waits.append(bank_state[6])
                        pv = mm_group(ctx, em, B6[0:1, 0:NTl],
                                      [(ctx.ones_pcol_bf[64 * h2:64 * h2 + 64, :],
                                        ctx.xcastT[64 * h2:64 * h2 + 64,
                                                   base + qt * NTl: base + (qt + 1) * NTl])],
                                      waits=waits)

                        def nrd(v, h2=h2, i=i, qt=qt, pv=pv):
                            v.wait_ge(S["pe"].h, pv)
                            v.reduce_max(nrm[0:1, 8 * h2 + 4 * i + qt: 8 * h2 + 4 * i + qt + 1],
                                         B6[0:1, 0:NTl], axis=AX.X).then_inc(S["ve"].h, 1)
                        em.add("vector", nrd)
                        bank_state[6] = ("ve", S["ve"].inc())

            def mcomb_v(v):
                v.drain()
                for h2 in range(2):
                    v.reduce_max(nrm[0:1, 16 + h2:17 + h2], nrm[0:1, 8 * h2:8 * h2 + 4], axis=AX.X)
                    v.reduce_max(nrm[0:1, 18 + h2:19 + h2], nrm[0:1, 8 * h2 + 4:8 * h2 + 8], axis=AX.X)
                v.drain()
                for h2 in range(2):
                    v.tensor_tensor(nrm[0:1, 20 + h2:21 + h2], nrm[0:1, 16 + h2:17 + h2],
                                    nrm[0:1, 18 + h2:19 + h2], ALU.mult)
                v.drain().then_inc(S["ve"].h, 1)
            em.add("vector", mcomb_v)
            vq = S["ve"].inc()

            def mcomb_s(s, vq=vq):
                s.wait_ge(S["ve"].h, vq)
                s.activation(nrm[0:1, 22:24], nrm[0:1, 20:22], AF.Ln, bias=ctx.zcol[0:1, :])
                s.drain()
                s.activation(nrm[0:1, 24:26], nrm[0:1, 22:24], AF.Exp, scale=0.5,
                             bias=ctx.ln8col[0:1, :])
                s.drain().then_inc(S["ac"].h, 1)
            em.add("scalar", mcomb_s)
            aM = S["ac"].inc()

            def mneg(v, aM=aM):
                v.wait_ge(S["ac"].h, aM)
                v.tensor_scalar_mul(nrm[0:1, 26:28], nrm[0:1, 24:26], -1.0)
                v.drain().then_inc(S["ve"].h, 1)
            em.add("vector", mneg)
            vneg = S["ve"].inc()
            for h2 in range(2):
                waits = [("ve", vneg)]
                if bank_state[6]:
                    waits.append(bank_state[6])
                pv = mm_group(ctx, em, B6[:, 0:1],
                              [(ctx.ones_row_f[:], nrm[0:1, 26 + h2:27 + h2])], waits=waits)

                def mbc(v, h2=h2, par=par, pv=pv):
                    v.wait_ge(S["pe"].h, pv)
                    v.tensor_copy(mbias[:, par * 2 + h2:par * 2 + h2 + 1],
                                  B6[:, 0:1]).then_inc(S["ve"].h, 1)
                em.add("vector", mbc)
                bank_state[6] = ("ve", S["ve"].inc())
            mb_done[p] = S["ve"].v

        def emit_inner(p, fillers):
            par = p % 2
            for qt in range(4):
                expv = {}
                pvv = {}

                def emit_pv(kt, qt=qt, par=par, expv=expv, pvv=pvv):
                    for h2, pvbank in ((0, B4), (1, B5)):
                        g = pv_free[h2] if kt == 0 else None

                        def pvmm(t, kt=kt, h2=h2, pvbank=pvbank, exv=expv[(h2, kt // 2)],
                                 g=g, par=par):
                            t.wait_ge(S["ac"].h, exv)
                            if g:
                                t.wait_ge(S[g[0]].h, g[1])
                            t.matmul(pvbank[0:65, 0:NTl],
                                     vsb[par][:, kt * 130 + 65 * h2: kt * 130 + 65 * h2 + 65],
                                     pch[:, ((h2 * 2 + kt // 2 % 2) * 1024 + (kt % 2) * 512):
                                          ((h2 * 2 + kt // 2 % 2) * 1024 + (kt % 2) * 512) + NTl],
                                     start=(kt == 0), stop=(kt == 15)).then_inc(S["pe"].h, 1)
                        em.add("tensor", pvmm)
                        pvv[(h2, kt)] = S["pe"].inc()

                for kp in range(8):
                    if fillers:
                        fillers.pop(0)()
                    stv = {}
                    for h2 in range(2):
                        waits = [("ve", qkv_done[p])]
                        if streg_state[h2]:
                            waits.append(streg_state[h2])
                        pairs = []
                        stv[h2] = None
                        for half in range(2):
                            kt = 2 * kp + half
                            pr = [(ksb[par][64 * h2:64 * h2 + 64, kt * 128:(kt + 1) * 128],
                                   qsb[par][64 * h2:64 * h2 + 64, qt * NTl:(qt + 1) * NTl])]
                            w = waits if half == 0 else ()
                            stv[h2] = mm_group(ctx, em,
                                               streg[h2][:, half * 512:half * 512 + NTl],
                                               pr, waits=w)

                        def ex(s, kp=kp, h2=h2, par=par, stv=stv[h2],
                               mbd=(mb_done[p] if kp == 0 else None),
                               pvg=(pvv.get((h2, 2 * kp - 3)))):
                            s.wait_ge(S["pe"].h, stv)
                            if mbd is not None:
                                s.wait_ge(S["ve"].h, mbd)
                            if pvg is not None:
                                s.wait_ge(S["pe"].h, pvg)
                            s.activation(pch[:, (h2 * 2 + kp % 2) * 1024:
                                              (h2 * 2 + kp % 2) * 1024 + 1024],
                                         streg[h2][:], AF.Exp,
                                         bias=mbias[:, par * 2 + h2:par * 2 + h2 + 1],
                                         scale=0.125).then_inc(S["ac"].h, 1)
                        em.add("scalar", ex)
                        exv = S["ac"].inc()
                        expv[(h2, kp)] = exv
                        streg_state[h2] = ("ac", exv)
                    if kp >= 1:
                        emit_pv(2 * kp - 2)
                        emit_pv(2 * kp - 1)
                emit_pv(14)
                emit_pv(15)
                for h2, pvbank in ((0, B4), (1, B5)):
                    pv_done = pvv[(h2, 15)]

                    def cp65(v, pv_done=pv_done, pvbank=pvbank, h2=h2):
                        v.wait_ge(S["pe"].h, pv_done)
                        v.tensor_copy(out65[h2][0:64, 0:NTl], pvbank[0:64, 0:NTl])
                        v.tensor_copy(ctx.statS[0:1, 0:NTl], pvbank[64:65, 0:NTl])
                        v.drain().then_inc(S["ve"].h, 1)
                    em.add("vector", cp65)
                    vrec = S["ve"].inc()
                    pv_free[h2] = ("ve", vrec)
                    waits = [("ve", vrec)]
                    if bank_state[6]:
                        waits.append(bank_state[6])
                    pbc = mm_group(ctx, em, B6[0:64, 0:NTl],
                                   [(ctx.ones_row_f[0:1, 0:64], ctx.statS[0:1, 0:NTl])],
                                   waits=waits)

                    def srec(s, pbc=pbc):
                        s.wait_ge(S["pe"].h, pbc)
                        s.activation(ctx.lnTmp[0:64, 0:NTl], B6[0:64, 0:NTl],
                                     AF.Ln, bias=ctx.zcol[0:64, 0:1])
                        s.activation(B6[0:64, 0:NTl], ctx.lnTmp[0:64, 0:NTl],
                                     AF.Exp, scale=-1.0).then_inc(S["ac"].h, 1)
                    em.add("scalar", srec)
                    a_rec = S["ac"].inc()

                    def fin(v, p=p, h2=h2, qt=qt, a_rec=a_rec):
                        v.wait_ge(S["ac"].h, a_rec)
                        v.tensor_tensor(
                            xattn[64 * h2:64 * h2 + 64,
                                  p * 2048 + qt * NTl: p * 2048 + (qt + 1) * NTl],
                            out65[h2][0:64, 0:NTl], B6[0:64, 0:NTl],
                            ALU.mult).then_inc(S["ve"].h, 2)
                    em.add("vector", fin)
                    S["ve"].inc(2)
                    bank_state[6] = ("ve", S["ve"].v)
                if qt == 2 and p + 1 < 8:
                    while fillers:
                        fillers.pop(0)()
                    emit_norm(p + 1)

        emit_norm(0)
        for p in range(8):
            fillers = []
            if p + 1 < 8:
                wload(p + 1)
                fillers = make_proj_steps(p + 1)
            emit_inner(p, fillers)
        xattn_done = S["ve"].v

    wo = sb("wo", [128, 8 * 1024], BF16)
    h_sb = sb("h_sb", [128, 8 * NTl], F32)

    def wol(g, xattn_done=xattn_done):
        g.wait_ge(S["ve"].h, xattn_done)
        g.dma_start(out=wo[:], in_=ctx.dram["wo"][:, :]).then_inc(S["wl"].h, 16)
    em.add("gpsimd", wol)
    wo_ready = S["wl"].inc(16)

    bank_state = {4: None, 5: None}
    ln_prev2 = [None]
    for kt in range(4):
        h_ready = []
        hguard = ln_prev2[0]["apply"] if ln_prev2[0] else None
        for m in range(8):
            bidx = 4 + (m % 2)
            waits = [("ve", xattn_done), ("wl", wo_ready)]
            if bank_state[bidx]:
                waits.append(bank_state[bidx])
            pairs = [(wo[:, c * 1024 + m * 128: c * 1024 + (m + 1) * 128],
                      xattn[:, c * 2048 + kt * NTl: c * 2048 + (kt + 1) * NTl])
                     for c in range(8)]
            pv = mm_group(ctx, em, ctx.bank[bidx][:, 0:NTl], pairs, waits=waits)

            def hev(s, m=m, bidx=bidx, pv=pv, hguard=hguard):
                s.wait_ge(S["pe"].h, pv)
                if m == 0 and hguard is not None:
                    s.wait_ge(S["ve"].h, hguard)
                s.activation(h_sb[:, m * NTl:(m + 1) * NTl], ctx.bank[bidx][:, 0:NTl],
                             AF.Identity, bias=bo[:, m:m + 1]).then_inc(S["ac"].h, 1)
            em.add("scalar", hev)
            a1 = S["ac"].inc()
            bank_state[bidx] = ("ac", a1)

            def hadd(v, m=m, kt=kt, a1=a1):
                v.wait_ge(S["ac"].h, a1)
                hv = h_sb[:, m * NTl:(m + 1) * NTl]
                v.tensor_tensor(hv, hv, xpe[:, m * 2048 + kt * NTl: m * 2048 + (kt + 1) * NTl],
                                ALU.add).then_inc(S["ve"].h, 1)
            em.add("vector", hadd)
            h_ready.append(S["ve"].inc())
        lnp = ln_prev2[0]
        res = emit_ln_tile(
            ctx, em, kt, NTl,
            src_fn=lambda c: h_sb[:, c * NTl:(c + 1) * NTl],
            mode="gb_resid",
            dst_fn=lambda c, kt=kt: ctx.resid[:, c * T + kt * NTl: c * T + (kt + 1) * NTl],
            gb=(lambda c: gam[:, c:c + 1], lambda c: bet[:, c:c + 1]),
            row_off=(kt % 2) * 512,
            bank_sxx_free=(("ve", lnp["var"]) if lnp else None),
            bank_ab_free=(("ve", lnp["apply"]) if lnp else None),
            src_ready=[("ve", h_ready[-1])])
        ln_prev2[0] = res
    em.flush()
    ctx.resid_ready = [("gp", S["gp"].v)]
    ctx.resid_chunk_ready = None
    ctx.resid_tile_ready = None



def emit_l1attn(ctx, block, scope, uid):
    nc, S = ctx.nc, ctx.sems
    NTl, ntl = 512, 4
    sb = lambda n, sh, dt: scope.enter_context(nc.sbuf_tensor(f"{n}_{uid}", sh, dt))
    xlnT = [sb("xln0", [128, 8 * NTl], BF16), sb("xln1", [128, 8 * NTl], BF16)]
    uc = sb("uc", [128, 16 * NTl], BF16)
    sg = sb("sg", [128, 16 * NTl], BF16)
    yg = sb("yg", [128, 16 * NTl], BF16)
    winb = sb("winb", [128, 2 * 1024], BF16)
    wbcb = sb("wbcb", [128, 4 * 2048], BF16)
    woutb = sb("woutb", [128, 2 * 2048], BF16)
    cacc = sb("cacc", [128, NTl], F32)
    ssb = sb("ssb", [128, NTl], F32)
    ysb = sb("ysb", [128, NTl], F32)
    bsb = sb("bsb", [128, NTl], F32)
    otm = sb("otm", [128, NTl], F32)

    em = E(ctx, block)
    bing = ctx.smalls[:, SM["b_in_g"][0]:SM["b_in_g"][1]]
    bbcb = ctx.smalls[:, SM["b_bc_b"][0]:SM["b_bc_b"][1]]
    bbcc = ctx.smalls[:, SM["b_bc_c"][0]:SM["b_bc_c"][1]]
    bout = ctx.smalls[:, SM["b_out"][0]:SM["b_out"][1]]
    sbias = ctx.smalls2[:, 32:48]

    def vinit(v):
        v.memset(ctx.carry[:], 0.0)
        v.memset(ctx.tails[:], 0.0).then_inc(S["ve"].h, 1)
    em.add("vector", vinit)
    init_done = S["ve"].inc()

    wl_ready = {}

    pc_tgt = S["pc"].v

    def wdma(key, src_ap, dst_ap, guard):
        def f(g, src_ap=src_ap, dst_ap=dst_ap, guard=guard):
            if pc_tgt:
                g.wait_ge(S["pc"].h, pc_tgt)
            if guard is not None:
                g.wait_ge(S["pe"].h, guard)
            g.dma_start(out=dst_ap, in_=src_ap).then_inc(S["wl"].h, 16)
        em.add("gpsimd", f)
        wl_ready[key] = S["wl"].inc(16)

    def win_slot(i):
        return winb[:, (i % 2) * 1024:(i % 2 + 1) * 1024]

    def wbc_slot(i):
        return wbcb[:, (i % 4) * 2048:(i % 4 + 1) * 2048]

    def wout_slot(i):
        return woutb[:, (i % 2) * 2048:(i % 2 + 1) * 2048]

    bank_state = {4: None, 5: None}
    ln_prev = [None]
    ln_apply = {}
    stage_end = {}
    bc_end = {}
    yg_done_t = {}
    wout_end = {}

    def emit_stage(tc):
        par = tc % 2
        lnv = ln_apply[tc]
        silu_prev = [None]
        for j in range(16):
            jb = j % 2
            waits = [("ve", lnv), ("wl", wl_ready[("win", j, tc)])]
            if bank_state[4 + jb]:
                waits.append(bank_state[4 + jb])
            pairs = [(win_slot(j)[:, c * 128:(c + 1) * 128],
                      xlnT[par][:, c * NTl:(c + 1) * NTl]) for c in range(8)]
            pv = mm_group(ctx, em, ctx.bank[4 + jb][:, 0:NTl], pairs, waits=waits)
            if j + 2 < 16:
                wdma(("win", j + 2, tc), ctx.dram["win16"][j + 2, :, :], win_slot(j), pv)
            else:
                wdma(("wgate", j + 2 - 16, tc), ctx.dram["win16"][16 + j + 2 - 16, :, :],
                     win_slot(j), pv)

            wj = [ctx.wconv[:, 4 * j + i:4 * j + i + 1] for i in range(4)]
            tlr = ctx.tails[:, (tc % 2) * 48 + 3 * j:(tc % 2) * 48 + 3 * j + 3]
            tlw = ctx.tails[:, (1 - tc % 2) * 48 + 3 * j:(1 - tc % 2) * 48 + 3 * j + 3]
            bank = ctx.bank[4 + jb]

            def conv(v, j=j, tc=tc, pv=pv, wj=wj, tlr=tlr, tlw=tlw, bank=bank, sp=silu_prev[0]):
                v.wait_ge(S["pe"].h, pv)
                if tc == 0 and j == 0:
                    v.wait_ge(S["ve"].h, init_done)
                if sp is not None:
                    v.wait_ge(S["ac"].h, sp)
                v.tensor_scalar_mul(cacc[:], bank[:, 0:NTl], wj[3])
                v.scalar_tensor_tensor(cacc[:, 1:NTl], bank[:, 0:NTl - 1], wj[2],
                                       cacc[:, 1:NTl], op0=ALU.mult, op1=ALU.add)
                v.scalar_tensor_tensor(cacc[:, 2:NTl], bank[:, 0:NTl - 2], wj[1],
                                       cacc[:, 2:NTl], op0=ALU.mult, op1=ALU.add)
                v.scalar_tensor_tensor(cacc[:, 3:NTl], bank[:, 0:NTl - 3], wj[0],
                                       cacc[:, 3:NTl], op0=ALU.mult, op1=ALU.add)
                v.scalar_tensor_tensor(cacc[:, 0:3], tlr, wj[0], cacc[:, 0:3],
                                       op0=ALU.mult, op1=ALU.add)
                v.drain()
                v.scalar_tensor_tensor(cacc[:, 0:2], tlr[:, 1:3], wj[1], cacc[:, 0:2],
                                       op0=ALU.mult, op1=ALU.add)
                v.drain()
                v.scalar_tensor_tensor(cacc[:, 0:1], tlr[:, 2:3], wj[2], cacc[:, 0:1],
                                       op0=ALU.mult, op1=ALU.add)
                v.tensor_copy(tlw, bank[:, NTl - 3:NTl])
                v.drain().then_inc(S["ve"].h, 1)
            em.add("vector", conv)
            cv = S["ve"].inc()
            bank_state[4 + jb] = ("ve", cv)

            def silu(s, j=j, tc=tc, cv=cv, bce=bc_end.get(tc - 1)):
                s.wait_ge(S["ve"].h, cv)
                if j == 0 and bce is not None:
                    s.wait_ge(S["pe"].h, bce)
                s.activation(uc[:, j * NTl:(j + 1) * NTl], cacc[:], AF.Silu,
                             bias=sbias[:, j:j + 1]).then_inc(S["ac"].h, 1)
            em.add("scalar", silu)
            silu_prev[0] = S["ac"].inc()
        uc_done_ac = S["ac"].v

        for j in range(16):
            jb = j % 2
            waits = [("ve", lnv), ("wl", wl_ready[("wgate", j, tc)])]
            if bank_state[4 + jb]:
                waits.append(bank_state[4 + jb])
            pairs = [(win_slot(j)[:, c * 128:(c + 1) * 128],
                      xlnT[par][:, c * NTl:(c + 1) * NTl]) for c in range(8)]
            pv = mm_group(ctx, em, ctx.bank[4 + jb][:, 0:NTl], pairs, waits=waits)
            if j + 2 < 16:
                wdma(("wgate", j + 2, tc), ctx.dram["win16"][16 + j + 2, :, :], win_slot(j), pv)
            elif j == 14:
                wdma(("wbc_b", 0, tc), ctx.dram["wbc16"][0, :, :], wbc_slot(0), pv)
                wdma(("wbc_c", 0, tc), ctx.dram["wbc16"][16, :, :], wbc_slot(1), pv)
            elif j == 15:
                wdma(("wbc_b", 1, tc), ctx.dram["wbc16"][1, :, :], wbc_slot(2), pv)
                wdma(("wbc_c", 1, tc), ctx.dram["wbc16"][17, :, :], wbc_slot(3), pv)

            def gev(s, j=j, tc=tc, jb=jb, pv=pv, ygd=yg_done_t.get(tc - 1)):
                s.wait_ge(S["pe"].h, pv)
                if j == 0 and ygd is not None:
                    s.wait_ge(S["ve"].h, ygd)
                s.activation(sg[:, j * NTl:(j + 1) * NTl], ctx.bank[4 + jb][:, 0:NTl],
                             AF.Sigmoid, bias=bing[:, j:j + 1]).then_inc(S["ac"].h, 1)
            em.add("scalar", gev)
            bank_state[4 + jb] = ("ac", S["ac"].inc())

        scan_prev = [None]
        for j in range(16):
            waits = [("ac", uc_done_ac), ("wl", wl_ready[("wbc_b", j, tc)])]
            if bank_state[4]:
                waits.append(bank_state[4])
            pairs = [(wbc_slot(2 * (j % 2))[:, ci * 128:(ci + 1) * 128],
                      uc[:, ci * NTl:(ci + 1) * NTl]) for ci in range(16)]
            pvb = mm_group(ctx, em, ctx.bank[4][:, 0:NTl], pairs, waits=waits)
            if j + 2 < 16:
                wdma(("wbc_b", j + 2, tc), ctx.dram["wbc16"][j + 2, :, :],
                     wbc_slot(2 * (j % 2)), pvb)

            def bev(s, j=j, pvb=pvb, sp=scan_prev[0]):
                s.wait_ge(S["pe"].h, pvb)
                if sp is not None:
                    s.wait_ge(S["ve"].h, sp)
                s.activation(bsb[:], ctx.bank[4][:, 0:NTl], AF.Identity,
                             bias=bbcb[:, j:j + 1]).then_inc(S["ac"].h, 1)
            em.add("scalar", bev)
            b_ready = S["ac"].inc()
            bank_state[4] = ("ac", b_ready)

            waits = [("ac", uc_done_ac), ("wl", wl_ready[("wbc_c", j, tc)])]
            if bank_state[5]:
                waits.append(bank_state[5])
            pairs = [(wbc_slot(2 * (j % 2) + 1)[:, ci * 128:(ci + 1) * 128],
                      uc[:, ci * NTl:(ci + 1) * NTl]) for ci in range(16)]
            pvc = mm_group(ctx, em, ctx.bank[5][:, 0:NTl], pairs, waits=waits)
            if j + 2 < 16:
                wdma(("wbc_c", j + 2, tc), ctx.dram["wbc16"][16 + j + 2, :, :],
                     wbc_slot(2 * (j % 2) + 1), pvc)

            def scan(v, j=j, tc=tc, b_ready=b_ready, pvc=pvc, we=wout_end.get(tc - 1)):
                v.wait_ge(S["ac"].h, b_ready)
                v.wait_ge(S["pe"].h, pvc)
                if j == 0 and we is not None:
                    v.wait_ge(S["pe"].h, we)
                a_b = ctx.adecay[:, j:j + 1].broadcast_to([128, NTl])
                v.tensor_tensor_scan(ssb[:], a_b, bsb[:], ctx.carry[:, j:j + 1],
                                     op0=ALU.mult, op1=ALU.add)
                v.tensor_tensor(ysb[:], ssb[:], ctx.bank[5][:, 0:NTl], ALU.mult)
                v.scalar_tensor_tensor(ysb[:], ssb[:], bbcc[:, j:j + 1], ysb[:],
                                       op0=ALU.mult, op1=ALU.add)
                v.tensor_tensor(yg[:, j * NTl:(j + 1) * NTl], ysb[:],
                                sg[:, j * NTl:(j + 1) * NTl], ALU.mult)
                v.tensor_copy(ctx.carry[:, j:j + 1], ssb[:, NTl - 1:NTl])
                v.drain().then_inc(S["ve"].h, 1)
            em.add("vector", scan)
            scan_prev[0] = S["ve"].inc()
            bank_state[5] = ("ve", scan_prev[0])
        yg_done = S["ve"].v
        yg_done_t[tc] = yg_done
        bc_end[tc] = S["pe"].v

        rup_prev = [None]
        for m in range(8):
            mb = m % 2
            waits = [("ve", yg_done), ("wl", wl_ready[("wout", m, tc)])]
            if bank_state[4 + mb]:
                waits.append(bank_state[4 + mb])
            pairs = [(wout_slot(m)[:, ci * 128:(ci + 1) * 128],
                      yg[:, ci * NTl:(ci + 1) * NTl]) for ci in range(16)]
            pv = mm_group(ctx, em, ctx.bank[4 + mb][:, 0:NTl], pairs, waits=waits)
            if m + 2 < 8:
                wdma(("wout", m + 2, tc), ctx.dram["wout16"][m + 2, :, :], wout_slot(m), pv)
            elif tc + 1 < ntl:
                if m == 6:
                    wdma(("win", 0, tc + 1), ctx.dram["win16"][0, :, :], win_slot(0), pv)
                    wdma(("win", 1, tc + 1), ctx.dram["win16"][1, :, :], win_slot(1), pv)
                else:
                    wdma(("wout", 0, tc + 1), ctx.dram["wout16"][0, :, :], wout_slot(0), pv)
                    wdma(("wout", 1, tc + 1), ctx.dram["wout16"][1, :, :], wout_slot(1), pv)

            def oev(s, m=m, mb=mb, pv=pv, rp=rup_prev[0]):
                s.wait_ge(S["pe"].h, pv)
                if rp is not None:
                    s.wait_ge(S["ve"].h, rp)
                s.activation(otm[:], ctx.bank[4 + mb][:, 0:NTl], AF.Identity,
                             bias=bout[:, m:m + 1]).then_inc(S["ac"].h, 1)
            em.add("scalar", oev)
            a1 = S["ac"].inc()
            bank_state[4 + mb] = ("ac", a1)

            def rup(v, m=m, tc=tc, a1=a1):
                v.wait_ge(S["ac"].h, a1)
                rv = ctx.resid[:, m * T + tc * NTl: m * T + (tc + 1) * NTl]
                v.tensor_tensor(rv, rv, otm[:], ALU.add).then_inc(S["ve"].h, 1)
            em.add("vector", rup)
            rup_prev[0] = S["ve"].inc()
        stage_end[tc] = S["pe"].v
        wout_end[tc] = S["pe"].v

    wdma(("win", 0, 0), ctx.dram["win16"][0, :, :], win_slot(0), None)
    wdma(("win", 1, 0), ctx.dram["win16"][1, :, :], win_slot(1), None)
    wdma(("wout", 0, 0), ctx.dram["wout16"][0, :, :], wout_slot(0), None)
    wdma(("wout", 1, 0), ctx.dram["wout16"][1, :, :], wout_slot(1), None)

    def src_of(k):
        return lambda c, k=k: ctx.resid[:, c * T + k * NTl: c * T + (k + 1) * NTl]

    cast_vals = {0: emit_ln_cast(ctx, em, NTl, src_of(0), src_ready=ctx.resid_ready)}
    for k in range(ntl):
        lnp = ln_prev[0]
        aw = [("pe", stage_end[k - 2])] if (k - 2) in stage_end else None
        res = emit_ln_tile(
            ctx, em, k, NTl,
            src_fn=src_of(k),
            mode="bf16",
            dst_fn=lambda c, k=k: xlnT[k % 2][:, c * NTl:(c + 1) * NTl],
            row_off=(k % 2) * 512,
            bank_sxx_free=(("ve", lnp["var"]) if lnp else None),
            bank_ab_free=(("ve", lnp["apply"]) if lnp else None),
            src_ready=ctx.resid_ready, precast=cast_vals[k], apply_waits=aw)
        ln_prev[0] = res
        ln_apply[k] = res["apply"]
        if k + 1 < ntl:
            cast_vals[k + 1] = emit_ln_cast(ctx, em, NTl, src_of(k + 1),
                                            src_ready=ctx.resid_ready,
                                            cast_guard=[("pe", res["stats"])])
        if k >= 1:
            emit_stage(k - 1)
    emit_stage(ntl - 1)
    em.flush()
    ctx.resid_ready = [("ve", S["ve"].v)]
    ctx.resid_chunk_ready = None
    ctx.resid_tile_ready = None


EMITTERS.update({"l0ffn": emit_l0ffn, "l1ffn": emit_l1ffn, "l0attn": emit_l0attn, "l1attn": emit_l1attn})


# version 29
# speedup vs baseline: 1.0743x; 1.0743x over previous
import numpy as np
import ml_dtypes
from contextlib import ExitStack

import concourse.bass as bass
import concourse.mybir as mybir
from concourse.bass_utils import run_bass_kernel_spmd

F32 = mybir.dt.float32
BF16 = mybir.dt.bfloat16
AF = mybir.ActivationFunctionType
ALU = mybir.AluOpType
AX = mybir.AxisListType

D = 1024
T = 2048
C8 = 8
C16 = 16
NT = 512
NTILES = T // NT
EPS = 1e-5
PAD = 4
XW = T + PAD

BF = ml_dtypes.bfloat16


def _prep_lhsT(w):
    IN, OUT = w.shape
    return np.ascontiguousarray(w.reshape(IN // 128, 128, OUT).transpose(1, 0, 2).reshape(128, -1))


def _prep_out_blocked(w, blk=128):
    IN, OUT = w.shape
    return np.ascontiguousarray(
        w.reshape(IN // 128, 128, OUT // blk, blk).transpose(2, 1, 0, 3).reshape(OUT // blk, 128, -1))


def _cols(v):
    return np.ascontiguousarray(v.reshape(-1, 128).T)


def host_prep(I):
    p = {}
    p["wr"] = _prep_lhsT(np.ascontiguousarray(I["L0_ffn_W_r"].T))
    p["wk"] = _prep_lhsT(np.ascontiguousarray(I["L0_ffn_W_k"].T))
    p["wv"] = _prep_lhsT(np.ascontiguousarray(I["L0_ffn_W_v"].T))
    p["wq"] = _prep_out_blocked(I["L0_attn_wq"])
    p["wkk"] = _prep_out_blocked(I["L0_attn_wk"])
    p["wvv"] = _prep_out_blocked(I["L0_attn_wv"])
    p["wo"] = _prep_lhsT(I["L0_attn_wo"])
    p["peT"] = np.ascontiguousarray(I["L0_attn_pe"][0, :T, :].T)
    p["win"] = _prep_out_blocked(I["L1_attn_W_in"])
    p["wbc"] = _prep_out_blocked(I["L1_attn_W_bc"])
    p["wout"] = _prep_out_blocked(I["L1_attn_W_out"])
    p["wconv"] = np.ascontiguousarray(
        I["L1_attn_conv_w"][:, 0, :].reshape(16, 128, 4).transpose(1, 0, 2).reshape(128, 64))
    p["bw"] = _prep_lhsT(np.ascontiguousarray(I["L1_ffn_base_w"].T))
    p["rm"] = _prep_lhsT(np.ascontiguousarray(
        np.concatenate([I["L1_ffn_router_w"].T, I["L1_ffn_lora_A"]], axis=1)))
    p["bflat"] = np.ascontiguousarray(I["L1_ffn_lora_B"].reshape(64, 1024))
    cols = [
        _cols(I["L0_ffn_mu_r"]),
        _cols(I["L0_ffn_mu_k"]),
        _cols(I["L0_attn_bq"]),
        _cols(I["L0_attn_bk"]),
        _cols(I["L0_attn_bo"]),
        _cols(I["L0_attn_gamma"]),
        _cols(I["L0_attn_beta"]),
        np.ascontiguousarray(I["L1_attn_b_in"][:2048].reshape(16, 128).T),
        np.ascontiguousarray(I["L1_attn_b_in"][2048:].reshape(16, 128).T),
        np.ascontiguousarray(I["L1_attn_conv_b"].reshape(16, 128).T),
        np.ascontiguousarray(I["L1_attn_log_decay"].reshape(16, 128).T),
        np.ascontiguousarray(I["L1_attn_b_bc"][:2048].reshape(16, 128).T),
        np.ascontiguousarray(I["L1_attn_b_bc"][2048:].reshape(16, 128).T),
        _cols(I["L1_attn_b_out"]),
        _cols(I["L1_ffn_base_b"]),
    ]
    p["smalls"] = np.ascontiguousarray(np.concatenate(cols, axis=1))
    rows = [
        I["L0_attn_bv"],
        np.concatenate([I["L1_ffn_router_b"], np.zeros(8, np.float32)]),
    ]
    p["rows"] = np.ascontiguousarray(np.concatenate(rows)[None, :])
    p["ident"] = np.eye(128, dtype=np.float32)
    return p


SM = dict(mu_r=(0, 8), mu_k=(8, 16), bq=(16, 24), bk=(24, 32), bo=(32, 40),
          gamma=(40, 48), beta=(48, 56), b_in_u=(56, 72), b_in_g=(72, 88),
          conv_b=(88, 104), log_decay=(104, 120), b_bc_b=(120, 136),
          b_bc_c=(136, 152), b_out=(152, 160), base_b=(160, 168))
NSMALL = 168


class Sem:
    def __init__(self, h):
        self.h = h
        self.v = 0

    def inc(self, n=1):
        self.v += n
        return self.v


class Ctx:

    def __init__(self, nc, es):
        self.nc = nc
        self.es = es
        self.sems = {}
        for name in ("wl", "xl", "pe", "ac", "ve", "gp", "st", "pc"):
            self.sems[name] = Sem(es.enter_context(nc.semaphore(name)))
        self.psA = es.enter_context(nc.psum_tensor("psA", [128, 2048], F32))
        self.psB = es.enter_context(nc.psum_tensor("psB", [128, 1536], F32))
        self.bank = [self.psA[:, 512 * i:512 * (i + 1)] for i in range(4)] + \
                    [self.psB[:, 512 * i:512 * (i + 1)] for i in range(3)]
        self.bankb = es.enter_context(nc.psum_tensor("bankb", [128, 1024], BF16))

    def sb(self, name, shape, dtype):
        return self.es.enter_context(self.nc.sbuf_tensor(name, shape, dtype))


def ceil_div(a, b):
    return -(-a // b)


class E:

    ENGINES = ("tensor", "vector", "scalar", "sync", "gpsimd")

    def __init__(self, ctx, block):
        self.ctx = ctx
        self.block = block
        self.q = {e: [] for e in self.ENGINES}

    def add(self, eng, f):
        self.q[eng].append(f)

    def flush(self):
        for eng in self.ENGINES:
            steps = self.q[eng]
            if not steps:
                continue
            reg = getattr(self.block, eng)

            def body(engine, steps=steps):
                for s in steps:
                    s(engine)
            reg(body)
        self.q = {e: [] for e in self.ENGINES}


def _wait(S, waits):
    def f(engine, waits=tuple(waits)):
        for sname, val in waits:
            engine.wait_ge(S[sname].h, val)
    return f


def mm_group(ctx, em, bank_ap, pairs, waits=(), sem="pe", n_inc=1):
    S = ctx.sems

    def f(t, bank_ap=bank_ap, pairs=tuple(pairs), waits=tuple(waits)):
        for sname, val in waits:
            t.wait_ge(S[sname].h, val)
        n = len(pairs)
        inst = None
        for i, (l, r) in enumerate(pairs):
            inst = t.matmul(bank_ap, l, r, start=(i == 0), stop=(i == n - 1))
        inst.then_inc(S[sem].h, n_inc)
    em.add("tensor", f)
    return S[sem].inc(n_inc)


def emit_ln_cast(ctx, em, NTl, src_fn, src_ready=None, cast_guard=None):
    S = ctx.sems
    xc = ctx.xcastT

    def cast(v):
        for sname, val in (src_ready or ()):
            v.wait_ge(S[sname].h, val)
        for sname, val in (cast_guard or ()):
            v.wait_ge(S[sname].h, val)
        inst = None
        for c in range(8):
            inst = v.tensor_copy(xc[:, c * 512:c * 512 + NTl], src_fn(c))
        inst.then_inc(S["ve"].h, 1)
    em.add("vector", cast)
    return S["ve"].inc()


def emit_ln_tile(ctx, em, k, NTl, src_fn, mode, dst_fn=None, extra_fn=None,
                 gb=None, row_off=0, bank_sxx_free=None, bank_ab_free=None,
                 src_ready=None, rowsA=None, rowsB=None, apply_waits=None,
                 extra_ready=None, cast_eng="vector", cast_guard=None,
                 precast=None):
    S = ctx.sems
    nc = ctx.nc
    B0, B1, B2, B3 = ctx.bank[0], ctx.bank[1], ctx.bank[2], ctx.bank[3]
    xc, xq = ctx.xcastT, ctx.xsqT
    stat = ctx.statS

    csem = "ve" if cast_eng == "vector" else "gp"
    if precast is not None:
        csem, v_cast = "ve", precast
    else:
        def cast(v, k=k):
            for sname, val in (src_ready or ()):
                v.wait_ge(S[sname].h, val)
            for sname, val in (cast_guard or ()):
                v.wait_ge(S[sname].h, val)
            inst = None
            for c in range(8):
                inst = v.tensor_copy(xc[:, c * 512:c * 512 + NTl], src_fn(c))
            inst.then_inc(S[csem].h, 1)
        em.add(cast_eng, cast)
        v_cast = S[csem].inc()

    def sq(s, k=k):
        for sname, val in (src_ready or ()):
            s.wait_ge(S[sname].h, val)
        inst = None
        for c in range(8):
            inst = s.activation(xq[:, c * 512:c * 512 + NTl], src_fn(c), AF.Square)
        inst.then_inc(S["ac"].h, 1)
    em.add("scalar", sq)
    a_sq = S["ac"].inc()

    waits = [(csem, v_cast), ("ac", a_sq)]
    if bank_sxx_free is not None:
        waits.append(bank_sxx_free)
    pairs0 = [(ctx.ones_pcol_bf[:], xc[:, c * 512:c * 512 + NTl]) for c in range(8)]
    pairs1 = [(ctx.ones_pcol_bf[:], xq[:, c * 512:c * 512 + NTl]) for c in range(8)]
    mm_group(ctx, em, B0[0:1, 0:NTl], pairs0, waits=waits)
    p_stats = mm_group(ctx, em, B1[0:1, 0:NTl], pairs1)

    s0 = stat[0:1, 0:NTl]
    s1 = stat[0:1, 512:512 + NTl]
    s2 = stat[0:1, 1024:1024 + NTl]
    Arow = rowsA if rowsA is not None else ctx.rowA[0:1, row_off:row_off + NTl]

    def statmath_v(v):
        v.wait_ge(S["pe"].h, p_stats)
        v.tensor_scalar_mul(s0, B0[0:1, 0:NTl], -1.0 / 1024.0)
        v.tensor_scalar_mul(s1, B1[0:1, 0:NTl], 1.0 / 1024.0)
        v.tensor_tensor(s2, s0, s0, ALU.mult)
        v.tensor_sub(s2, s1, s2).then_inc(S["ve"].h, 1)
    em.add("vector", statmath_v)
    v_var = S["ve"].inc()

    waits = [("ve", v_var)]
    if bank_ab_free is not None:
        waits.append(bank_ab_free)
    mm_group(ctx, em, B3[:, 0:NTl], [(ctx.ones_row_f[:], s0)], waits=waits)

    def statmath_s(s):
        s.wait_ge(S["ve"].h, v_var)
        s.activation(s1, s2, AF.Ln, bias=ctx.epscol[0:1, 0:1])
        s.activation(Arow, s1, AF.Exp, scale=-0.5).then_inc(S["ac"].h, 1)
    em.add("scalar", statmath_s)
    a_A = S["ac"].inc()

    p_bcast = mm_group(ctx, em, B2[:, 0:NTl], [(ctx.ones_row_f[:], Arow)],
                       waits=[("ac", a_A)])

    tmp = ctx.lnTmp

    def apply(v):
        v.wait_ge(S["pe"].h, p_bcast)
        for sname, val in (apply_waits or ()):
            v.wait_ge(S[sname].h, val)
        for sname, val in (extra_ready or ()):
            v.wait_ge(S[sname].h, val)
        inst = None
        for c in range(8):
            v.tensor_tensor(tmp[:, 0:NTl], src_fn(c), B3[:, 0:NTl], ALU.add)
            if mode == "bf16":
                inst = v.tensor_tensor(dst_fn(c), tmp[:, 0:NTl], B2[:, 0:NTl], ALU.mult)
            elif mode == "pe":
                v.tensor_tensor(tmp[:, 0:NTl], tmp[:, 0:NTl], B2[:, 0:NTl], ALU.mult)
                inst = v.tensor_tensor(dst_fn(c), tmp[:, 0:NTl], extra_fn(c), ALU.add)
            elif mode == "gb_resid":
                v.tensor_tensor(tmp[:, 0:NTl], tmp[:, 0:NTl], B2[:, 0:NTl], ALU.mult)
                gcol, bcol = gb
                v.tensor_scalar(tmp[:, 0:NTl], tmp[:, 0:NTl], gcol(c), bcol(c),
                                op0=ALU.mult, op1=ALU.add)
                inst = v.tensor_tensor(dst_fn(c), dst_fn(c), tmp[:, 0:NTl], ALU.add)
        inst.then_inc(S["ve"].h, 1)
    em.add("vector", apply)
    v_apply = S["ve"].inc()

    return {"stats": p_stats, "bcast": p_bcast, "apply": v_apply,
            "cast": v_cast, "sq": a_sq, "var": v_var}


LN_CAST_ENG = {}


def emit_l0ffn(ctx, block, scope, uid):
    nc, S = ctx.nc, ctx.sems
    NTl, ntl = 512, 4
    XS = NTl + 4
    sb = lambda n, sh, dt: scope.enter_context(nc.sbuf_tensor(f"{n}_{uid}", sh, dt))
    xlnT = [sb("xln0", [128, 8 * XS], BF16)]
    xlnT.append(xlnT[0])
    sv = sb("sv", [128, 8], BF16)
    wr = sb("wr", [128, 8 * 1024], BF16)
    wk = sb("wk", [128, 8 * 1024], BF16)
    wv = sb("wv", [128, 8 * 1024], BF16)
    xr = [sb("xr0", [128, 8 * NTl], BF16), sb("xr1", [128, 8 * NTl], BF16)]
    xk = [sb("xk0", [128, 8 * NTl], BF16), sb("xk1", [128, 8 * NTl], BF16)]
    rbuf = sb("r0", [128, 8 * NTl], BF16)
    kact = sb("ka0", [128, 8 * NTl], BF16)
    mtmp = sb("mtmp", [128, NTl], F32)
    gtmp = sb("gtmp", [128, NTl], BF16)
    otmp = mtmp

    em = E(ctx, block)

    def wload(g):
        g.dma_start(out=wr[:], in_=ctx.dram["wr"][:, :]).then_inc(S["wl"].h, 16)
        g.dma_start(out=wk[:], in_=ctx.dram["wk"][:, :]).then_inc(S["wl"].h, 16)
        g.dma_start(out=wv[:], in_=ctx.dram["wv"][:, :]).then_inc(S["wl"].h, 16)
    em.add("gpsimd", wload)
    wl_done = S["wl"].inc(48)

    mu_r = ctx.smalls[:, SM["mu_r"][0]:SM["mu_r"][1]]
    mu_k = ctx.smalls[:, SM["mu_k"][0]:SM["mu_k"][1]]
    omu_r = ctx.smalls2[:, 0:8]
    omu_k = ctx.smalls2[:, 8:16]

    bank_state = {4: None, 5: None}
    ln_prev = [None]
    bcopy_done = {}
    mix_done = {}
    stage_end = {}
    evac_end = {}

    def emit_mix(k):
        par = k % 2

        apv = ln_prev[0]["apply"]

        def bcopy(g, k=k, apv=apv):
            g.wait_ge(S["ve"].h, apv)
            if k == 0:
                g.memset(xlnT[0][:, 3:8 * XS:XS], 0.0)
            else:
                g.tensor_copy(xlnT[0][:, 3:8 * XS:XS], sv[:])
            g.drain().then_inc(S["gp"].h, 1)
        em.add("gpsimd", bcopy)
        bcopy_done[k] = S["gp"].inc()

        def mix(v, k=k, par=par, apv=apv):
            v.wait_ge(S["gp"].h, bcopy_done[k])
            if k >= 2 and stage_end.get(k - 2):
                v.wait_ge(S["pe"].h, stage_end[k - 2]["k"])
            inst = None
            for c in range(8):
                cur = xlnT[par][:, c * XS + 4: c * XS + 4 + NTl]
                prv = xlnT[par][:, c * XS + 3: c * XS + 3 + NTl]
                v.tensor_scalar_mul(gtmp[:], cur, mu_r[:, c:c + 1])
                v.scalar_tensor_tensor(xr[par][:, c * NTl:(c + 1) * NTl], prv,
                                       omu_r[:, c:c + 1], gtmp[:], op0=ALU.mult, op1=ALU.add)
                v.tensor_scalar_mul(gtmp[:], cur, mu_k[:, c:c + 1])
                inst = v.scalar_tensor_tensor(xk[par][:, c * NTl:(c + 1) * NTl], prv,
                                              omu_k[:, c:c + 1], gtmp[:], op0=ALU.mult, op1=ALU.add)
            v.tensor_copy(sv[:], xlnT[0][:, XS - 1:8 * XS:XS])
            inst = v.drain()
            inst.then_inc(S["ve"].h, 1)
        em.add("vector", mix)
        mix_done[k] = S["ve"].inc()

    def emit_stages(k):
        par = k % 2
        ends = {}
        for m in range(8):
            bidx = 4 + (m % 2)
            waits = [("ve", mix_done[k]), ("wl", wl_done)]
            if bank_state[bidx]:
                waits.append(bank_state[bidx])
            pairs = [(wr[:, c * 1024 + m * 128: c * 1024 + (m + 1) * 128],
                      xr[par][:, c * NTl:(c + 1) * NTl]) for c in range(8)]
            pv = mm_group(ctx, em, ctx.bank[bidx][:, 0:NTl], pairs, waits=waits)

            def ev(s, m=m, k=k, bidx=bidx, pv=pv):
                s.wait_ge(S["pe"].h, pv)
                if m == 0 and k >= 1 and (k - 1) in evac_end:
                    s.wait_ge(S["ve"].h, evac_end[k - 1])
                s.activation(rbuf[:, m * NTl:(m + 1) * NTl],
                             ctx.bank[bidx][:, 0:NTl], AF.Sigmoid).then_inc(S["ac"].h, 1)
            em.add("scalar", ev)
            bank_state[bidx] = ("ac", S["ac"].inc())
        ends["r"] = S["pe"].v
        for m in range(8):
            bidx = 4 + (m % 2)
            waits = [("ve", mix_done[k])]
            if bank_state[bidx]:
                waits.append(bank_state[bidx])
            pairs = [(wk[:, c * 1024 + m * 128: c * 1024 + (m + 1) * 128],
                      xk[par][:, c * NTl:(c + 1) * NTl]) for c in range(8)]
            pv = mm_group(ctx, em, ctx.bank[bidx][:, 0:NTl], pairs, waits=waits)

            def ev(s, m=m, bidx=bidx, pv=pv):
                s.wait_ge(S["pe"].h, pv)
                s.activation(ctx.bank[bidx][:, 0:NTl], ctx.bank[bidx][:, 0:NTl], AF.Relu)
                s.activation(kact[:, m * NTl:(m + 1) * NTl], ctx.bank[bidx][:, 0:NTl],
                             AF.Square).then_inc(S["ac"].h, 1)
            em.add("scalar", ev)
            bank_state[bidx] = ("ac", S["ac"].inc())
        ends["k"] = S["pe"].v
        ac_kdone = S["ac"].v
        for m in range(8):
            bidx = 4 + (m % 2)
            waits = [("ac", ac_kdone)]
            if bank_state[bidx]:
                waits.append(bank_state[bidx])
            pairs = [(wv[:, c * 1024 + m * 128: c * 1024 + (m + 1) * 128],
                      kact[:, c * NTl:(c + 1) * NTl]) for c in range(8)]
            pv = mm_group(ctx, em, ctx.bank[bidx][:, 0:NTl], pairs, waits=waits)

            def ev(v, m=m, k=k, bidx=bidx, pv=pv, par=par):
                v.wait_ge(S["pe"].h, pv)
                v.tensor_tensor(otmp[:], ctx.bank[bidx][:, 0:NTl],
                                rbuf[:, m * NTl:(m + 1) * NTl], ALU.mult)
                rv = ctx.resid[:, m * T + k * NTl: m * T + k * NTl + NTl]
                v.scalar_tensor_tensor(rv, otmp[:], 0.5, rv,
                                       op0=ALU.mult, op1=ALU.add).then_inc(S["ve"].h, 1)
            em.add("vector", ev)
            bank_state[bidx] = ("ve", S["ve"].inc())
        ends["v"] = S["pe"].v
        evac_end[k] = S["ve"].v
        stage_end[k] = ends

    def src_of(k):
        return lambda c, k=k: ctx.resid[:, c * T + k * NTl: c * T + (k + 1) * NTl]

    def src_ready_of(k):
        tr = getattr(ctx, "resid_tile_ready", None)
        return tr[k] if tr else ctx.resid_ready

    cast_vals = {0: emit_ln_cast(ctx, em, NTl, src_of(0), src_ready=src_ready_of(0))}
    for k in range(ntl):
        par = k % 2
        lnp = ln_prev[0]
        res = emit_ln_tile(
            ctx, em, k, NTl,
            src_fn=src_of(k),
            mode="bf16",
            dst_fn=lambda c, k=k, par=par: xlnT[par][:, c * XS + 4: c * XS + 4 + NTl],
            row_off=(k % 2) * 512,
            bank_sxx_free=(("ve", lnp["var"]) if lnp else None),
            bank_ab_free=(("ve", lnp["apply"]) if lnp else None),
            src_ready=src_ready_of(k), precast=cast_vals[k],
            apply_waits=([("ve", mix_done[k - 1])] if (k - 1) in mix_done else None))
        ln_prev[0] = res
        emit_mix(k)
        if k + 1 < ntl:
            cast_vals[k + 1] = emit_ln_cast(ctx, em, NTl, src_of(k + 1),
                                            src_ready=src_ready_of(k + 1),
                                            cast_guard=[("pe", res["stats"])])
        if k >= 1:
            emit_stages(k - 1)
    emit_stages(ntl - 1)
    em.flush()
    ctx.resid_ready = [("ve", S["ve"].v)]
    ctx.resid_chunk_ready = None
    ctx.resid_tile_ready = None


DRAM_DECLS = [
    ("wr", [128, 8 * 1024]), ("wk", [128, 8 * 1024]), ("wv", [128, 8 * 1024]),
    ("wq", [8, 128, 8 * 128]), ("wkk", [8, 128, 8 * 128]), ("wvv", [8, 128, 8 * 128]),
    ("wo", [128, 8 * 1024]), ("peT", [1024, 2048]),
    ("win", [32, 128, 8 * 128]), ("wbc", [32, 128, 16 * 128]), ("wout", [8, 128, 16 * 128]),
    ("wconv", [128, 64]),
    ("bw", [128, 8 * 1024]), ("rm", [128, 8 * 16]), ("bflat", [64, 1024]),
    ("smalls", [128, NSMALL]), ("rows", [1, 1040]), ("ident", [128, 128]),
]


def emit_setup(ctx, block):
    nc, S = ctx.nc, ctx.sems
    em = E(ctx, block)

    def loads(sy):
        sy.dma_start(out=ctx.smalls[:], in_=ctx.dram["smalls"][:, :]).then_inc(S["xl"].h, 16)
        sy.dma_start(out=ctx.wconv[:], in_=ctx.dram["wconv"][:, :]).then_inc(S["xl"].h, 16)
    em.add("sync", loads)
    xl_v = S["xl"].inc(32)

    wl_tgt = S["wl"].v + 32

    def gloads(g, wl_tgt=wl_tgt):
        g.dma_start(out=ctx.rows_bf[:], in_=ctx.dram["rows"][:, :]).then_inc(S["wl"].h, 16)
        g.dma_start(out=ctx.idn[:], in_=ctx.dram["ident"][:, :]).then_inc(S["wl"].h, 16)
        g.memset(ctx.ones_pcol_bf[:], 1.0)
        g.memset(ctx.ones_row_f[:], 1.0)
        g.memset(ctx.ones_row_bf[:], 1.0)
        g.wait_ge(S["wl"].h, wl_tgt)
    em.add("gpsimd", gloads)
    S["wl"].inc(32)

    def vsetup(v, xl_v=xl_v):
        v.wait_ge(S["xl"].h, xl_v)
        mu_r = ctx.smalls[:, SM["mu_r"][0]:SM["mu_r"][1]]
        mu_k = ctx.smalls[:, SM["mu_k"][0]:SM["mu_k"][1]]
        v.tensor_scalar(ctx.smalls2[:, 0:8], mu_r, -1.0, 1.0, op0=ALU.mult, op1=ALU.add)
        v.tensor_scalar(ctx.smalls2[:, 8:16], mu_k, -1.0, 1.0, op0=ALU.mult, op1=ALU.add)
        v.memset(ctx.epscol[:], EPS)
        for j in range(16):
            v.reduce_sum(ctx.smalls2[:, 16 + j:17 + j], ctx.wconv[:, 4 * j:4 * j + 4], axis=AX.X)
        v.drain()
        binu_ = ctx.smalls[:, SM["b_in_u"][0]:SM["b_in_u"][1]]
        cvb_ = ctx.smalls[:, SM["conv_b"][0]:SM["conv_b"][1]]
        for j in range(16):
            v.scalar_tensor_tensor(ctx.smalls2[:, 32 + j:33 + j], binu_[:, j:j + 1],
                                   ctx.smalls2[:, 16 + j:17 + j], cvb_[:, j:j + 1],
                                   op0=ALU.mult, op1=ALU.add)
        v.memset(ctx.zcol[:], 0.0)
        v.memset(ctx.ln8col[:], -2.0794415416798357)
    em.add("vector", vsetup)

    def ssetup(s, xl_v=xl_v):
        s.wait_ge(S["xl"].h, xl_v)
        ld = ctx.smalls[:, SM["log_decay"][0]:SM["log_decay"][1]]
        s.activation(ctx.adecay[:], ld, AF.Sigmoid, scale=-1.0)
    em.add("scalar", ssetup)
    em.flush()


def emit_load(ctx, block, seq):
    S = ctx.sems
    em = E(ctx, block)
    st_base = S["st"].v - 128 if S["st"].v else 0
    first = S["st"].v == 0

    def loads(sy, st_base=st_base, first=first):
        for kt in range(4):
            for c in range(8):
                if not first and kt == 0:
                    sy.wait_ge(S["st"].h, st_base + 16 * (c + 1))
                sy.dma_start(
                    out=ctx.resid[:, c * T + kt * 512: c * T + (kt + 1) * 512],
                    in_=ctx.dram["xT"][seq, c * 128:(c + 1) * 128,
                                       kt * 512:(kt + 1) * 512]).then_inc(S["xl"].h, 16)
    em.add("sync", loads)
    xl0 = S["xl"].v
    ctx.resid_tile_ready = [[("xl", xl0 + 128 * (kt + 1))] for kt in range(4)]
    ctx.resid_ready = [("xl", S["xl"].inc(512))]
    ctx.resid_chunk_ready = None
    em.flush()


def emit_store(ctx, block, seq, last):
    S = ctx.sems
    em = E(ctx, block)
    waits = list(ctx.resid_ready)
    chunk_ready = getattr(ctx, "resid_chunk_ready", None)
    st_final = S["st"].v + 128

    def stores(sy, st_final=st_final, chunk_ready=chunk_ready):
        for c in range(8):
            if chunk_ready is not None:
                sname, val = chunk_ready[c]
                sy.wait_ge(S[sname].h, val)
            elif c == 0:
                for sname, val in waits:
                    sy.wait_ge(S[sname].h, val)
            sy.dma_start(out=ctx.dram["outT"][seq, c * 128:(c + 1) * 128, :],
                         in_=ctx.resid[:, c * T:(c + 1) * T]).then_inc(S["st"].h, 16)
        if last:
            sy.wait_ge(S["st"].h, st_final)
    em.add("sync", stores)
    S["st"].inc(128)
    em.flush()


def build_program(subs):
    nc = bass.Bass("TRN2", target_bir_lowering=False)
    dram = {"xT": nc.dram_tensor("xT", [2, 1024, 2048], F32, kind="ExternalInput").ap(),
            "outT": nc.dram_tensor("outT", [2, 1024, 2048], F32, kind="ExternalOutput").ap()}
    for name, shape in DRAM_DECLS:
        dram[name] = nc.dram_tensor(name, shape, F32, kind="ExternalInput").ap()
    for name, shape in (("win16", [32, 128, 8 * 128]), ("wbc16", [32, 128, 16 * 128]),
                        ("wout16", [8, 128, 16 * 128])):
        dram[name] = nc.dram_tensor(name, shape, BF16, kind="Internal").ap()
    with ExitStack() as es:
        ctx = Ctx(nc, es)
        ctx.dram = dram
        ctx.resid = ctx.sb("resid", [128, 8 * T], F32)
        ctx.xcastT = ctx.sb("xcastT", [128, 8 * 512], BF16)
        ctx.xsqT = ctx.sb("xsqT", [128, 8 * 512], BF16)
        ctx.statS = ctx.sb("statS", [1, 1536], F32)
        ctx.rowA = ctx.sb("rowA", [1, 1024], F32)
        ctx.rowB = ctx.sb("rowB", [1, 1024], F32)
        ctx.lnTmp = ctx.sb("lnTmp", [128, 512], F32)
        ctx.smalls = ctx.sb("sb_smalls", [128, NSMALL], F32)
        ctx.smalls2 = ctx.sb("smalls2", [128, 48], F32)
        ctx.rows_bf = ctx.sb("rows_bf", [1, 1040], BF16)
        ctx.ones_pcol_bf = ctx.sb("ones_pcol_bf", [128, 1], BF16)
        ctx.ones_row_f = ctx.sb("ones_row_f", [1, 128], F32)
        ctx.ones_row_bf = ctx.sb("ones_row_bf", [1, 128], BF16)
        ctx.idn = ctx.sb("idn", [128, 128], BF16)
        ctx.wconv = ctx.sb("sb_wconv", [128, 64], F32)
        ctx.adecay = ctx.sb("adecay", [128, 16], F32)
        ctx.epscol = ctx.sb("epscol", [128, 1], F32)
        ctx.zcol = ctx.sb("zcol", [128, 1], F32)
        ctx.ln8col = ctx.sb("ln8col", [128, 1], F32)
        ctx.carry = ctx.sb("carry", [128, 16], F32)
        ctx.tails = ctx.sb("tails", [128, 96], F32)

        with nc.Block("setup") as b:
            emit_setup(ctx, b)
        uid = 0
        for seq in range(2):
            with nc.Block(f"load{seq}") as b:
                emit_load(ctx, b, seq)
            for sname in subs:
                with nc.Block(f"{sname}_{uid}") as b:
                    with ExitStack() as scope:
                        EMITTERS[sname](ctx, b, scope, uid)
                    uid += 1
            with nc.Block(f"store{seq}") as b:
                emit_store(ctx, b, seq, last=(seq == 1))
    return nc


EMITTERS = {}

SUBS = ["l0ffn", "l0attn", "l0ffn", "l1ffn", "l1attn", "l1ffn"]
FUSED = True


def kernel(**inputs):
    prep = host_prep(inputs)
    x = inputs["L0_attn_x"]
    xTs = [np.ascontiguousarray(x[2 * c:2 * c + 2].transpose(0, 2, 1)) for c in range(8)]
    groups = [SUBS] if FUSED else [[s] for s in SUBS]
    cur = xTs
    for g in groups:
        nc = build_program(g)
        in_maps = [dict(prep, xT=cur[c]) for c in range(8)]
        res = run_bass_kernel_spmd(nc, in_maps, core_ids=list(range(8)))
        cur = [np.asarray(res.results[c]["outT"]) for c in range(8)]
    out = np.empty((16, 2048, 1024), np.float32)
    for c in range(8):
        out[2 * c:2 * c + 2] = cur[c].transpose(0, 2, 1)
    return out


def emit_l1ffn(ctx, block, scope, uid):
    nc, S = ctx.nc, ctx.sems
    NTl, ntl = 512, 4
    sb = lambda n, sh, dt: scope.enter_context(nc.sbuf_tensor(f"{n}_{uid}", sh, dt))
    xlnT = [sb("xln0", [128, 8 * NTl], BF16), sb("xln1", [128, 8 * NTl], BF16)]
    bw = sb("bw", [128, 8 * 1024], BF16)
    rm = sb("rm", [128, 8 * 16], BF16)
    bflat = sb("bflat", [64, 1024], BF16)
    rmf = sb("rmf", [128, 64], F32)
    mx = sb("mx", [128, 16], F32)
    gat = sb("gat", [128, 32], F32)
    gbf = sb("gbf", [128, 32], BF16)
    gatf = sb("gatf", [128, 32], F32)
    G = sb("G", [128, 64], BF16)
    gt = sb("gt", [64, 512], BF16)
    lora = sb("lora", [128, 8 * NTl], F32)
    bbuf = sb("bbuf", [128, 8 * NTl], F32)

    em = E(ctx, block)

    def wload(g):
        g.dma_start(out=bw[:], in_=ctx.dram["bw"][:, :]).then_inc(S["wl"].h, 16)
        g.dma_start(out=rm[:], in_=ctx.dram["rm"][:, :]).then_inc(S["wl"].h, 16)
        g.dma_start(out=bflat[:], in_=ctx.dram["bflat"][:, :]).then_inc(S["wl"].h, 16)
    em.add("gpsimd", wload)
    wl_done = S["wl"].inc(48)

    bb = ctx.smalls[:, SM["base_b"][0]:SM["base_b"][1]]
    B6 = ctx.bank[6]
    ln_prev = [None]
    stage_end = {}
    ve_end = {}

    def emit_stage(k):
        par = k % 2
        lnv = ln_apply[k]
        v1s = {}
        for tt in range(4):
            bidx = 4 + (tt % 2)
            waits = [("ve", lnv), ("wl", wl_done)]
            if bank_state[bidx]:
                waits.append(bank_state[bidx])
            pairs = [(xlnT[par][:, c * NTl + tt * 128: c * NTl + (tt + 1) * 128],
                      rm[:, c * 16:(c + 1) * 16]) for c in range(8)]
            pairs.append((ctx.ones_row_bf[:], ctx.rows_bf[0:1, 1024:1040]))
            pv = mm_group(ctx, em, ctx.bank[bidx][:, 0:16], pairs, waits=waits)

            def small(v, tt=tt, bidx=bidx, pv=pv):
                v.wait_ge(S["pe"].h, pv)
                v.tensor_copy(rmf[:, 16 * tt:16 * tt + 16], ctx.bank[bidx][:, 0:16])
                v.reduce_max(mx[:, 4 * tt:4 * tt + 1], ctx.bank[bidx][:, 0:8], axis=AX.X)
                v.drain()
                v.tensor_scalar_mul(mx[:, 4 * tt + 1:4 * tt + 2],
                                    mx[:, 4 * tt:4 * tt + 1], -1.0)
                v.drain().then_inc(S["ve"].h, 1)
            em.add("vector", small)
            v1s[tt] = S["ve"].inc()
            bank_state[bidx] = ("ve", v1s[tt])

        a2s = {}
        for m in range(8):
            bidx = 4 + (m % 2)
            waits = [("ve", lnv)]
            if bank_state[bidx]:
                waits.append(bank_state[bidx])
            pairs = [(bw[:, c * 1024 + m * 128: c * 1024 + (m + 1) * 128],
                      xlnT[par][:, c * NTl:(c + 1) * NTl]) for c in range(8)]
            pv = mm_group(ctx, em, ctx.bank[bidx][:, 0:NTl], pairs, waits=waits)

            def bev(s, m=m, bidx=bidx, pv=pv, rp=r1_of_m.get(m)):
                s.wait_ge(S["pe"].h, pv)
                if rp is not None:
                    s.wait_ge(S["ve"].h, rp)
                s.activation(bbuf[:, m * NTl:(m + 1) * NTl], ctx.bank[bidx][:, 0:NTl],
                             AF.Identity, bias=bb[:, m:m + 1]).then_inc(S["ac"].h, 1)
            em.add("scalar", bev)
            a2s[m] = S["ac"].inc()
            bank_state[bidx] = ("ac", a2s[m])

        def sexp(s):
            for tt in range(4):
                s.wait_ge(S["ve"].h, v1s[tt])
            inst = None
            for tt in range(4):
                inst = s.activation(gat[:, 8 * tt:8 * tt + 8],
                                    rmf[:, 16 * tt:16 * tt + 8], AF.Exp,
                                    bias=mx[:, 4 * tt + 1:4 * tt + 2])
            inst.then_inc(S["ac"].h, 1)
        em.add("scalar", sexp)
        a1 = S["ac"].inc()

        def vsum(v, a1=a1):
            v.wait_ge(S["ac"].h, a1)
            for tt in range(4):
                v.reduce_sum(mx[:, 4 * tt + 2:4 * tt + 3],
                             gat[:, 8 * tt:8 * tt + 8], axis=AX.X)
            v.drain()
            v.reciprocal(mx[:, 3:16:4], mx[:, 2:16:4])
            v.drain().then_inc(S["ve"].h, 1)
        em.add("vector", vsum)
        vsum_v = S["ve"].inc()

        gt_ready = []
        for tt in range(4):
            def small2(v, tt=tt, vsum_v=vsum_v):
                v.wait_ge(S["ve"].h, vsum_v)
                v.tensor_scalar_mul(gbf[:, 8 * tt:8 * tt + 8], gat[:, 8 * tt:8 * tt + 8],
                                    mx[:, 4 * tt + 3:4 * tt + 4])
                v.drain()
                v.tensor_copy(gatf[:, 8 * tt:8 * tt + 8], gbf[:, 8 * tt:8 * tt + 8])
                v.drain()
                inst = None
                for e in range(8):
                    inst = v.tensor_scalar_mul(G[:, 8 * e:8 * e + 8],
                                               rmf[:, 16 * tt + 8:16 * tt + 16],
                                               gatf[:, 8 * tt + e:8 * tt + e + 1])
                v.drain().then_inc(S["ve"].h, 1)
            em.add("vector", small2)
            v2 = S["ve"].inc()

            bf_guard = tuple(bankb_free[0]) if (tt == 0 and bankb_free[0]) else None

            def tp(t, tt=tt, v2=v2, bf_guard=bf_guard):
                t.wait_ge(S["ve"].h, v2)
                if bf_guard:
                    t.wait_ge(S[bf_guard[0]].h, bf_guard[1])
                t.transpose(ctx.bankb[0:64, tt * 128:(tt + 1) * 128], G[:], ctx.idn[:]
                            ).then_inc(S["pe"].h, 1)
            em.add("tensor", tp)
            p2 = S["pe"].inc()

            def gtc(v, tt=tt, p2=p2):
                v.wait_ge(S["pe"].h, p2)
                v.tensor_copy(gt[:, tt * 128:(tt + 1) * 128],
                              ctx.bankb[0:64, tt * 128:(tt + 1) * 128]).then_inc(S["ve"].h, 1)
            em.add("vector", gtc)
            gt_ready.append(S["ve"].inc())
        bankb_free[0] = ("ve", gt_ready[-1])

        lora_ready = []
        rg = rups_done.get(k - 1)
        for m in range(8):
            waits = [("ve", gt_ready[-1])]
            if bank_state[6]:
                waits.append(bank_state[6])
            pv = mm_group(ctx, em, B6[:, 0:NTl],
                          [(bflat[0:64, m * 128:(m + 1) * 128], gt[0:64, 0:NTl])],
                          waits=waits)

            def lev(v, m=m, pv=pv, rg=rg):
                v.wait_ge(S["pe"].h, pv)
                if m == 0 and rg is not None:
                    v.wait_ge(S["gp"].h, rg)
                v.tensor_copy(lora[:, m * NTl:(m + 1) * NTl], B6[:, 0:NTl]).then_inc(S["ve"].h, 1)
            em.add("vector", lev)
            bank_state[6] = ("ve", S["ve"].inc())
            lora_ready.append(bank_state[6][1])

        for m in range(8):
            def rup1(v, m=m, a2=a2s[m], lr=lora_ready[m]):
                v.wait_ge(S["ac"].h, a2)
                v.wait_ge(S["ve"].h, lr)
                lm = lora[:, m * NTl:(m + 1) * NTl]
                v.scalar_tensor_tensor(lm, bbuf[:, m * NTl:(m + 1) * NTl], 0.5, lm,
                                       op0=ALU.mult, op1=ALU.add).then_inc(S["ve"].h, 1)
            em.add("vector", rup1)
            r1 = S["ve"].inc()
            r1_of_m[m] = r1

            def rup(g, m=m, k=k, r1=r1):
                g.wait_ge(S["ve"].h, r1)
                lm = lora[:, m * NTl:(m + 1) * NTl]
                rv = ctx.resid[:, m * T + k * NTl: m * T + (k + 1) * NTl]
                g.tensor_tensor(rv, rv, lm, ALU.add).then_inc(S["gp"].h, 1)
            em.add("gpsimd", rup)
            S["gp"].inc()
            if k == ntl - 1:
                chunk_gp[m] = S["gp"].v
        rups_done[k] = S["gp"].v
        stage_end[k] = S["pe"].v

    bank_state = {4: None, 5: None, 6: None}
    bankb_free = [None]
    rups_done = {}
    r1_of_m = {}
    chunk_gp = {}
    ln_apply = {}
    def src_of(k):
        return lambda c, k=k: ctx.resid[:, c * T + k * NTl: c * T + (k + 1) * NTl]

    cast_vals = {0: emit_ln_cast(ctx, em, NTl, src_of(0), src_ready=ctx.resid_ready)}
    for k in range(ntl):
        lnp = ln_prev[0]
        aw = [("pe", stage_end[k - 2])] if (k - 2) in stage_end else None
        res = emit_ln_tile(
            ctx, em, k, NTl,
            src_fn=src_of(k),
            mode="bf16",
            dst_fn=lambda c, k=k: xlnT[k % 2][:, c * NTl:(c + 1) * NTl],
            row_off=(k % 2) * 512,
            bank_sxx_free=(("ve", lnp["var"]) if lnp else None),
            bank_ab_free=(("ve", lnp["apply"]) if lnp else None),
            src_ready=ctx.resid_ready, precast=cast_vals[k], apply_waits=aw)
        ln_prev[0] = res
        ln_apply[k] = res["apply"]
        if k + 1 < ntl:
            cast_vals[k + 1] = emit_ln_cast(ctx, em, NTl, src_of(k + 1),
                                            src_ready=ctx.resid_ready,
                                            cast_guard=[("pe", res["stats"])])
        if k >= 1:
            emit_stage(k - 1)
    emit_stage(ntl - 1)
    em.flush()
    ctx.resid_ready = [("gp", S["gp"].v)]
    ctx.resid_chunk_ready = [("gp", chunk_gp[m]) for m in range(8)]
    ctx.resid_tile_ready = None


def emit_l0attn(ctx, block, scope, uid):
    nc, S = ctx.nc, ctx.sems
    NTl = 512
    sb = lambda n, sh, dt: scope.enter_context(nc.sbuf_tensor(f"{n}_{uid}", sh, dt))
    xattn = sb("xattn", [128, 8 * 2048], BF16)
    xpe = sb("xpe", [128, 8 * 2048], BF16)
    em = E(ctx, block)
    bq = ctx.smalls[:, SM["bq"][0]:SM["bq"][1]]
    bk = ctx.smalls[:, SM["bk"][0]:SM["bk"][1]]
    bo = ctx.smalls[:, SM["bo"][0]:SM["bo"][1]]
    gam = ctx.smalls[:, SM["gamma"][0]:SM["gamma"][1]]
    bet = ctx.smalls[:, SM["beta"][0]:SM["beta"][1]]
    B0, B1, B2, B3, B6 = ctx.bank[0], ctx.bank[1], ctx.bank[2], ctx.bank[3], ctx.bank[6]

    with ExitStack() as sA:
        sba = lambda n, sh, dt: sA.enter_context(nc.sbuf_tensor(f"{n}_{uid}", sh, dt))
        wqb = sba("wqb", [128, 1024], BF16)
        wkb = sba("wkb", [128, 1024], BF16)
        wvb = sba("wvb", [128, 1024], BF16)
        ksb = [sba("ksb0", [128, 2048], BF16), sba("ksb1", [128, 2048], BF16)]
        vsb = [sba("vsb0", [128, 16 * 130], BF16), sba("vsb1", [128, 16 * 130], BF16)]

        B4, B5, B6 = ctx.bank[4], ctx.bank[5], ctx.bank[6]
        streg = [ctx.psA[:, 0:1024], ctx.psA[:, 1024:2048]]
        bank_state = {6: None}
        streg_state = {0: None, 1: None}
        pv_free = {0: None, 1: None}
        wl_ready = {}
        proj_end = {}
        qkv_done = {}
        mb_done = {}

        def wload(p):
            guard = proj_end.get(p - 1)

            def f(g, p=p, guard=guard):
                if guard is not None:
                    g.wait_ge(S["pe"].h, guard)
                for nm, buf in (("wq", wqb), ("wkk", wkb), ("wvv", wvb)):
                    g.dma_start(out=buf[:, 0:1024],
                                in_=ctx.dram[nm][p, :, :]).then_inc(S["wl"].h, 16)
            em.add("gpsimd", f)
            wl_ready[p] = S["wl"].inc(48)

        def vones(v):
            for par in range(2):
                v.memset(vsb[par][:, 64:16 * 130:130], 1.0)
                v.memset(vsb[par][:, 129:16 * 130:130], 1.0)
        em.add("vector", vones)
        wload(0)

        def qk_step0(wbuf, dst, bias, qt, wait_v):
            waits = [("ve", wait_v), ("wl", wl_ready[0])]
            if bank_state[6]:
                waits.append(bank_state[6])
            pairs = [(wbuf[:, c * 128:(c + 1) * 128],
                      xpe[:, c * 2048 + qt * NTl: c * 2048 + (qt + 1) * NTl])
                     for c in range(8)]
            pv = mm_group(ctx, em, B6[:, 0:NTl], pairs, waits=waits)
            proj_end[0] = pv

            def ev(v, dst=dst, bias=bias, qt=qt, pv=pv):
                v.wait_ge(S["pe"].h, pv)
                v.tensor_scalar_add(dst[:, qt * NTl:(qt + 1) * NTl],
                                    B6[:, 0:NTl], bias[:, 0:1]).then_inc(S["ve"].h, 1)
            em.add("vector", ev)
            bank_state[6] = ("ve", S["ve"].inc())
            qkv_done[0] = bank_state[6][1]

        def v_step0(tt, wait_v):
            waits = [("ve", wait_v), ("wl", wl_ready[0])]
            if bank_state[6]:
                waits.append(bank_state[6])
            pairs = [(xpe[:, c * 2048 + tt * 128: c * 2048 + (tt + 1) * 128],
                      wvb[:, c * 128:(c + 1) * 128])
                     for c in range(8)]
            pairs.append((ctx.ones_row_bf[:], ctx.rows_bf[0:1, 0:128]))
            pv = mm_group(ctx, em, B6[:, 0:128], pairs, waits=waits)
            proj_end[0] = pv

            def vev(v, tt=tt, pv=pv):
                v.wait_ge(S["pe"].h, pv)
                v.tensor_copy(vsb[0][:, tt * 130:tt * 130 + 64], B6[:, 0:64])
                v.tensor_copy(vsb[0][:, tt * 130 + 65:tt * 130 + 129],
                              B6[:, 64:128]).then_inc(S["ve"].h, 1)
            em.add("vector", vev)
            bank_state[6] = ("ve", S["ve"].inc())
            qkv_done[0] = bank_state[6][1]

        ln_prev = [None]
        pe_ready = {}
        with ExitStack() as sA1:
            pet = sA1.enter_context(nc.sbuf_tensor(f"pet_{uid}", [128, 8 * NTl], BF16))
            for k in range(4):
                ap_prev = ln_prev[0]["apply"] if ln_prev[0] else None

                def peload(g, k=k, ap_prev=ap_prev):
                    if ap_prev is not None:
                        g.wait_ge(S["ve"].h, ap_prev)
                    for c in range(8):
                        g.dma_start(out=pet[:, c * NTl:(c + 1) * NTl],
                                    in_=ctx.dram["peT"][c * 128:(c + 1) * 128,
                                                        k * NTl:(k + 1) * NTl]).then_inc(S["wl"].h, 16)
                em.add("gpsimd", peload)
                pe_ready[k] = S["wl"].inc(128)
                lnp = ln_prev[0]
                res = emit_ln_tile(
                    ctx, em, k, NTl,
                    src_fn=lambda c, k=k: ctx.resid[:, c * T + k * NTl: c * T + (k + 1) * NTl],
                    mode="pe",
                    dst_fn=lambda c, k=k: xpe[:, c * 2048 + k * NTl: c * 2048 + (k + 1) * NTl],
                    extra_fn=lambda c: pet[:, c * NTl:(c + 1) * NTl],
                    row_off=(k % 2) * 512,
                    bank_sxx_free=(("ve", lnp["var"]) if lnp else None),
                    bank_ab_free=(("ve", lnp["apply"]) if lnp else None),
                    src_ready=ctx.resid_ready,
                    extra_ready=[("wl", pe_ready[k])], cast_eng="vector",
                    cast_guard=([("pe", lnp["stats"])] if lnp else None))
                ln_prev[0] = res
                qk_step0(wkb, ksb[0], bk, k, res["apply"])
                for tt in range(4 * k, 4 * k + 4):
                    v_step0(tt, res["apply"])
        xpe_done = ln_prev[0]["apply"]
        qsb = [sba("qsb0", [128, 2048], BF16), sba("qsb1", [128, 2048], BF16)]
        pch = sba("pch", [128, 4 * 1024], BF16)
        nrm = sba("nrm", [1, 64], F32)
        mbias = sba("mbias", [128, 4], F32)
        out65x = sba("o65a", [65, 512], F32)
        out65 = [out65x, out65x]
        for qt in range(4):
            qk_step0(wqb, qsb[0], bq, qt, xpe_done)

        def make_proj_steps(p):
            par = p % 2
            steps = []
            for wbuf, dst, bias in ((wqb, qsb[par], bq), (wkb, ksb[par], bk)):
                for qt in range(4):
                    def step(wbuf=wbuf, dst=dst, bias=bias, qt=qt, p=p, par=par):
                        waits = [("ve", xpe_done), ("wl", wl_ready[p])]
                        if bank_state[6]:
                            waits.append(bank_state[6])
                        pairs = [(wbuf[:, c * 128:(c + 1) * 128],
                                  xpe[:, c * 2048 + qt * NTl: c * 2048 + (qt + 1) * NTl])
                                 for c in range(8)]
                        pv = mm_group(ctx, em, B6[:, 0:NTl], pairs, waits=waits)
                        proj_end[p] = pv

                        def ev(v, dst=dst, bias=bias, p=p, qt=qt, pv=pv):
                            v.wait_ge(S["pe"].h, pv)
                            v.tensor_scalar_add(dst[:, qt * NTl:(qt + 1) * NTl],
                                                B6[:, 0:NTl],
                                                bias[:, p:p + 1]).then_inc(S["ve"].h, 1)
                        em.add("vector", ev)
                        bank_state[6] = ("ve", S["ve"].inc())
                        qkv_done[p] = bank_state[6][1]
                    steps.append(step)
            for tt in range(16):
                def step(tt=tt, p=p, par=par):
                    waits = [("ve", xpe_done), ("wl", wl_ready[p])]
                    if bank_state[6]:
                        waits.append(bank_state[6])
                    pairs = [(xpe[:, c * 2048 + tt * 128: c * 2048 + (tt + 1) * 128],
                              wvb[:, c * 128:(c + 1) * 128])
                             for c in range(8)]
                    pairs.append((ctx.ones_row_bf[:], ctx.rows_bf[0:1, p * 128:(p + 1) * 128]))
                    pv = mm_group(ctx, em, B6[:, 0:128], pairs, waits=waits)
                    proj_end[p] = pv

                    def vev(v, tt=tt, par=par, pv=pv):
                        v.wait_ge(S["pe"].h, pv)
                        v.tensor_copy(vsb[par][:, tt * 130:tt * 130 + 64], B6[:, 0:64])
                        v.tensor_copy(vsb[par][:, tt * 130 + 65:tt * 130 + 129],
                                      B6[:, 64:128]).then_inc(S["ve"].h, 1)
                    em.add("vector", vev)
                    bank_state[6] = ("ve", S["ve"].inc())
                    qkv_done[p] = bank_state[6][1]
                steps.append(step)
            return steps

        def emit_norm(p):
            par = p % 2

            def sq(v, p=p, par=par, qd=qkv_done[p]):
                v.wait_ge(S["ve"].h, qd)
                v.tensor_tensor(ctx.xcastT[:, 0:2048], qsb[par][:], qsb[par][:], ALU.mult)
                v.tensor_tensor(ctx.xcastT[:, 2048:4096], ksb[par][:], ksb[par][:],
                                ALU.mult).then_inc(S["ve"].h, 1)
            em.add("vector", sq)
            sq_done = S["ve"].inc()
            for h2 in range(2):
                for i, base in enumerate((0, 2048)):
                    for qt in range(4):
                        waits = [("ve", sq_done)]
                        if bank_state[6]:
                            waits.append(bank_state[6])
                        pv = mm_group(ctx, em, B6[0:1, 0:NTl],
                                      [(ctx.ones_pcol_bf[64 * h2:64 * h2 + 64, :],
                                        ctx.xcastT[64 * h2:64 * h2 + 64,
                                                   base + qt * NTl: base + (qt + 1) * NTl])],
                                      waits=waits)

                        def nrd(v, h2=h2, i=i, qt=qt, pv=pv):
                            v.wait_ge(S["pe"].h, pv)
                            v.reduce_max(nrm[0:1, 8 * h2 + 4 * i + qt: 8 * h2 + 4 * i + qt + 1],
                                         B6[0:1, 0:NTl], axis=AX.X).then_inc(S["ve"].h, 1)
                        em.add("vector", nrd)
                        bank_state[6] = ("ve", S["ve"].inc())

            def mcomb_v(v):
                v.drain()
                for h2 in range(2):
                    v.reduce_max(nrm[0:1, 16 + h2:17 + h2], nrm[0:1, 8 * h2:8 * h2 + 4], axis=AX.X)
                    v.reduce_max(nrm[0:1, 18 + h2:19 + h2], nrm[0:1, 8 * h2 + 4:8 * h2 + 8], axis=AX.X)
                v.drain()
                for h2 in range(2):
                    v.tensor_tensor(nrm[0:1, 20 + h2:21 + h2], nrm[0:1, 16 + h2:17 + h2],
                                    nrm[0:1, 18 + h2:19 + h2], ALU.mult)
                v.drain().then_inc(S["ve"].h, 1)
            em.add("vector", mcomb_v)
            vq = S["ve"].inc()

            def mcomb_s(s, vq=vq):
                s.wait_ge(S["ve"].h, vq)
                s.activation(nrm[0:1, 22:24], nrm[0:1, 20:22], AF.Ln, bias=ctx.zcol[0:1, :])
                s.drain()
                s.activation(nrm[0:1, 24:26], nrm[0:1, 22:24], AF.Exp, scale=0.5,
                             bias=ctx.ln8col[0:1, :])
                s.drain().then_inc(S["ac"].h, 1)
            em.add("scalar", mcomb_s)
            aM = S["ac"].inc()

            def mneg(v, aM=aM):
                v.wait_ge(S["ac"].h, aM)
                v.tensor_scalar_mul(nrm[0:1, 26:28], nrm[0:1, 24:26], -1.0)
                v.drain().then_inc(S["ve"].h, 1)
            em.add("vector", mneg)
            vneg = S["ve"].inc()
            for h2 in range(2):
                waits = [("ve", vneg)]
                if bank_state[6]:
                    waits.append(bank_state[6])
                pv = mm_group(ctx, em, B6[:, 0:1],
                              [(ctx.ones_row_f[:], nrm[0:1, 26 + h2:27 + h2])], waits=waits)

                def mbc(v, h2=h2, par=par, pv=pv):
                    v.wait_ge(S["pe"].h, pv)
                    v.tensor_copy(mbias[:, par * 2 + h2:par * 2 + h2 + 1],
                                  B6[:, 0:1]).then_inc(S["ve"].h, 1)
                em.add("vector", mbc)
                bank_state[6] = ("ve", S["ve"].inc())
            mb_done[p] = S["ve"].v

        def emit_inner(p, fillers):
            par = p % 2
            for qt in range(4):
                expv = {}
                pvv = {}

                def emit_pv(kt, qt=qt, par=par, expv=expv, pvv=pvv):
                    for h2, pvbank in ((0, B4), (1, B5)):
                        g = pv_free[h2] if kt == 0 else None

                        def pvmm(t, kt=kt, h2=h2, pvbank=pvbank, exv=expv[(h2, kt // 2)],
                                 g=g, par=par):
                            t.wait_ge(S["ac"].h, exv)
                            if g:
                                t.wait_ge(S[g[0]].h, g[1])
                            t.matmul(pvbank[0:65, 0:NTl],
                                     vsb[par][:, kt * 130 + 65 * h2: kt * 130 + 65 * h2 + 65],
                                     pch[:, ((h2 * 2 + kt // 2 % 2) * 1024 + (kt % 2) * 512):
                                          ((h2 * 2 + kt // 2 % 2) * 1024 + (kt % 2) * 512) + NTl],
                                     start=(kt == 0), stop=(kt == 15)).then_inc(S["pe"].h, 1)
                        em.add("tensor", pvmm)
                        pvv[(h2, kt)] = S["pe"].inc()

                for kp in range(8):
                    if fillers:
                        fillers.pop(0)()
                    stv = {}
                    for h2 in range(2):
                        waits = [("ve", qkv_done[p])]
                        if streg_state[h2]:
                            waits.append(streg_state[h2])
                        pairs = []
                        stv[h2] = None
                        for half in range(2):
                            kt = 2 * kp + half
                            pr = [(ksb[par][64 * h2:64 * h2 + 64, kt * 128:(kt + 1) * 128],
                                   qsb[par][64 * h2:64 * h2 + 64, qt * NTl:(qt + 1) * NTl])]
                            w = waits if half == 0 else ()
                            stv[h2] = mm_group(ctx, em,
                                               streg[h2][:, half * 512:half * 512 + NTl],
                                               pr, waits=w)

                        def ex(s, kp=kp, h2=h2, par=par, stv=stv[h2],
                               mbd=(mb_done[p] if kp == 0 else None),
                               pvg=(pvv.get((h2, 2 * kp - 3)))):
                            s.wait_ge(S["pe"].h, stv)
                            if mbd is not None:
                                s.wait_ge(S["ve"].h, mbd)
                            if pvg is not None:
                                s.wait_ge(S["pe"].h, pvg)
                            s.activation(pch[:, (h2 * 2 + kp % 2) * 1024:
                                              (h2 * 2 + kp % 2) * 1024 + 1024],
                                         streg[h2][:], AF.Exp,
                                         bias=mbias[:, par * 2 + h2:par * 2 + h2 + 1],
                                         scale=0.125).then_inc(S["ac"].h, 1)
                        em.add("scalar", ex)
                        exv = S["ac"].inc()
                        expv[(h2, kp)] = exv
                        streg_state[h2] = ("ac", exv)
                    if kp >= 1:
                        emit_pv(2 * kp - 2)
                        emit_pv(2 * kp - 1)
                emit_pv(14)
                emit_pv(15)
                for h2, pvbank in ((0, B4), (1, B5)):
                    pv_done = pvv[(h2, 15)]

                    def cp65(v, pv_done=pv_done, pvbank=pvbank, h2=h2):
                        v.wait_ge(S["pe"].h, pv_done)
                        v.tensor_copy(out65[h2][0:64, 0:NTl], pvbank[0:64, 0:NTl])
                        v.tensor_copy(ctx.statS[0:1, 0:NTl], pvbank[64:65, 0:NTl])
                        v.drain().then_inc(S["ve"].h, 1)
                    em.add("vector", cp65)
                    vrec = S["ve"].inc()
                    pv_free[h2] = ("ve", vrec)
                    waits = [("ve", vrec)]
                    if bank_state[6]:
                        waits.append(bank_state[6])
                    pbc = mm_group(ctx, em, B6[0:64, 0:NTl],
                                   [(ctx.ones_row_f[0:1, 0:64], ctx.statS[0:1, 0:NTl])],
                                   waits=waits)

                    def srec(s, pbc=pbc):
                        s.wait_ge(S["pe"].h, pbc)
                        s.activation(ctx.lnTmp[0:64, 0:NTl], B6[0:64, 0:NTl],
                                     AF.Ln, bias=ctx.zcol[0:64, 0:1])
                        s.activation(B6[0:64, 0:NTl], ctx.lnTmp[0:64, 0:NTl],
                                     AF.Exp, scale=-1.0).then_inc(S["ac"].h, 1)
                    em.add("scalar", srec)
                    a_rec = S["ac"].inc()

                    def fin(v, p=p, h2=h2, qt=qt, a_rec=a_rec):
                        v.wait_ge(S["ac"].h, a_rec)
                        v.tensor_tensor(
                            xattn[64 * h2:64 * h2 + 64,
                                  p * 2048 + qt * NTl: p * 2048 + (qt + 1) * NTl],
                            out65[h2][0:64, 0:NTl], B6[0:64, 0:NTl],
                            ALU.mult).then_inc(S["ve"].h, 2)
                    em.add("vector", fin)
                    S["ve"].inc(2)
                    bank_state[6] = ("ve", S["ve"].v)
                if qt == 2 and p + 1 < 8:
                    while fillers:
                        fillers.pop(0)()
                    emit_norm(p + 1)

        pc_jobs = ([("win16", "win", i) for i in range(32)] +
                   [("wbc16", "wbc", i) for i in range(32)] +
                   [("wout16", "wout", i) for i in range(8)])

        def precast_batch(jobs):
            def f(g, jobs=tuple(jobs)):
                for d, s, i in jobs:
                    g.dma_start(out=ctx.dram[d][i, :, :],
                                in_=ctx.dram[s][i, :, :]).then_inc(S["pc"].h, 16)
            em.add("gpsimd", f)
            S["pc"].inc(16 * len(jobs))

        emit_norm(0)
        for p in range(8):
            fillers = []
            if p + 1 < 8:
                wload(p + 1)
                fillers = make_proj_steps(p + 1)
            if uid == 1:
                precast_batch(pc_jobs[9 * p: 9 * (p + 1)])
            emit_inner(p, fillers)
        xattn_done = S["ve"].v

    wo = sb("wo", [128, 8 * 1024], BF16)
    h_sb = sb("h_sb", [128, 8 * NTl], F32)

    def wol(g, xattn_done=xattn_done):
        g.wait_ge(S["ve"].h, xattn_done)
        g.dma_start(out=wo[:], in_=ctx.dram["wo"][:, :]).then_inc(S["wl"].h, 16)
    em.add("gpsimd", wol)
    wo_ready = S["wl"].inc(16)

    bank_state = {4: None, 5: None}
    ln_prev2 = [None]
    for kt in range(4):
        h_ready = []
        hguard = ln_prev2[0]["apply"] if ln_prev2[0] else None
        for m in range(8):
            bidx = 4 + (m % 2)
            waits = [("ve", xattn_done), ("wl", wo_ready)]
            if bank_state[bidx]:
                waits.append(bank_state[bidx])
            pairs = [(wo[:, c * 1024 + m * 128: c * 1024 + (m + 1) * 128],
                      xattn[:, c * 2048 + kt * NTl: c * 2048 + (kt + 1) * NTl])
                     for c in range(8)]
            pv = mm_group(ctx, em, ctx.bank[bidx][:, 0:NTl], pairs, waits=waits)

            def hev(s, m=m, bidx=bidx, pv=pv, hguard=hguard):
                s.wait_ge(S["pe"].h, pv)
                if m == 0 and hguard is not None:
                    s.wait_ge(S["ve"].h, hguard)
                s.activation(h_sb[:, m * NTl:(m + 1) * NTl], ctx.bank[bidx][:, 0:NTl],
                             AF.Identity, bias=bo[:, m:m + 1]).then_inc(S["ac"].h, 1)
            em.add("scalar", hev)
            a1 = S["ac"].inc()
            bank_state[bidx] = ("ac", a1)

            def hadd(v, m=m, kt=kt, a1=a1):
                v.wait_ge(S["ac"].h, a1)
                hv = h_sb[:, m * NTl:(m + 1) * NTl]
                v.tensor_tensor(hv, hv, xpe[:, m * 2048 + kt * NTl: m * 2048 + (kt + 1) * NTl],
                                ALU.add).then_inc(S["ve"].h, 1)
            em.add("vector", hadd)
            h_ready.append(S["ve"].inc())
        lnp = ln_prev2[0]
        res = emit_ln_tile(
            ctx, em, kt, NTl,
            src_fn=lambda c: h_sb[:, c * NTl:(c + 1) * NTl],
            mode="gb_resid",
            dst_fn=lambda c, kt=kt: ctx.resid[:, c * T + kt * NTl: c * T + (kt + 1) * NTl],
            gb=(lambda c: gam[:, c:c + 1], lambda c: bet[:, c:c + 1]),
            row_off=(kt % 2) * 512,
            bank_sxx_free=(("ve", lnp["var"]) if lnp else None),
            bank_ab_free=(("ve", lnp["apply"]) if lnp else None),
            src_ready=[("ve", h_ready[-1])])
        ln_prev2[0] = res
    em.flush()
    ctx.resid_ready = [("gp", S["gp"].v)]
    ctx.resid_chunk_ready = None
    ctx.resid_tile_ready = None



def emit_l1attn(ctx, block, scope, uid):
    nc, S = ctx.nc, ctx.sems
    NTl, ntl = 512, 4
    sb = lambda n, sh, dt: scope.enter_context(nc.sbuf_tensor(f"{n}_{uid}", sh, dt))
    xlnT = [sb("xln0", [128, 8 * NTl], BF16), sb("xln1", [128, 8 * NTl], BF16)]
    uc = sb("uc", [128, 16 * NTl], BF16)
    sg = sb("sg", [128, 16 * NTl], BF16)
    yg = sb("yg", [128, 16 * NTl], BF16)
    winb = sb("winb", [128, 2 * 1024], BF16)
    wbcb = sb("wbcb", [128, 4 * 2048], BF16)
    woutb = sb("woutb", [128, 2 * 2048], BF16)
    cacc = sb("cacc", [128, NTl], F32)
    ssb = sb("ssb", [128, NTl], F32)
    ysb = sb("ysb", [128, NTl], F32)
    bsb = sb("bsb", [128, NTl], F32)
    otm = sb("otm", [128, NTl], F32)

    em = E(ctx, block)
    bing = ctx.smalls[:, SM["b_in_g"][0]:SM["b_in_g"][1]]
    bbcb = ctx.smalls[:, SM["b_bc_b"][0]:SM["b_bc_b"][1]]
    bbcc = ctx.smalls[:, SM["b_bc_c"][0]:SM["b_bc_c"][1]]
    bout = ctx.smalls[:, SM["b_out"][0]:SM["b_out"][1]]
    sbias = ctx.smalls2[:, 32:48]

    def vinit(v):
        v.memset(ctx.carry[:], 0.0)
        v.memset(ctx.tails[:], 0.0).then_inc(S["ve"].h, 1)
    em.add("vector", vinit)
    init_done = S["ve"].inc()

    wl_ready = {}

    pc_tgt = S["pc"].v

    def wdma(key, src_ap, dst_ap, guard):
        def f(g, src_ap=src_ap, dst_ap=dst_ap, guard=guard):
            if pc_tgt:
                g.wait_ge(S["pc"].h, pc_tgt)
            if guard is not None:
                g.wait_ge(S["pe"].h, guard)
            g.dma_start(out=dst_ap, in_=src_ap).then_inc(S["wl"].h, 16)
        em.add("gpsimd", f)
        wl_ready[key] = S["wl"].inc(16)

    def win_slot(i):
        return winb[:, (i % 2) * 1024:(i % 2 + 1) * 1024]

    def wbc_slot(i):
        return wbcb[:, (i % 4) * 2048:(i % 4 + 1) * 2048]

    def wout_slot(i):
        return woutb[:, (i % 2) * 2048:(i % 2 + 1) * 2048]

    bank_state = {4: None, 5: None}
    ln_prev = [None]
    ln_apply = {}
    stage_end = {}
    bc_end = {}
    yg_done_t = {}
    wout_end = {}

    def emit_stage(tc):
        par = tc % 2
        lnv = ln_apply[tc]
        silu_prev = [None]
        for j in range(16):
            jb = j % 2
            waits = [("ve", lnv), ("wl", wl_ready[("win", j, tc)])]
            if bank_state[4 + jb]:
                waits.append(bank_state[4 + jb])
            pairs = [(win_slot(j)[:, c * 128:(c + 1) * 128],
                      xlnT[par][:, c * NTl:(c + 1) * NTl]) for c in range(8)]
            pv = mm_group(ctx, em, ctx.bank[4 + jb][:, 0:NTl], pairs, waits=waits)
            if j + 2 < 16:
                wdma(("win", j + 2, tc), ctx.dram["win16"][j + 2, :, :], win_slot(j), pv)
            else:
                wdma(("wgate", j + 2 - 16, tc), ctx.dram["win16"][16 + j + 2 - 16, :, :],
                     win_slot(j), pv)

            wj = [ctx.wconv[:, 4 * j + i:4 * j + i + 1] for i in range(4)]
            tlr = ctx.tails[:, (tc % 2) * 48 + 3 * j:(tc % 2) * 48 + 3 * j + 3]
            tlw = ctx.tails[:, (1 - tc % 2) * 48 + 3 * j:(1 - tc % 2) * 48 + 3 * j + 3]
            bank = ctx.bank[4 + jb]

            def conv(v, j=j, tc=tc, pv=pv, wj=wj, tlr=tlr, tlw=tlw, bank=bank, sp=silu_prev[0]):
                v.wait_ge(S["pe"].h, pv)
                if tc == 0 and j == 0:
                    v.wait_ge(S["ve"].h, init_done)
                if sp is not None:
                    v.wait_ge(S["ac"].h, sp)
                v.tensor_scalar_mul(cacc[:], bank[:, 0:NTl], wj[3])
                v.scalar_tensor_tensor(cacc[:, 1:NTl], bank[:, 0:NTl - 1], wj[2],
                                       cacc[:, 1:NTl], op0=ALU.mult, op1=ALU.add)
                v.scalar_tensor_tensor(cacc[:, 2:NTl], bank[:, 0:NTl - 2], wj[1],
                                       cacc[:, 2:NTl], op0=ALU.mult, op1=ALU.add)
                v.scalar_tensor_tensor(cacc[:, 3:NTl], bank[:, 0:NTl - 3], wj[0],
                                       cacc[:, 3:NTl], op0=ALU.mult, op1=ALU.add)
                v.scalar_tensor_tensor(cacc[:, 0:3], tlr, wj[0], cacc[:, 0:3],
                                       op0=ALU.mult, op1=ALU.add)
                v.drain()
                v.scalar_tensor_tensor(cacc[:, 0:2], tlr[:, 1:3], wj[1], cacc[:, 0:2],
                                       op0=ALU.mult, op1=ALU.add)
                v.drain()
                v.scalar_tensor_tensor(cacc[:, 0:1], tlr[:, 2:3], wj[2], cacc[:, 0:1],
                                       op0=ALU.mult, op1=ALU.add)
                v.tensor_copy(tlw, bank[:, NTl - 3:NTl])
                v.drain().then_inc(S["ve"].h, 1)
            em.add("vector", conv)
            cv = S["ve"].inc()
            bank_state[4 + jb] = ("ve", cv)

            def silu(s, j=j, tc=tc, cv=cv, bce=bc_end.get(tc - 1)):
                s.wait_ge(S["ve"].h, cv)
                if j == 0 and bce is not None:
                    s.wait_ge(S["pe"].h, bce)
                s.activation(uc[:, j * NTl:(j + 1) * NTl], cacc[:], AF.Silu,
                             bias=sbias[:, j:j + 1]).then_inc(S["ac"].h, 1)
            em.add("scalar", silu)
            silu_prev[0] = S["ac"].inc()
        uc_done_ac = S["ac"].v

        for j in range(16):
            jb = j % 2
            waits = [("ve", lnv), ("wl", wl_ready[("wgate", j, tc)])]
            if bank_state[4 + jb]:
                waits.append(bank_state[4 + jb])
            pairs = [(win_slot(j)[:, c * 128:(c + 1) * 128],
                      xlnT[par][:, c * NTl:(c + 1) * NTl]) for c in range(8)]
            pv = mm_group(ctx, em, ctx.bank[4 + jb][:, 0:NTl], pairs, waits=waits)
            if j + 2 < 16:
                wdma(("wgate", j + 2, tc), ctx.dram["win16"][16 + j + 2, :, :], win_slot(j), pv)
            elif j == 14:
                wdma(("wbc_b", 0, tc), ctx.dram["wbc16"][0, :, :], wbc_slot(0), pv)
                wdma(("wbc_c", 0, tc), ctx.dram["wbc16"][16, :, :], wbc_slot(1), pv)
            elif j == 15:
                wdma(("wbc_b", 1, tc), ctx.dram["wbc16"][1, :, :], wbc_slot(2), pv)
                wdma(("wbc_c", 1, tc), ctx.dram["wbc16"][17, :, :], wbc_slot(3), pv)

            def gev(s, j=j, tc=tc, jb=jb, pv=pv, ygd=yg_done_t.get(tc - 1)):
                s.wait_ge(S["pe"].h, pv)
                if j == 0 and ygd is not None:
                    s.wait_ge(S["ve"].h, ygd)
                s.activation(sg[:, j * NTl:(j + 1) * NTl], ctx.bank[4 + jb][:, 0:NTl],
                             AF.Sigmoid, bias=bing[:, j:j + 1]).then_inc(S["ac"].h, 1)
            em.add("scalar", gev)
            bank_state[4 + jb] = ("ac", S["ac"].inc())

        scan_prev = [None]
        for j in range(16):
            waits = [("ac", uc_done_ac), ("wl", wl_ready[("wbc_b", j, tc)])]
            if bank_state[4]:
                waits.append(bank_state[4])
            pairs = [(wbc_slot(2 * (j % 2))[:, ci * 128:(ci + 1) * 128],
                      uc[:, ci * NTl:(ci + 1) * NTl]) for ci in range(16)]
            pvb = mm_group(ctx, em, ctx.bank[4][:, 0:NTl], pairs, waits=waits)
            if j + 2 < 16:
                wdma(("wbc_b", j + 2, tc), ctx.dram["wbc16"][j + 2, :, :],
                     wbc_slot(2 * (j % 2)), pvb)

            def bev(s, j=j, pvb=pvb, sp=scan_prev[0]):
                s.wait_ge(S["pe"].h, pvb)
                if sp is not None:
                    s.wait_ge(S["ve"].h, sp)
                s.activation(bsb[:], ctx.bank[4][:, 0:NTl], AF.Identity,
                             bias=bbcb[:, j:j + 1]).then_inc(S["ac"].h, 1)
            em.add("scalar", bev)
            b_ready = S["ac"].inc()
            bank_state[4] = ("ac", b_ready)

            waits = [("ac", uc_done_ac), ("wl", wl_ready[("wbc_c", j, tc)])]
            if bank_state[5]:
                waits.append(bank_state[5])
            pairs = [(wbc_slot(2 * (j % 2) + 1)[:, ci * 128:(ci + 1) * 128],
                      uc[:, ci * NTl:(ci + 1) * NTl]) for ci in range(16)]
            pvc = mm_group(ctx, em, ctx.bank[5][:, 0:NTl], pairs, waits=waits)
            if j + 2 < 16:
                wdma(("wbc_c", j + 2, tc), ctx.dram["wbc16"][16 + j + 2, :, :],
                     wbc_slot(2 * (j % 2) + 1), pvc)

            def scan(v, j=j, tc=tc, b_ready=b_ready, pvc=pvc, we=wout_end.get(tc - 1)):
                v.wait_ge(S["ac"].h, b_ready)
                v.wait_ge(S["pe"].h, pvc)
                if j == 0 and we is not None:
                    v.wait_ge(S["pe"].h, we)
                a_b = ctx.adecay[:, j:j + 1].broadcast_to([128, NTl])
                v.tensor_tensor_scan(ssb[:], a_b, bsb[:], ctx.carry[:, j:j + 1],
                                     op0=ALU.mult, op1=ALU.add)
                v.tensor_tensor(ysb[:], ssb[:], ctx.bank[5][:, 0:NTl], ALU.mult)
                v.scalar_tensor_tensor(ysb[:], ssb[:], bbcc[:, j:j + 1], ysb[:],
                                       op0=ALU.mult, op1=ALU.add)
                v.tensor_tensor(yg[:, j * NTl:(j + 1) * NTl], ysb[:],
                                sg[:, j * NTl:(j + 1) * NTl], ALU.mult)
                v.tensor_copy(ctx.carry[:, j:j + 1], ssb[:, NTl - 1:NTl])
                v.drain().then_inc(S["ve"].h, 1)
            em.add("vector", scan)
            scan_prev[0] = S["ve"].inc()
            bank_state[5] = ("ve", scan_prev[0])
        yg_done = S["ve"].v
        yg_done_t[tc] = yg_done
        bc_end[tc] = S["pe"].v

        rup_prev = [None]
        for m in range(8):
            mb = m % 2
            waits = [("ve", yg_done), ("wl", wl_ready[("wout", m, tc)])]
            if bank_state[4 + mb]:
                waits.append(bank_state[4 + mb])
            pairs = [(wout_slot(m)[:, ci * 128:(ci + 1) * 128],
                      yg[:, ci * NTl:(ci + 1) * NTl]) for ci in range(16)]
            pv = mm_group(ctx, em, ctx.bank[4 + mb][:, 0:NTl], pairs, waits=waits)
            if m + 2 < 8:
                wdma(("wout", m + 2, tc), ctx.dram["wout16"][m + 2, :, :], wout_slot(m), pv)
            elif tc + 1 < ntl:
                if m == 6:
                    wdma(("win", 0, tc + 1), ctx.dram["win16"][0, :, :], win_slot(0), pv)
                    wdma(("win", 1, tc + 1), ctx.dram["win16"][1, :, :], win_slot(1), pv)
                else:
                    wdma(("wout", 0, tc + 1), ctx.dram["wout16"][0, :, :], wout_slot(0), pv)
                    wdma(("wout", 1, tc + 1), ctx.dram["wout16"][1, :, :], wout_slot(1), pv)

            def oev(s, m=m, mb=mb, pv=pv, rp=rup_prev[0]):
                s.wait_ge(S["pe"].h, pv)
                if rp is not None:
                    s.wait_ge(S["ve"].h, rp)
                s.activation(otm[:], ctx.bank[4 + mb][:, 0:NTl], AF.Identity,
                             bias=bout[:, m:m + 1]).then_inc(S["ac"].h, 1)
            em.add("scalar", oev)
            a1 = S["ac"].inc()
            bank_state[4 + mb] = ("ac", a1)

            def rup(v, m=m, tc=tc, a1=a1):
                v.wait_ge(S["ac"].h, a1)
                rv = ctx.resid[:, m * T + tc * NTl: m * T + (tc + 1) * NTl]
                v.tensor_tensor(rv, rv, otm[:], ALU.add).then_inc(S["ve"].h, 1)
            em.add("vector", rup)
            rup_prev[0] = S["ve"].inc()
        stage_end[tc] = S["pe"].v
        wout_end[tc] = S["pe"].v

    wdma(("win", 0, 0), ctx.dram["win16"][0, :, :], win_slot(0), None)
    wdma(("win", 1, 0), ctx.dram["win16"][1, :, :], win_slot(1), None)
    wdma(("wout", 0, 0), ctx.dram["wout16"][0, :, :], wout_slot(0), None)
    wdma(("wout", 1, 0), ctx.dram["wout16"][1, :, :], wout_slot(1), None)

    def src_of(k):
        return lambda c, k=k: ctx.resid[:, c * T + k * NTl: c * T + (k + 1) * NTl]

    cast_vals = {0: emit_ln_cast(ctx, em, NTl, src_of(0), src_ready=ctx.resid_ready)}
    for k in range(ntl):
        lnp = ln_prev[0]
        aw = [("pe", stage_end[k - 2])] if (k - 2) in stage_end else None
        res = emit_ln_tile(
            ctx, em, k, NTl,
            src_fn=src_of(k),
            mode="bf16",
            dst_fn=lambda c, k=k: xlnT[k % 2][:, c * NTl:(c + 1) * NTl],
            row_off=(k % 2) * 512,
            bank_sxx_free=(("ve", lnp["var"]) if lnp else None),
            bank_ab_free=(("ve", lnp["apply"]) if lnp else None),
            src_ready=ctx.resid_ready, precast=cast_vals[k], apply_waits=aw)
        ln_prev[0] = res
        ln_apply[k] = res["apply"]
        if k + 1 < ntl:
            cast_vals[k + 1] = emit_ln_cast(ctx, em, NTl, src_of(k + 1),
                                            src_ready=ctx.resid_ready,
                                            cast_guard=[("pe", res["stats"])])
        if k >= 1:
            emit_stage(k - 1)
    emit_stage(ntl - 1)
    em.flush()
    ctx.resid_ready = [("ve", S["ve"].v)]
    ctx.resid_chunk_ready = None
    ctx.resid_tile_ready = None


EMITTERS.update({"l0ffn": emit_l0ffn, "l1ffn": emit_l1ffn, "l0attn": emit_l0attn, "l1attn": emit_l1attn})


# version 30
# speedup vs baseline: 1.0931x; 1.0175x over previous
import numpy as np
import ml_dtypes
from contextlib import ExitStack

import concourse.bass as bass
import concourse.mybir as mybir
from concourse.bass_utils import run_bass_kernel_spmd

F32 = mybir.dt.float32
BF16 = mybir.dt.bfloat16
AF = mybir.ActivationFunctionType
ALU = mybir.AluOpType
AX = mybir.AxisListType

D = 1024
T = 2048
C8 = 8
C16 = 16
NT = 512
NTILES = T // NT
EPS = 1e-5
PAD = 4
XW = T + PAD

BF = ml_dtypes.bfloat16


def _prep_lhsT(w):
    IN, OUT = w.shape
    return np.ascontiguousarray(w.reshape(IN // 128, 128, OUT).transpose(1, 0, 2).reshape(128, -1))


def _prep_out_blocked(w, blk=128):
    IN, OUT = w.shape
    return np.ascontiguousarray(
        w.reshape(IN // 128, 128, OUT // blk, blk).transpose(2, 1, 0, 3).reshape(OUT // blk, 128, -1))


def _cols(v):
    return np.ascontiguousarray(v.reshape(-1, 128).T)


def host_prep(I):
    p = {}
    p["wr"] = _prep_lhsT(np.ascontiguousarray(I["L0_ffn_W_r"].T))
    p["wk"] = _prep_lhsT(np.ascontiguousarray(I["L0_ffn_W_k"].T))
    p["wv"] = _prep_lhsT(np.ascontiguousarray(I["L0_ffn_W_v"].T))
    p["wq"] = _prep_out_blocked(I["L0_attn_wq"])
    p["wkk"] = _prep_out_blocked(I["L0_attn_wk"])
    p["wvv"] = _prep_out_blocked(I["L0_attn_wv"])
    p["wo"] = _prep_lhsT(I["L0_attn_wo"])
    p["peT"] = np.ascontiguousarray(I["L0_attn_pe"][0, :T, :].T)
    p["win"] = _prep_out_blocked(I["L1_attn_W_in"])
    p["wbc"] = _prep_out_blocked(I["L1_attn_W_bc"])
    p["wout"] = _prep_out_blocked(I["L1_attn_W_out"])
    p["wconv"] = np.ascontiguousarray(
        I["L1_attn_conv_w"][:, 0, :].reshape(16, 128, 4).transpose(1, 0, 2).reshape(128, 64))
    p["bw"] = _prep_lhsT(np.ascontiguousarray(I["L1_ffn_base_w"].T))
    p["rm"] = _prep_lhsT(np.ascontiguousarray(
        np.concatenate([I["L1_ffn_router_w"].T, I["L1_ffn_lora_A"]], axis=1)))
    p["bflat"] = np.ascontiguousarray(I["L1_ffn_lora_B"].reshape(64, 1024))
    cols = [
        _cols(I["L0_ffn_mu_r"]),
        _cols(I["L0_ffn_mu_k"]),
        _cols(I["L0_attn_bq"]),
        _cols(I["L0_attn_bk"]),
        _cols(I["L0_attn_bo"]),
        _cols(I["L0_attn_gamma"]),
        _cols(I["L0_attn_beta"]),
        np.ascontiguousarray(I["L1_attn_b_in"][:2048].reshape(16, 128).T),
        np.ascontiguousarray(I["L1_attn_b_in"][2048:].reshape(16, 128).T),
        np.ascontiguousarray(I["L1_attn_conv_b"].reshape(16, 128).T),
        np.ascontiguousarray(I["L1_attn_log_decay"].reshape(16, 128).T),
        np.ascontiguousarray(I["L1_attn_b_bc"][:2048].reshape(16, 128).T),
        np.ascontiguousarray(I["L1_attn_b_bc"][2048:].reshape(16, 128).T),
        _cols(I["L1_attn_b_out"]),
        _cols(I["L1_ffn_base_b"]),
    ]
    p["smalls"] = np.ascontiguousarray(np.concatenate(cols, axis=1))
    rows = [
        I["L0_attn_bv"],
        np.concatenate([I["L1_ffn_router_b"], np.zeros(8, np.float32)]),
    ]
    p["rows"] = np.ascontiguousarray(np.concatenate(rows)[None, :])
    p["ident"] = np.eye(128, dtype=np.float32)
    return p


SM = dict(mu_r=(0, 8), mu_k=(8, 16), bq=(16, 24), bk=(24, 32), bo=(32, 40),
          gamma=(40, 48), beta=(48, 56), b_in_u=(56, 72), b_in_g=(72, 88),
          conv_b=(88, 104), log_decay=(104, 120), b_bc_b=(120, 136),
          b_bc_c=(136, 152), b_out=(152, 160), base_b=(160, 168))
NSMALL = 168


class Sem:
    def __init__(self, h):
        self.h = h
        self.v = 0

    def inc(self, n=1):
        self.v += n
        return self.v


class Ctx:

    def __init__(self, nc, es):
        self.nc = nc
        self.es = es
        self.sems = {}
        for name in ("wl", "xl", "pe", "ac", "ve", "gp", "st", "pc"):
            self.sems[name] = Sem(es.enter_context(nc.semaphore(name)))
        self.psA = es.enter_context(nc.psum_tensor("psA", [128, 2048], F32))
        self.psB = es.enter_context(nc.psum_tensor("psB", [128, 1536], F32))
        self.bank = [self.psA[:, 512 * i:512 * (i + 1)] for i in range(4)] + \
                    [self.psB[:, 512 * i:512 * (i + 1)] for i in range(3)]
        self.bankb = es.enter_context(nc.psum_tensor("bankb", [128, 1024], BF16))

    def sb(self, name, shape, dtype):
        return self.es.enter_context(self.nc.sbuf_tensor(name, shape, dtype))


def ceil_div(a, b):
    return -(-a // b)


class E:

    ENGINES = ("tensor", "vector", "scalar", "sync", "gpsimd")

    def __init__(self, ctx, block):
        self.ctx = ctx
        self.block = block
        self.q = {e: [] for e in self.ENGINES}

    def add(self, eng, f):
        self.q[eng].append(f)

    def flush(self):
        for eng in self.ENGINES:
            steps = self.q[eng]
            if not steps:
                continue
            reg = getattr(self.block, eng)

            def body(engine, steps=steps):
                for s in steps:
                    s(engine)
            reg(body)
        self.q = {e: [] for e in self.ENGINES}


def _wait(S, waits):
    def f(engine, waits=tuple(waits)):
        for sname, val in waits:
            engine.wait_ge(S[sname].h, val)
    return f


def mm_group(ctx, em, bank_ap, pairs, waits=(), sem="pe", n_inc=1):
    S = ctx.sems

    def f(t, bank_ap=bank_ap, pairs=tuple(pairs), waits=tuple(waits)):
        for sname, val in waits:
            t.wait_ge(S[sname].h, val)
        n = len(pairs)
        inst = None
        for i, (l, r) in enumerate(pairs):
            inst = t.matmul(bank_ap, l, r, start=(i == 0), stop=(i == n - 1))
        inst.then_inc(S[sem].h, n_inc)
    em.add("tensor", f)
    return S[sem].inc(n_inc)


def emit_ln_cast(ctx, em, NTl, src_fn, src_ready=None, cast_guard=None):
    S = ctx.sems
    xc = ctx.xcastT

    def cast(v):
        for sname, val in (src_ready or ()):
            v.wait_ge(S[sname].h, val)
        for sname, val in (cast_guard or ()):
            v.wait_ge(S[sname].h, val)
        inst = None
        for c in range(8):
            inst = v.tensor_copy(xc[:, c * 512:c * 512 + NTl], src_fn(c))
        inst.then_inc(S["ve"].h, 1)
    em.add("vector", cast)
    return S["ve"].inc()


def emit_ln_tile(ctx, em, k, NTl, src_fn, mode, dst_fn=None, extra_fn=None,
                 gb=None, row_off=0, bank_sxx_free=None, bank_ab_free=None,
                 src_ready=None, rowsA=None, rowsB=None, apply_waits=None,
                 extra_ready=None, cast_eng="vector", cast_guard=None,
                 precast=None, defer_tail=False):
    S = ctx.sems
    nc = ctx.nc
    B0, B1, B2, B3 = ctx.bank[0], ctx.bank[1], ctx.bank[2], ctx.bank[3]
    xc, xq = ctx.xcastT, ctx.xsqT
    stat = ctx.statS

    csem = "ve" if cast_eng == "vector" else "gp"
    if precast is not None:
        csem, v_cast = "ve", precast
    else:
        def cast(v, k=k):
            for sname, val in (src_ready or ()):
                v.wait_ge(S[sname].h, val)
            for sname, val in (cast_guard or ()):
                v.wait_ge(S[sname].h, val)
            inst = None
            for c in range(8):
                inst = v.tensor_copy(xc[:, c * 512:c * 512 + NTl], src_fn(c))
            inst.then_inc(S[csem].h, 1)
        em.add(cast_eng, cast)
        v_cast = S[csem].inc()

    def sq(s, k=k):
        for sname, val in (src_ready or ()):
            s.wait_ge(S[sname].h, val)
        inst = None
        for c in range(8):
            inst = s.activation(xq[:, c * 512:c * 512 + NTl], src_fn(c), AF.Square)
        inst.then_inc(S["ac"].h, 1)
    em.add("scalar", sq)
    a_sq = S["ac"].inc()

    waits = [(csem, v_cast), ("ac", a_sq)]
    if bank_sxx_free is not None:
        waits.append(bank_sxx_free)
    pairs0 = [(ctx.ones_pcol_bf[:], xc[:, c * 512:c * 512 + NTl]) for c in range(8)]
    pairs1 = [(ctx.ones_pcol_bf[:], xq[:, c * 512:c * 512 + NTl]) for c in range(8)]
    mm_group(ctx, em, B0[0:1, 0:NTl], pairs0, waits=waits)
    p_stats = mm_group(ctx, em, B1[0:1, 0:NTl], pairs1)

    s0 = stat[0:1, 0:NTl]
    s1 = stat[0:1, 512:512 + NTl]
    s2 = stat[0:1, 1024:1024 + NTl]
    Arow = rowsA if rowsA is not None else ctx.rowA[0:1, row_off:row_off + NTl]

    def statmath_v(v):
        v.wait_ge(S["pe"].h, p_stats)
        v.tensor_scalar_mul(s0, B0[0:1, 0:NTl], -1.0 / 1024.0)
        v.tensor_scalar_mul(s1, B1[0:1, 0:NTl], 1.0 / 1024.0)
        v.tensor_tensor(s2, s0, s0, ALU.mult)
        v.tensor_sub(s2, s1, s2).then_inc(S["ve"].h, 1)
    em.add("vector", statmath_v)
    v_var = S["ve"].inc()

    def statmath_s(s):
        s.wait_ge(S["ve"].h, v_var)
        s.activation(s1, s2, AF.Ln, bias=ctx.epscol[0:1, 0:1])
        s.activation(Arow, s1, AF.Exp, scale=-0.5).then_inc(S["ac"].h, 1)
    em.add("scalar", statmath_s)
    a_A = S["ac"].inc()

    res = {"stats": p_stats, "cast": v_cast, "sq": a_sq, "var": v_var}

    def finish():
        waits = [("ve", v_var)]
        if bank_ab_free is not None:
            waits.append(bank_ab_free)
        mm_group(ctx, em, B3[:, 0:NTl], [(ctx.ones_row_f[:], s0)], waits=waits)
        p_bcast = mm_group(ctx, em, B2[:, 0:NTl], [(ctx.ones_row_f[:], Arow)],
                           waits=[("ac", a_A)])

        tmp = ctx.lnTmp

        def apply(v):
            v.wait_ge(S["pe"].h, p_bcast)
            for sname, val in (apply_waits or ()):
                v.wait_ge(S[sname].h, val)
            for sname, val in (extra_ready or ()):
                v.wait_ge(S[sname].h, val)
            inst = None
            for c in range(8):
                v.tensor_tensor(tmp[:, 0:NTl], src_fn(c), B3[:, 0:NTl], ALU.add)
                if mode == "bf16":
                    inst = v.tensor_tensor(dst_fn(c), tmp[:, 0:NTl], B2[:, 0:NTl], ALU.mult)
                elif mode == "pe":
                    v.tensor_tensor(tmp[:, 0:NTl], tmp[:, 0:NTl], B2[:, 0:NTl], ALU.mult)
                    inst = v.tensor_tensor(dst_fn(c), tmp[:, 0:NTl], extra_fn(c), ALU.add)
                elif mode == "gb_resid":
                    v.tensor_tensor(tmp[:, 0:NTl], tmp[:, 0:NTl], B2[:, 0:NTl], ALU.mult)
                    gcol, bcol = gb
                    v.tensor_scalar(tmp[:, 0:NTl], tmp[:, 0:NTl], gcol(c), bcol(c),
                                    op0=ALU.mult, op1=ALU.add)
                    inst = v.tensor_tensor(dst_fn(c), dst_fn(c), tmp[:, 0:NTl], ALU.add)
            inst.then_inc(S["ve"].h, 1)
        em.add("vector", apply)
        res["bcast"] = p_bcast
        res["apply"] = S["ve"].inc()
        return res

    if defer_tail:
        return res, finish
    finish()
    return res


LN_CAST_ENG = {}


def emit_l0ffn(ctx, block, scope, uid):
    nc, S = ctx.nc, ctx.sems
    NTl, ntl = 512, 4
    XS = NTl + 4
    sb = lambda n, sh, dt: scope.enter_context(nc.sbuf_tensor(f"{n}_{uid}", sh, dt))
    xlnT = [sb("xln0", [128, 8 * XS], BF16)]
    xlnT.append(xlnT[0])
    sv = sb("sv", [128, 8], BF16)
    wr = sb("wr", [128, 8 * 1024], BF16)
    wk = sb("wk", [128, 8 * 1024], BF16)
    wv = sb("wv", [128, 8 * 1024], BF16)
    xr = [sb("xr0", [128, 8 * NTl], BF16), sb("xr1", [128, 8 * NTl], BF16)]
    xk = [sb("xk0", [128, 8 * NTl], BF16), sb("xk1", [128, 8 * NTl], BF16)]
    rbuf = sb("r0", [128, 8 * NTl], BF16)
    kact = sb("ka0", [128, 8 * NTl], BF16)
    mtmp = sb("mtmp", [128, NTl], F32)
    gtmp = sb("gtmp", [128, NTl], BF16)
    otmp = mtmp

    em = E(ctx, block)

    def wload(g):
        g.dma_start(out=wr[:], in_=ctx.dram["wr"][:, :]).then_inc(S["wl"].h, 16)
        g.dma_start(out=wk[:], in_=ctx.dram["wk"][:, :]).then_inc(S["wl"].h, 16)
        g.dma_start(out=wv[:], in_=ctx.dram["wv"][:, :]).then_inc(S["wl"].h, 16)
    em.add("gpsimd", wload)
    wl_done = S["wl"].inc(48)

    def emit_precast():
        rr = list(ctx.resid_ready)

        def precast(g, rr=rr):
            for sname, val in rr:
                g.wait_ge(S[sname].h, val)
            g.wait_ge(S["wl"].h, wl_done)
            for nm16, nm, n in (("win16", "win", 32), ("wbc16", "wbc", 32),
                                ("wout16", "wout", 8)):
                for i in range(n):
                    g.dma_start(out=ctx.dram[nm16][i, :, :],
                                in_=ctx.dram[nm][i, :, :]).then_inc(S["pc"].h, 16)
        em.add("gpsimd", precast)
        S["pc"].inc(72 * 16)

    mu_r = ctx.smalls[:, SM["mu_r"][0]:SM["mu_r"][1]]
    mu_k = ctx.smalls[:, SM["mu_k"][0]:SM["mu_k"][1]]
    omu_r = ctx.smalls2[:, 0:8]
    omu_k = ctx.smalls2[:, 8:16]

    bank_state = {4: None, 5: None}
    ln_prev = [None]
    bcopy_done = {}
    mix_done = {}
    stage_end = {}
    evac_end = {}

    def emit_mix(k):
        par = k % 2

        apv = ln_prev[0]["apply"]

        def bcopy(g, k=k, apv=apv):
            g.wait_ge(S["ve"].h, apv)
            if k == 0:
                g.memset(xlnT[0][:, 3:8 * XS:XS], 0.0)
            else:
                g.tensor_copy(xlnT[0][:, 3:8 * XS:XS], sv[:])
            g.drain().then_inc(S["gp"].h, 1)
        em.add("gpsimd", bcopy)
        bcopy_done[k] = S["gp"].inc()

        def mix(v, k=k, par=par, apv=apv):
            v.wait_ge(S["gp"].h, bcopy_done[k])
            if k >= 2 and stage_end.get(k - 2):
                v.wait_ge(S["pe"].h, stage_end[k - 2]["k"])
            inst = None
            for c in range(8):
                cur = xlnT[par][:, c * XS + 4: c * XS + 4 + NTl]
                prv = xlnT[par][:, c * XS + 3: c * XS + 3 + NTl]
                v.tensor_scalar_mul(gtmp[:], cur, mu_r[:, c:c + 1])
                v.scalar_tensor_tensor(xr[par][:, c * NTl:(c + 1) * NTl], prv,
                                       omu_r[:, c:c + 1], gtmp[:], op0=ALU.mult, op1=ALU.add)
                v.tensor_scalar_mul(gtmp[:], cur, mu_k[:, c:c + 1])
                inst = v.scalar_tensor_tensor(xk[par][:, c * NTl:(c + 1) * NTl], prv,
                                              omu_k[:, c:c + 1], gtmp[:], op0=ALU.mult, op1=ALU.add)
            v.tensor_copy(sv[:], xlnT[0][:, XS - 1:8 * XS:XS])
            inst = v.drain()
            inst.then_inc(S["ve"].h, 1)
        em.add("vector", mix)
        mix_done[k] = S["ve"].inc()

    def emit_stages(k, after_r=None):
        par = k % 2
        ends = {}
        for m in range(8):
            bidx = 4 + (m % 2)
            waits = [("ve", mix_done[k]), ("wl", wl_done)]
            if bank_state[bidx]:
                waits.append(bank_state[bidx])
            pairs = [(wr[:, c * 1024 + m * 128: c * 1024 + (m + 1) * 128],
                      xr[par][:, c * NTl:(c + 1) * NTl]) for c in range(8)]
            pv = mm_group(ctx, em, ctx.bank[bidx][:, 0:NTl], pairs, waits=waits)

            def ev(s, m=m, k=k, bidx=bidx, pv=pv):
                s.wait_ge(S["pe"].h, pv)
                if m == 0 and k >= 1 and (k - 1) in evac_end:
                    s.wait_ge(S["ve"].h, evac_end[k - 1])
                s.activation(rbuf[:, m * NTl:(m + 1) * NTl],
                             ctx.bank[bidx][:, 0:NTl], AF.Sigmoid).then_inc(S["ac"].h, 1)
            em.add("scalar", ev)
            bank_state[bidx] = ("ac", S["ac"].inc())
        ends["r"] = S["pe"].v
        if after_r is not None:
            after_r()
        for m in range(8):
            bidx = 4 + (m % 2)
            waits = [("ve", mix_done[k])]
            if bank_state[bidx]:
                waits.append(bank_state[bidx])
            pairs = [(wk[:, c * 1024 + m * 128: c * 1024 + (m + 1) * 128],
                      xk[par][:, c * NTl:(c + 1) * NTl]) for c in range(8)]
            pv = mm_group(ctx, em, ctx.bank[bidx][:, 0:NTl], pairs, waits=waits)

            def ev(s, m=m, bidx=bidx, pv=pv):
                s.wait_ge(S["pe"].h, pv)
                s.activation(ctx.bank[bidx][:, 0:NTl], ctx.bank[bidx][:, 0:NTl], AF.Relu)
                s.activation(kact[:, m * NTl:(m + 1) * NTl], ctx.bank[bidx][:, 0:NTl],
                             AF.Square).then_inc(S["ac"].h, 1)
            em.add("scalar", ev)
            bank_state[bidx] = ("ac", S["ac"].inc())
        ends["k"] = S["pe"].v
        ac_kdone = S["ac"].v
        for m in range(8):
            bidx = 4 + (m % 2)
            waits = [("ac", ac_kdone)]
            if bank_state[bidx]:
                waits.append(bank_state[bidx])
            pairs = [(wv[:, c * 1024 + m * 128: c * 1024 + (m + 1) * 128],
                      kact[:, c * NTl:(c + 1) * NTl]) for c in range(8)]
            pv = mm_group(ctx, em, ctx.bank[bidx][:, 0:NTl], pairs, waits=waits)

            def ev(v, m=m, k=k, bidx=bidx, pv=pv, par=par):
                v.wait_ge(S["pe"].h, pv)
                v.tensor_tensor(otmp[:], ctx.bank[bidx][:, 0:NTl],
                                rbuf[:, m * NTl:(m + 1) * NTl], ALU.mult)
                rv = ctx.resid[:, m * T + k * NTl: m * T + k * NTl + NTl]
                v.scalar_tensor_tensor(rv, otmp[:], 0.5, rv,
                                       op0=ALU.mult, op1=ALU.add).then_inc(S["ve"].h, 1)
            em.add("vector", ev)
            bank_state[bidx] = ("ve", S["ve"].inc())
        ends["v"] = S["pe"].v
        evac_end[k] = S["ve"].v
        stage_end[k] = ends

    def src_of(k):
        return lambda c, k=k: ctx.resid[:, c * T + k * NTl: c * T + (k + 1) * NTl]

    def src_ready_of(k):
        tr = getattr(ctx, "resid_tile_ready", None)
        return tr[k] if tr else ctx.resid_ready

    cast_vals = {0: emit_ln_cast(ctx, em, NTl, src_of(0), src_ready=src_ready_of(0))}
    fins = {}
    for k in range(ntl):
        par = k % 2
        lnp = ln_prev[0]
        res, fin = emit_ln_tile(
            ctx, em, k, NTl,
            src_fn=src_of(k),
            mode="bf16",
            dst_fn=lambda c, k=k, par=par: xlnT[par][:, c * XS + 4: c * XS + 4 + NTl],
            row_off=(k % 2) * 512,
            bank_sxx_free=(("ve", lnp["var"]) if lnp else None),
            bank_ab_free=(("ve", lnp["apply"]) if lnp else None),
            src_ready=src_ready_of(k), precast=cast_vals[k],
            apply_waits=([("ve", mix_done[k - 1])] if (k - 1) in mix_done else None),
            defer_tail=True)
        ln_prev[0] = res
        fins[k] = fin
        if uid == 0 and k == ntl - 1:
            emit_precast()
        if k + 1 < ntl:
            cast_vals[k + 1] = emit_ln_cast(ctx, em, NTl, src_of(k + 1),
                                            src_ready=src_ready_of(k + 1),
                                            cast_guard=[("pe", res["stats"])])
        if k == 0:
            fins[0]()
            emit_mix(0)
        else:
            emit_stages(k - 1, after_r=(lambda k=k: (fins[k](), emit_mix(k))))
    emit_stages(ntl - 1)
    em.flush()
    ctx.resid_ready = [("ve", S["ve"].v)]
    ctx.resid_chunk_ready = None
    ctx.resid_tile_ready = None


DRAM_DECLS = [
    ("wr", [128, 8 * 1024]), ("wk", [128, 8 * 1024]), ("wv", [128, 8 * 1024]),
    ("wq", [8, 128, 8 * 128]), ("wkk", [8, 128, 8 * 128]), ("wvv", [8, 128, 8 * 128]),
    ("wo", [128, 8 * 1024]), ("peT", [1024, 2048]),
    ("win", [32, 128, 8 * 128]), ("wbc", [32, 128, 16 * 128]), ("wout", [8, 128, 16 * 128]),
    ("wconv", [128, 64]),
    ("bw", [128, 8 * 1024]), ("rm", [128, 8 * 16]), ("bflat", [64, 1024]),
    ("smalls", [128, NSMALL]), ("rows", [1, 1040]), ("ident", [128, 128]),
]


def emit_setup(ctx, block):
    nc, S = ctx.nc, ctx.sems
    em = E(ctx, block)

    def loads(sy):
        sy.dma_start(out=ctx.smalls[:], in_=ctx.dram["smalls"][:, :]).then_inc(S["xl"].h, 16)
        sy.dma_start(out=ctx.wconv[:], in_=ctx.dram["wconv"][:, :]).then_inc(S["xl"].h, 16)
    em.add("sync", loads)
    xl_v = S["xl"].inc(32)

    wl_tgt = S["wl"].v + 32

    def gloads(g, wl_tgt=wl_tgt):
        g.dma_start(out=ctx.rows_bf[:], in_=ctx.dram["rows"][:, :]).then_inc(S["wl"].h, 16)
        g.dma_start(out=ctx.idn[:], in_=ctx.dram["ident"][:, :]).then_inc(S["wl"].h, 16)
        g.memset(ctx.ones_pcol_bf[:], 1.0)
        g.memset(ctx.ones_row_f[:], 1.0)
        g.memset(ctx.ones_row_bf[:], 1.0)
        g.wait_ge(S["wl"].h, wl_tgt)
    em.add("gpsimd", gloads)
    S["wl"].inc(32)

    def vsetup(v, xl_v=xl_v):
        v.wait_ge(S["xl"].h, xl_v)
        mu_r = ctx.smalls[:, SM["mu_r"][0]:SM["mu_r"][1]]
        mu_k = ctx.smalls[:, SM["mu_k"][0]:SM["mu_k"][1]]
        v.tensor_scalar(ctx.smalls2[:, 0:8], mu_r, -1.0, 1.0, op0=ALU.mult, op1=ALU.add)
        v.tensor_scalar(ctx.smalls2[:, 8:16], mu_k, -1.0, 1.0, op0=ALU.mult, op1=ALU.add)
        v.memset(ctx.epscol[:], EPS)
        for j in range(16):
            v.reduce_sum(ctx.smalls2[:, 16 + j:17 + j], ctx.wconv[:, 4 * j:4 * j + 4], axis=AX.X)
        v.drain()
        binu_ = ctx.smalls[:, SM["b_in_u"][0]:SM["b_in_u"][1]]
        cvb_ = ctx.smalls[:, SM["conv_b"][0]:SM["conv_b"][1]]
        for j in range(16):
            v.scalar_tensor_tensor(ctx.smalls2[:, 32 + j:33 + j], binu_[:, j:j + 1],
                                   ctx.smalls2[:, 16 + j:17 + j], cvb_[:, j:j + 1],
                                   op0=ALU.mult, op1=ALU.add)
        v.memset(ctx.zcol[:], 0.0)
        v.memset(ctx.ln8col[:], -2.0794415416798357)
    em.add("vector", vsetup)

    def ssetup(s, xl_v=xl_v):
        s.wait_ge(S["xl"].h, xl_v)
        ld = ctx.smalls[:, SM["log_decay"][0]:SM["log_decay"][1]]
        s.activation(ctx.adecay[:], ld, AF.Sigmoid, scale=-1.0)
    em.add("scalar", ssetup)
    em.flush()


def emit_load(ctx, block, seq):
    S = ctx.sems
    em = E(ctx, block)
    st_base = S["st"].v - 128 if S["st"].v else 0
    first = S["st"].v == 0

    def loads(sy, st_base=st_base, first=first):
        for kt in range(4):
            for c in range(8):
                if not first and kt == 0:
                    sy.wait_ge(S["st"].h, st_base + 16 * (c + 1))
                sy.dma_start(
                    out=ctx.resid[:, c * T + kt * 512: c * T + (kt + 1) * 512],
                    in_=ctx.dram["xT"][seq, c * 128:(c + 1) * 128,
                                       kt * 512:(kt + 1) * 512]).then_inc(S["xl"].h, 16)
    em.add("sync", loads)
    xl0 = S["xl"].v
    ctx.resid_tile_ready = [[("xl", xl0 + 128 * (kt + 1))] for kt in range(4)]
    ctx.resid_ready = [("xl", S["xl"].inc(512))]
    ctx.resid_chunk_ready = None
    em.flush()


def emit_store(ctx, block, seq, last):
    S = ctx.sems
    em = E(ctx, block)
    waits = list(ctx.resid_ready)
    chunk_ready = getattr(ctx, "resid_chunk_ready", None)
    st_final = S["st"].v + 128

    def stores(sy, st_final=st_final, chunk_ready=chunk_ready):
        for c in range(8):
            if chunk_ready is not None:
                sname, val = chunk_ready[c]
                sy.wait_ge(S[sname].h, val)
            elif c == 0:
                for sname, val in waits:
                    sy.wait_ge(S[sname].h, val)
            sy.dma_start(out=ctx.dram["outT"][seq, c * 128:(c + 1) * 128, :],
                         in_=ctx.resid[:, c * T:(c + 1) * T]).then_inc(S["st"].h, 16)
        if last:
            sy.wait_ge(S["st"].h, st_final)
    em.add("sync", stores)
    S["st"].inc(128)
    em.flush()


def build_program(subs):
    nc = bass.Bass("TRN2", target_bir_lowering=False)
    dram = {"xT": nc.dram_tensor("xT", [2, 1024, 2048], F32, kind="ExternalInput").ap(),
            "outT": nc.dram_tensor("outT", [2, 1024, 2048], F32, kind="ExternalOutput").ap()}
    for name, shape in DRAM_DECLS:
        dram[name] = nc.dram_tensor(name, shape, F32, kind="ExternalInput").ap()
    for name, shape in (("win16", [32, 128, 8 * 128]), ("wbc16", [32, 128, 16 * 128]),
                        ("wout16", [8, 128, 16 * 128])):
        dram[name] = nc.dram_tensor(name, shape, BF16, kind="Internal").ap()
    with ExitStack() as es:
        ctx = Ctx(nc, es)
        ctx.dram = dram
        ctx.resid = ctx.sb("resid", [128, 8 * T], F32)
        ctx.xcastT = ctx.sb("xcastT", [128, 8 * 512], BF16)
        ctx.xsqT = ctx.sb("xsqT", [128, 8 * 512], BF16)
        ctx.statS = ctx.sb("statS", [1, 1536], F32)
        ctx.rowA = ctx.sb("rowA", [1, 1024], F32)
        ctx.rowB = ctx.sb("rowB", [1, 1024], F32)
        ctx.lnTmp = ctx.sb("lnTmp", [128, 512], F32)
        ctx.smalls = ctx.sb("sb_smalls", [128, NSMALL], F32)
        ctx.smalls2 = ctx.sb("smalls2", [128, 48], F32)
        ctx.rows_bf = ctx.sb("rows_bf", [1, 1040], BF16)
        ctx.ones_pcol_bf = ctx.sb("ones_pcol_bf", [128, 1], BF16)
        ctx.ones_row_f = ctx.sb("ones_row_f", [1, 128], F32)
        ctx.ones_row_bf = ctx.sb("ones_row_bf", [1, 128], BF16)
        ctx.idn = ctx.sb("idn", [128, 128], BF16)
        ctx.wconv = ctx.sb("sb_wconv", [128, 64], F32)
        ctx.adecay = ctx.sb("adecay", [128, 16], F32)
        ctx.epscol = ctx.sb("epscol", [128, 1], F32)
        ctx.zcol = ctx.sb("zcol", [128, 1], F32)
        ctx.ln8col = ctx.sb("ln8col", [128, 1], F32)
        ctx.carry = ctx.sb("carry", [128, 16], F32)
        ctx.tails = ctx.sb("tails", [128, 96], F32)

        with nc.Block("setup") as b:
            emit_setup(ctx, b)
        uid = 0
        for seq in range(2):
            with nc.Block(f"load{seq}") as b:
                emit_load(ctx, b, seq)
            for sname in subs:
                with nc.Block(f"{sname}_{uid}") as b:
                    with ExitStack() as scope:
                        EMITTERS[sname](ctx, b, scope, uid)
                    uid += 1
            with nc.Block(f"store{seq}") as b:
                emit_store(ctx, b, seq, last=(seq == 1))
    return nc


EMITTERS = {}

SUBS = ["l0ffn", "l0attn", "l0ffn", "l1ffn", "l1attn", "l1ffn"]
FUSED = True


def kernel(**inputs):
    prep = host_prep(inputs)
    x = inputs["L0_attn_x"]
    xTs = [np.ascontiguousarray(x[2 * c:2 * c + 2].transpose(0, 2, 1)) for c in range(8)]
    groups = [SUBS] if FUSED else [[s] for s in SUBS]
    cur = xTs
    for g in groups:
        nc = build_program(g)
        in_maps = [dict(prep, xT=cur[c]) for c in range(8)]
        res = run_bass_kernel_spmd(nc, in_maps, core_ids=list(range(8)))
        cur = [np.asarray(res.results[c]["outT"]) for c in range(8)]
    out = np.empty((16, 2048, 1024), np.float32)
    for c in range(8):
        out[2 * c:2 * c + 2] = cur[c].transpose(0, 2, 1)
    return out


def emit_l1ffn(ctx, block, scope, uid):
    nc, S = ctx.nc, ctx.sems
    NTl, ntl = 512, 4
    sb = lambda n, sh, dt: scope.enter_context(nc.sbuf_tensor(f"{n}_{uid}", sh, dt))
    xlnT = [sb("xln0", [128, 8 * NTl], BF16), sb("xln1", [128, 8 * NTl], BF16)]
    bw = sb("bw", [128, 8 * 1024], BF16)
    rm = sb("rm", [128, 8 * 16], BF16)
    bflat = sb("bflat", [64, 1024], BF16)
    rmf = sb("rmf", [128, 64], F32)
    mx = sb("mx", [128, 16], F32)
    gat = sb("gat", [128, 32], F32)
    gbf = sb("gbf", [128, 32], BF16)
    gatf = sb("gatf", [128, 32], F32)
    G = sb("G", [128, 64], BF16)
    gt = sb("gt", [64, 512], BF16)
    lora = sb("lora", [128, 8 * NTl], F32)
    bbuf = sb("bbuf", [128, 8 * NTl], F32)

    em = E(ctx, block)

    def wload(g):
        g.dma_start(out=bw[:], in_=ctx.dram["bw"][:, :]).then_inc(S["wl"].h, 16)
        g.dma_start(out=rm[:], in_=ctx.dram["rm"][:, :]).then_inc(S["wl"].h, 16)
        g.dma_start(out=bflat[:], in_=ctx.dram["bflat"][:, :]).then_inc(S["wl"].h, 16)
    em.add("gpsimd", wload)
    wl_done = S["wl"].inc(48)

    bb = ctx.smalls[:, SM["base_b"][0]:SM["base_b"][1]]
    B6 = ctx.bank[6]
    ln_prev = [None]
    stage_end = {}
    ve_end = {}

    def emit_stage(k, after_base=None):
        par = k % 2
        lnv = ln_apply[k]
        v1s = {}
        for tt in range(4):
            bidx = 4 + (tt % 2)
            waits = [("ve", lnv), ("wl", wl_done)]
            if bank_state[bidx]:
                waits.append(bank_state[bidx])
            pairs = [(xlnT[par][:, c * NTl + tt * 128: c * NTl + (tt + 1) * 128],
                      rm[:, c * 16:(c + 1) * 16]) for c in range(8)]
            pairs.append((ctx.ones_row_bf[:], ctx.rows_bf[0:1, 1024:1040]))
            pv = mm_group(ctx, em, ctx.bank[bidx][:, 0:16], pairs, waits=waits)

            def small(v, tt=tt, bidx=bidx, pv=pv):
                v.wait_ge(S["pe"].h, pv)
                v.tensor_copy(rmf[:, 16 * tt:16 * tt + 16], ctx.bank[bidx][:, 0:16])
                v.reduce_max(mx[:, 4 * tt:4 * tt + 1], ctx.bank[bidx][:, 0:8], axis=AX.X)
                v.drain()
                v.tensor_scalar_mul(mx[:, 4 * tt + 1:4 * tt + 2],
                                    mx[:, 4 * tt:4 * tt + 1], -1.0)
                v.drain().then_inc(S["ve"].h, 1)
            em.add("vector", small)
            v1s[tt] = S["ve"].inc()
            bank_state[bidx] = ("ve", v1s[tt])

        a2s = {}
        for m in range(8):
            bidx = 4 + (m % 2)
            waits = [("ve", lnv)]
            if bank_state[bidx]:
                waits.append(bank_state[bidx])
            pairs = [(bw[:, c * 1024 + m * 128: c * 1024 + (m + 1) * 128],
                      xlnT[par][:, c * NTl:(c + 1) * NTl]) for c in range(8)]
            pv = mm_group(ctx, em, ctx.bank[bidx][:, 0:NTl], pairs, waits=waits)

            def bev(s, m=m, bidx=bidx, pv=pv, rp=r1_of_m.get(m)):
                s.wait_ge(S["pe"].h, pv)
                if rp is not None:
                    s.wait_ge(S["ve"].h, rp)
                s.activation(bbuf[:, m * NTl:(m + 1) * NTl], ctx.bank[bidx][:, 0:NTl],
                             AF.Identity, bias=bb[:, m:m + 1]).then_inc(S["ac"].h, 1)
            em.add("scalar", bev)
            a2s[m] = S["ac"].inc()
            bank_state[bidx] = ("ac", a2s[m])
        if after_base is not None:
            after_base()

        def sexp(s):
            for tt in range(4):
                s.wait_ge(S["ve"].h, v1s[tt])
            inst = None
            for tt in range(4):
                inst = s.activation(gat[:, 8 * tt:8 * tt + 8],
                                    rmf[:, 16 * tt:16 * tt + 8], AF.Exp,
                                    bias=mx[:, 4 * tt + 1:4 * tt + 2])
            inst.then_inc(S["ac"].h, 1)
        em.add("scalar", sexp)
        a1 = S["ac"].inc()

        def vsum(v, a1=a1):
            v.wait_ge(S["ac"].h, a1)
            for tt in range(4):
                v.reduce_sum(mx[:, 4 * tt + 2:4 * tt + 3],
                             gat[:, 8 * tt:8 * tt + 8], axis=AX.X)
            v.drain()
            v.reciprocal(mx[:, 3:16:4], mx[:, 2:16:4])
            v.drain().then_inc(S["ve"].h, 1)
        em.add("vector", vsum)
        vsum_v = S["ve"].inc()

        gt_ready = []
        for tt in range(4):
            def small2(v, tt=tt, vsum_v=vsum_v):
                v.wait_ge(S["ve"].h, vsum_v)
                v.tensor_scalar_mul(gbf[:, 8 * tt:8 * tt + 8], gat[:, 8 * tt:8 * tt + 8],
                                    mx[:, 4 * tt + 3:4 * tt + 4])
                v.drain()
                v.tensor_copy(gatf[:, 8 * tt:8 * tt + 8], gbf[:, 8 * tt:8 * tt + 8])
                v.drain()
                inst = None
                for e in range(8):
                    inst = v.tensor_scalar_mul(G[:, 8 * e:8 * e + 8],
                                               rmf[:, 16 * tt + 8:16 * tt + 16],
                                               gatf[:, 8 * tt + e:8 * tt + e + 1])
                v.drain().then_inc(S["ve"].h, 1)
            em.add("vector", small2)
            v2 = S["ve"].inc()

            bf_guard = tuple(bankb_free[0]) if (tt == 0 and bankb_free[0]) else None

            def tp(t, tt=tt, v2=v2, bf_guard=bf_guard):
                t.wait_ge(S["ve"].h, v2)
                if bf_guard:
                    t.wait_ge(S[bf_guard[0]].h, bf_guard[1])
                t.transpose(ctx.bankb[0:64, tt * 128:(tt + 1) * 128], G[:], ctx.idn[:]
                            ).then_inc(S["pe"].h, 1)
            em.add("tensor", tp)
            p2 = S["pe"].inc()

            def gtc(v, tt=tt, p2=p2):
                v.wait_ge(S["pe"].h, p2)
                v.tensor_copy(gt[:, tt * 128:(tt + 1) * 128],
                              ctx.bankb[0:64, tt * 128:(tt + 1) * 128]).then_inc(S["ve"].h, 1)
            em.add("vector", gtc)
            gt_ready.append(S["ve"].inc())
        bankb_free[0] = ("ve", gt_ready[-1])

        lora_ready = []
        rg = rups_done.get(k - 1)
        for m in range(8):
            waits = [("ve", gt_ready[-1])]
            if bank_state[6]:
                waits.append(bank_state[6])
            pv = mm_group(ctx, em, B6[:, 0:NTl],
                          [(bflat[0:64, m * 128:(m + 1) * 128], gt[0:64, 0:NTl])],
                          waits=waits)

            def lev(v, m=m, pv=pv, rg=rg):
                v.wait_ge(S["pe"].h, pv)
                if m == 0 and rg is not None:
                    v.wait_ge(S["gp"].h, rg)
                v.tensor_copy(lora[:, m * NTl:(m + 1) * NTl], B6[:, 0:NTl]).then_inc(S["ve"].h, 1)
            em.add("vector", lev)
            bank_state[6] = ("ve", S["ve"].inc())
            lora_ready.append(bank_state[6][1])

        for m in range(8):
            def rup1(v, m=m, a2=a2s[m], lr=lora_ready[m]):
                v.wait_ge(S["ac"].h, a2)
                v.wait_ge(S["ve"].h, lr)
                lm = lora[:, m * NTl:(m + 1) * NTl]
                v.scalar_tensor_tensor(lm, bbuf[:, m * NTl:(m + 1) * NTl], 0.5, lm,
                                       op0=ALU.mult, op1=ALU.add).then_inc(S["ve"].h, 1)
            em.add("vector", rup1)
            r1 = S["ve"].inc()
            r1_of_m[m] = r1

            def rup(g, m=m, k=k, r1=r1):
                g.wait_ge(S["ve"].h, r1)
                lm = lora[:, m * NTl:(m + 1) * NTl]
                rv = ctx.resid[:, m * T + k * NTl: m * T + (k + 1) * NTl]
                g.tensor_tensor(rv, rv, lm, ALU.add).then_inc(S["gp"].h, 1)
            em.add("gpsimd", rup)
            S["gp"].inc()
            if k == ntl - 1:
                chunk_gp[m] = S["gp"].v
        rups_done[k] = S["gp"].v
        stage_end[k] = S["pe"].v

    bank_state = {4: None, 5: None, 6: None}
    bankb_free = [None]
    rups_done = {}
    r1_of_m = {}
    chunk_gp = {}
    ln_apply = {}
    def src_of(k):
        return lambda c, k=k: ctx.resid[:, c * T + k * NTl: c * T + (k + 1) * NTl]

    cast_vals = {0: emit_ln_cast(ctx, em, NTl, src_of(0), src_ready=ctx.resid_ready)}
    fins = {}

    def run_fin(k):
        fins[k]()
        ln_apply[k] = ln_res[k]["apply"]

    ln_res = {}
    for k in range(ntl):
        lnp = ln_prev[0]
        aw = [("pe", stage_end[k - 2])] if (k - 2) in stage_end else None
        res, fin = emit_ln_tile(
            ctx, em, k, NTl,
            src_fn=src_of(k),
            mode="bf16",
            dst_fn=lambda c, k=k: xlnT[k % 2][:, c * NTl:(c + 1) * NTl],
            row_off=(k % 2) * 512,
            bank_sxx_free=(("ve", lnp["var"]) if lnp else None),
            bank_ab_free=(("ve", lnp["apply"]) if lnp else None),
            src_ready=ctx.resid_ready, precast=cast_vals[k], apply_waits=aw,
            defer_tail=True)
        ln_prev[0] = res
        ln_res[k] = res
        fins[k] = fin
        if k + 1 < ntl:
            cast_vals[k + 1] = emit_ln_cast(ctx, em, NTl, src_of(k + 1),
                                            src_ready=ctx.resid_ready,
                                            cast_guard=[("pe", res["stats"])])
        if k == 0:
            run_fin(0)
        else:
            emit_stage(k - 1, after_base=(lambda k=k: run_fin(k)))
    emit_stage(ntl - 1)
    em.flush()
    ctx.resid_ready = [("gp", S["gp"].v)]
    ctx.resid_chunk_ready = [("gp", chunk_gp[m]) for m in range(8)]
    ctx.resid_tile_ready = None


def emit_l0attn(ctx, block, scope, uid):
    nc, S = ctx.nc, ctx.sems
    NTl = 512
    sb = lambda n, sh, dt: scope.enter_context(nc.sbuf_tensor(f"{n}_{uid}", sh, dt))
    xattn = sb("xattn", [128, 8 * 2048], BF16)
    xpe = sb("xpe", [128, 8 * 2048], BF16)
    em = E(ctx, block)
    bq = ctx.smalls[:, SM["bq"][0]:SM["bq"][1]]
    bk = ctx.smalls[:, SM["bk"][0]:SM["bk"][1]]
    bo = ctx.smalls[:, SM["bo"][0]:SM["bo"][1]]
    gam = ctx.smalls[:, SM["gamma"][0]:SM["gamma"][1]]
    bet = ctx.smalls[:, SM["beta"][0]:SM["beta"][1]]
    B0, B1, B2, B3, B6 = ctx.bank[0], ctx.bank[1], ctx.bank[2], ctx.bank[3], ctx.bank[6]

    with ExitStack() as sA:
        sba = lambda n, sh, dt: sA.enter_context(nc.sbuf_tensor(f"{n}_{uid}", sh, dt))
        wqb = sba("wqb", [128, 1024], BF16)
        wkb = sba("wkb", [128, 1024], BF16)
        wvb = sba("wvb", [128, 1024], BF16)
        ksb = [sba("ksb0", [128, 2048], BF16), sba("ksb1", [128, 2048], BF16)]
        vsb = [sba("vsb0", [128, 16 * 130], BF16), sba("vsb1", [128, 16 * 130], BF16)]

        B4, B5, B6 = ctx.bank[4], ctx.bank[5], ctx.bank[6]
        streg = [ctx.psA[:, 0:1024], ctx.psA[:, 1024:2048]]
        bank_state = {6: None}
        streg_state = {0: None, 1: None}
        pv_free = {0: None, 1: None}
        wl_ready = {}
        proj_end = {}
        qkv_done = {}
        mb_done = {}

        def wload(p):
            guard = proj_end.get(p - 1)

            def f(g, p=p, guard=guard):
                if guard is not None:
                    g.wait_ge(S["pe"].h, guard)
                for nm, buf in (("wq", wqb), ("wkk", wkb), ("wvv", wvb)):
                    g.dma_start(out=buf[:, 0:1024],
                                in_=ctx.dram[nm][p, :, :]).then_inc(S["wl"].h, 16)
            em.add("gpsimd", f)
            wl_ready[p] = S["wl"].inc(48)

        def vones(v):
            for par in range(2):
                v.memset(vsb[par][:, 64:16 * 130:130], 1.0)
                v.memset(vsb[par][:, 129:16 * 130:130], 1.0)
        em.add("vector", vones)
        wload(0)

        def qk_step0(wbuf, dst, bias, qt, wait_v):
            waits = [("ve", wait_v), ("wl", wl_ready[0])]
            if bank_state[6]:
                waits.append(bank_state[6])
            pairs = [(wbuf[:, c * 128:(c + 1) * 128],
                      xpe[:, c * 2048 + qt * NTl: c * 2048 + (qt + 1) * NTl])
                     for c in range(8)]
            pv = mm_group(ctx, em, B6[:, 0:NTl], pairs, waits=waits)
            proj_end[0] = pv

            def ev(v, dst=dst, bias=bias, qt=qt, pv=pv):
                v.wait_ge(S["pe"].h, pv)
                v.tensor_scalar_add(dst[:, qt * NTl:(qt + 1) * NTl],
                                    B6[:, 0:NTl], bias[:, 0:1]).then_inc(S["ve"].h, 1)
            em.add("vector", ev)
            bank_state[6] = ("ve", S["ve"].inc())
            qkv_done[0] = bank_state[6][1]

        def v_step0(tt, wait_v):
            waits = [("ve", wait_v), ("wl", wl_ready[0])]
            if bank_state[6]:
                waits.append(bank_state[6])
            pairs = [(xpe[:, c * 2048 + tt * 128: c * 2048 + (tt + 1) * 128],
                      wvb[:, c * 128:(c + 1) * 128])
                     for c in range(8)]
            pairs.append((ctx.ones_row_bf[:], ctx.rows_bf[0:1, 0:128]))
            pv = mm_group(ctx, em, B6[:, 0:128], pairs, waits=waits)
            proj_end[0] = pv

            def vev(v, tt=tt, pv=pv):
                v.wait_ge(S["pe"].h, pv)
                v.tensor_copy(vsb[0][:, tt * 130:tt * 130 + 64], B6[:, 0:64])
                v.tensor_copy(vsb[0][:, tt * 130 + 65:tt * 130 + 129],
                              B6[:, 64:128]).then_inc(S["ve"].h, 1)
            em.add("vector", vev)
            bank_state[6] = ("ve", S["ve"].inc())
            qkv_done[0] = bank_state[6][1]

        ln_prev = [None]
        pe_ready = {}
        with ExitStack() as sA1:
            pet = sA1.enter_context(nc.sbuf_tensor(f"pet_{uid}", [128, 8 * NTl], BF16))
            for k in range(4):
                ap_prev = ln_prev[0]["apply"] if ln_prev[0] else None

                def peload(g, k=k, ap_prev=ap_prev):
                    if ap_prev is not None:
                        g.wait_ge(S["ve"].h, ap_prev)
                    for c in range(8):
                        g.dma_start(out=pet[:, c * NTl:(c + 1) * NTl],
                                    in_=ctx.dram["peT"][c * 128:(c + 1) * 128,
                                                        k * NTl:(k + 1) * NTl]).then_inc(S["wl"].h, 16)
                em.add("gpsimd", peload)
                pe_ready[k] = S["wl"].inc(128)
                lnp = ln_prev[0]
                res = emit_ln_tile(
                    ctx, em, k, NTl,
                    src_fn=lambda c, k=k: ctx.resid[:, c * T + k * NTl: c * T + (k + 1) * NTl],
                    mode="pe",
                    dst_fn=lambda c, k=k: xpe[:, c * 2048 + k * NTl: c * 2048 + (k + 1) * NTl],
                    extra_fn=lambda c: pet[:, c * NTl:(c + 1) * NTl],
                    row_off=(k % 2) * 512,
                    bank_sxx_free=(("ve", lnp["var"]) if lnp else None),
                    bank_ab_free=(("ve", lnp["apply"]) if lnp else None),
                    src_ready=ctx.resid_ready,
                    extra_ready=[("wl", pe_ready[k])], cast_eng="vector",
                    cast_guard=([("pe", lnp["stats"])] if lnp else None))
                ln_prev[0] = res
                qk_step0(wkb, ksb[0], bk, k, res["apply"])
                for tt in range(4 * k, 4 * k + 4):
                    v_step0(tt, res["apply"])
        xpe_done = ln_prev[0]["apply"]
        qsb = [sba("qsb0", [128, 2048], BF16), sba("qsb1", [128, 2048], BF16)]
        pch = sba("pch", [128, 4 * 1024], BF16)
        nrm = sba("nrm", [1, 64], F32)
        mbias = sba("mbias", [128, 4], F32)
        out65x = sba("o65a", [65, 512], F32)
        out65 = [out65x, out65x]
        for qt in range(4):
            qk_step0(wqb, qsb[0], bq, qt, xpe_done)

        def make_proj_steps(p):
            par = p % 2
            steps = []
            for wbuf, dst, bias in ((wqb, qsb[par], bq), (wkb, ksb[par], bk)):
                for qt in range(4):
                    def step(wbuf=wbuf, dst=dst, bias=bias, qt=qt, p=p, par=par):
                        waits = [("ve", xpe_done), ("wl", wl_ready[p])]
                        if bank_state[6]:
                            waits.append(bank_state[6])
                        pairs = [(wbuf[:, c * 128:(c + 1) * 128],
                                  xpe[:, c * 2048 + qt * NTl: c * 2048 + (qt + 1) * NTl])
                                 for c in range(8)]
                        pv = mm_group(ctx, em, B6[:, 0:NTl], pairs, waits=waits)
                        proj_end[p] = pv

                        def ev(v, dst=dst, bias=bias, p=p, qt=qt, pv=pv):
                            v.wait_ge(S["pe"].h, pv)
                            v.tensor_scalar_add(dst[:, qt * NTl:(qt + 1) * NTl],
                                                B6[:, 0:NTl],
                                                bias[:, p:p + 1]).then_inc(S["ve"].h, 1)
                        em.add("vector", ev)
                        bank_state[6] = ("ve", S["ve"].inc())
                        qkv_done[p] = bank_state[6][1]
                    steps.append(step)
            for tt in range(16):
                def step(tt=tt, p=p, par=par):
                    waits = [("ve", xpe_done), ("wl", wl_ready[p])]
                    if bank_state[6]:
                        waits.append(bank_state[6])
                    pairs = [(xpe[:, c * 2048 + tt * 128: c * 2048 + (tt + 1) * 128],
                              wvb[:, c * 128:(c + 1) * 128])
                             for c in range(8)]
                    pairs.append((ctx.ones_row_bf[:], ctx.rows_bf[0:1, p * 128:(p + 1) * 128]))
                    pv = mm_group(ctx, em, B6[:, 0:128], pairs, waits=waits)
                    proj_end[p] = pv

                    def vev(v, tt=tt, par=par, pv=pv):
                        v.wait_ge(S["pe"].h, pv)
                        v.tensor_copy(vsb[par][:, tt * 130:tt * 130 + 64], B6[:, 0:64])
                        v.tensor_copy(vsb[par][:, tt * 130 + 65:tt * 130 + 129],
                                      B6[:, 64:128]).then_inc(S["ve"].h, 1)
                    em.add("vector", vev)
                    bank_state[6] = ("ve", S["ve"].inc())
                    qkv_done[p] = bank_state[6][1]
                steps.append(step)
            return steps

        def emit_norm(p):
            par = p % 2

            def sq(v, p=p, par=par, qd=qkv_done[p]):
                v.wait_ge(S["ve"].h, qd)
                v.tensor_tensor(ctx.xcastT[:, 0:2048], qsb[par][:], qsb[par][:], ALU.mult)
                v.tensor_tensor(ctx.xcastT[:, 2048:4096], ksb[par][:], ksb[par][:],
                                ALU.mult).then_inc(S["ve"].h, 1)
            em.add("vector", sq)
            sq_done = S["ve"].inc()
            for h2 in range(2):
                for i, base in enumerate((0, 2048)):
                    for qt in range(4):
                        waits = [("ve", sq_done)]
                        if bank_state[6]:
                            waits.append(bank_state[6])
                        pv = mm_group(ctx, em, B6[0:1, 0:NTl],
                                      [(ctx.ones_pcol_bf[64 * h2:64 * h2 + 64, :],
                                        ctx.xcastT[64 * h2:64 * h2 + 64,
                                                   base + qt * NTl: base + (qt + 1) * NTl])],
                                      waits=waits)

                        def nrd(v, h2=h2, i=i, qt=qt, pv=pv):
                            v.wait_ge(S["pe"].h, pv)
                            v.reduce_max(nrm[0:1, 8 * h2 + 4 * i + qt: 8 * h2 + 4 * i + qt + 1],
                                         B6[0:1, 0:NTl], axis=AX.X).then_inc(S["ve"].h, 1)
                        em.add("vector", nrd)
                        bank_state[6] = ("ve", S["ve"].inc())

            def mcomb_v(v):
                v.drain()
                for h2 in range(2):
                    v.reduce_max(nrm[0:1, 16 + h2:17 + h2], nrm[0:1, 8 * h2:8 * h2 + 4], axis=AX.X)
                    v.reduce_max(nrm[0:1, 18 + h2:19 + h2], nrm[0:1, 8 * h2 + 4:8 * h2 + 8], axis=AX.X)
                v.drain()
                for h2 in range(2):
                    v.tensor_tensor(nrm[0:1, 20 + h2:21 + h2], nrm[0:1, 16 + h2:17 + h2],
                                    nrm[0:1, 18 + h2:19 + h2], ALU.mult)
                v.drain().then_inc(S["ve"].h, 1)
            em.add("vector", mcomb_v)
            vq = S["ve"].inc()

            def mcomb_s(s, vq=vq):
                s.wait_ge(S["ve"].h, vq)
                s.activation(nrm[0:1, 22:24], nrm[0:1, 20:22], AF.Ln, bias=ctx.zcol[0:1, :])
                s.drain()
                s.activation(nrm[0:1, 24:26], nrm[0:1, 22:24], AF.Exp, scale=0.5,
                             bias=ctx.ln8col[0:1, :])
                s.drain().then_inc(S["ac"].h, 1)
            em.add("scalar", mcomb_s)
            aM = S["ac"].inc()

            def mneg(v, aM=aM):
                v.wait_ge(S["ac"].h, aM)
                v.tensor_scalar_mul(nrm[0:1, 26:28], nrm[0:1, 24:26], -1.0)
                v.drain().then_inc(S["ve"].h, 1)
            em.add("vector", mneg)
            vneg = S["ve"].inc()
            for h2 in range(2):
                waits = [("ve", vneg)]
                if bank_state[6]:
                    waits.append(bank_state[6])
                pv = mm_group(ctx, em, B6[:, 0:1],
                              [(ctx.ones_row_f[:], nrm[0:1, 26 + h2:27 + h2])], waits=waits)

                def mbc(v, h2=h2, par=par, pv=pv):
                    v.wait_ge(S["pe"].h, pv)
                    v.tensor_copy(mbias[:, par * 2 + h2:par * 2 + h2 + 1],
                                  B6[:, 0:1]).then_inc(S["ve"].h, 1)
                em.add("vector", mbc)
                bank_state[6] = ("ve", S["ve"].inc())
            mb_done[p] = S["ve"].v

        def emit_inner(p, fillers):
            par = p % 2
            for qt in range(4):
                expv = {}
                pvv = {}

                def emit_pv(kt, qt=qt, par=par, expv=expv, pvv=pvv):
                    for h2, pvbank in ((0, B4), (1, B5)):
                        g = pv_free[h2] if kt == 0 else None

                        def pvmm(t, kt=kt, h2=h2, pvbank=pvbank, exv=expv[(h2, kt // 2)],
                                 g=g, par=par):
                            t.wait_ge(S["ac"].h, exv)
                            if g:
                                t.wait_ge(S[g[0]].h, g[1])
                            t.matmul(pvbank[0:65, 0:NTl],
                                     vsb[par][:, kt * 130 + 65 * h2: kt * 130 + 65 * h2 + 65],
                                     pch[:, ((h2 * 2 + kt // 2 % 2) * 1024 + (kt % 2) * 512):
                                          ((h2 * 2 + kt // 2 % 2) * 1024 + (kt % 2) * 512) + NTl],
                                     start=(kt == 0), stop=(kt == 15)).then_inc(S["pe"].h, 1)
                        em.add("tensor", pvmm)
                        pvv[(h2, kt)] = S["pe"].inc()

                for kp in range(8):
                    if fillers:
                        fillers.pop(0)()
                    stv = {}
                    for h2 in range(2):
                        waits = [("ve", qkv_done[p])]
                        if streg_state[h2]:
                            waits.append(streg_state[h2])
                        pairs = []
                        stv[h2] = None
                        for half in range(2):
                            kt = 2 * kp + half
                            pr = [(ksb[par][64 * h2:64 * h2 + 64, kt * 128:(kt + 1) * 128],
                                   qsb[par][64 * h2:64 * h2 + 64, qt * NTl:(qt + 1) * NTl])]
                            w = waits if half == 0 else ()
                            stv[h2] = mm_group(ctx, em,
                                               streg[h2][:, half * 512:half * 512 + NTl],
                                               pr, waits=w)

                        def ex(s, kp=kp, h2=h2, par=par, stv=stv[h2],
                               mbd=(mb_done[p] if kp == 0 else None),
                               pvg=(pvv.get((h2, 2 * kp - 3)))):
                            s.wait_ge(S["pe"].h, stv)
                            if mbd is not None:
                                s.wait_ge(S["ve"].h, mbd)
                            if pvg is not None:
                                s.wait_ge(S["pe"].h, pvg)
                            s.activation(pch[:, (h2 * 2 + kp % 2) * 1024:
                                              (h2 * 2 + kp % 2) * 1024 + 1024],
                                         streg[h2][:], AF.Exp,
                                         bias=mbias[:, par * 2 + h2:par * 2 + h2 + 1],
                                         scale=0.125).then_inc(S["ac"].h, 1)
                        em.add("scalar", ex)
                        exv = S["ac"].inc()
                        expv[(h2, kp)] = exv
                        streg_state[h2] = ("ac", exv)
                    if kp >= 1:
                        emit_pv(2 * kp - 2)
                        emit_pv(2 * kp - 1)
                emit_pv(14)
                emit_pv(15)
                for h2, pvbank in ((0, B4), (1, B5)):
                    pv_done = pvv[(h2, 15)]

                    def cp65(v, pv_done=pv_done, pvbank=pvbank, h2=h2):
                        v.wait_ge(S["pe"].h, pv_done)
                        v.tensor_copy(out65[h2][0:64, 0:NTl], pvbank[0:64, 0:NTl])
                        v.tensor_copy(ctx.statS[0:1, 0:NTl], pvbank[64:65, 0:NTl])
                        v.drain().then_inc(S["ve"].h, 1)
                    em.add("vector", cp65)
                    vrec = S["ve"].inc()
                    pv_free[h2] = ("ve", vrec)
                    waits = [("ve", vrec)]
                    if bank_state[6]:
                        waits.append(bank_state[6])
                    pbc = mm_group(ctx, em, B6[0:64, 0:NTl],
                                   [(ctx.ones_row_f[0:1, 0:64], ctx.statS[0:1, 0:NTl])],
                                   waits=waits)

                    def srec(s, pbc=pbc):
                        s.wait_ge(S["pe"].h, pbc)
                        s.activation(ctx.lnTmp[0:64, 0:NTl], B6[0:64, 0:NTl],
                                     AF.Ln, bias=ctx.zcol[0:64, 0:1])
                        s.activation(B6[0:64, 0:NTl], ctx.lnTmp[0:64, 0:NTl],
                                     AF.Exp, scale=-1.0).then_inc(S["ac"].h, 1)
                    em.add("scalar", srec)
                    a_rec = S["ac"].inc()

                    def fin(v, p=p, h2=h2, qt=qt, a_rec=a_rec):
                        v.wait_ge(S["ac"].h, a_rec)
                        v.tensor_tensor(
                            xattn[64 * h2:64 * h2 + 64,
                                  p * 2048 + qt * NTl: p * 2048 + (qt + 1) * NTl],
                            out65[h2][0:64, 0:NTl], B6[0:64, 0:NTl],
                            ALU.mult).then_inc(S["ve"].h, 2)
                    em.add("vector", fin)
                    S["ve"].inc(2)
                    bank_state[6] = ("ve", S["ve"].v)
                if qt == 2 and p + 1 < 8:
                    while fillers:
                        fillers.pop(0)()
                    emit_norm(p + 1)

        emit_norm(0)
        for p in range(8):
            fillers = []
            if p + 1 < 8:
                wload(p + 1)
                fillers = make_proj_steps(p + 1)
            emit_inner(p, fillers)
        xattn_done = S["ve"].v

    wo = sb("wo", [128, 8 * 1024], BF16)
    h_sb = sb("h_sb", [128, 8 * NTl], F32)

    def wol(g, xattn_done=xattn_done):
        g.wait_ge(S["ve"].h, xattn_done)
        g.dma_start(out=wo[:], in_=ctx.dram["wo"][:, :]).then_inc(S["wl"].h, 16)
    em.add("gpsimd", wol)
    wo_ready = S["wl"].inc(16)

    bank_state = {4: None, 5: None}
    ln_prev2 = [None]
    for kt in range(4):
        h_ready = []
        hguard = ln_prev2[0]["apply"] if ln_prev2[0] else None
        for m in range(8):
            bidx = 4 + (m % 2)
            waits = [("ve", xattn_done), ("wl", wo_ready)]
            if bank_state[bidx]:
                waits.append(bank_state[bidx])
            pairs = [(wo[:, c * 1024 + m * 128: c * 1024 + (m + 1) * 128],
                      xattn[:, c * 2048 + kt * NTl: c * 2048 + (kt + 1) * NTl])
                     for c in range(8)]
            pv = mm_group(ctx, em, ctx.bank[bidx][:, 0:NTl], pairs, waits=waits)

            def hev(s, m=m, bidx=bidx, pv=pv, hguard=hguard):
                s.wait_ge(S["pe"].h, pv)
                if m == 0 and hguard is not None:
                    s.wait_ge(S["ve"].h, hguard)
                s.activation(h_sb[:, m * NTl:(m + 1) * NTl], ctx.bank[bidx][:, 0:NTl],
                             AF.Identity, bias=bo[:, m:m + 1]).then_inc(S["ac"].h, 1)
            em.add("scalar", hev)
            a1 = S["ac"].inc()
            bank_state[bidx] = ("ac", a1)

            def hadd(v, m=m, kt=kt, a1=a1):
                v.wait_ge(S["ac"].h, a1)
                hv = h_sb[:, m * NTl:(m + 1) * NTl]
                v.tensor_tensor(hv, hv, xpe[:, m * 2048 + kt * NTl: m * 2048 + (kt + 1) * NTl],
                                ALU.add).then_inc(S["ve"].h, 1)
            em.add("vector", hadd)
            h_ready.append(S["ve"].inc())
        lnp = ln_prev2[0]
        res = emit_ln_tile(
            ctx, em, kt, NTl,
            src_fn=lambda c: h_sb[:, c * NTl:(c + 1) * NTl],
            mode="gb_resid",
            dst_fn=lambda c, kt=kt: ctx.resid[:, c * T + kt * NTl: c * T + (kt + 1) * NTl],
            gb=(lambda c: gam[:, c:c + 1], lambda c: bet[:, c:c + 1]),
            row_off=(kt % 2) * 512,
            bank_sxx_free=(("ve", lnp["var"]) if lnp else None),
            bank_ab_free=(("ve", lnp["apply"]) if lnp else None),
            src_ready=[("ve", h_ready[-1])])
        ln_prev2[0] = res
    em.flush()
    ctx.resid_ready = [("gp", S["gp"].v)]
    ctx.resid_chunk_ready = None
    ctx.resid_tile_ready = None



def emit_l1attn(ctx, block, scope, uid):
    nc, S = ctx.nc, ctx.sems
    NTl, ntl = 512, 4
    sb = lambda n, sh, dt: scope.enter_context(nc.sbuf_tensor(f"{n}_{uid}", sh, dt))
    xlnT = [sb("xln0", [128, 8 * NTl], BF16), sb("xln1", [128, 8 * NTl], BF16)]
    uc = sb("uc", [128, 16 * NTl], BF16)
    sg = sb("sg", [128, 16 * NTl], BF16)
    yg = sb("yg", [128, 16 * NTl], BF16)
    winb = sb("winb", [128, 2 * 1024], BF16)
    wbcb = sb("wbcb", [128, 4 * 2048], BF16)
    woutb = sb("woutb", [128, 2 * 2048], BF16)
    cacc = sb("cacc", [128, NTl], F32)
    ssb = sb("ssb", [128, NTl], F32)
    ysb = sb("ysb", [128, NTl], F32)
    bsb = sb("bsb", [128, NTl], F32)
    otm = sb("otm", [128, NTl], F32)

    em = E(ctx, block)
    bing = ctx.smalls[:, SM["b_in_g"][0]:SM["b_in_g"][1]]
    bbcb = ctx.smalls[:, SM["b_bc_b"][0]:SM["b_bc_b"][1]]
    bbcc = ctx.smalls[:, SM["b_bc_c"][0]:SM["b_bc_c"][1]]
    bout = ctx.smalls[:, SM["b_out"][0]:SM["b_out"][1]]
    sbias = ctx.smalls2[:, 32:48]

    def vinit(v):
        v.memset(ctx.carry[:], 0.0)
        v.memset(ctx.tails[:], 0.0).then_inc(S["ve"].h, 1)
    em.add("vector", vinit)
    init_done = S["ve"].inc()

    wl_ready = {}

    pc_tgt = S["pc"].v

    def wdma(key, src_ap, dst_ap, guard):
        def f(g, src_ap=src_ap, dst_ap=dst_ap, guard=guard):
            if pc_tgt:
                g.wait_ge(S["pc"].h, pc_tgt)
            if guard is not None:
                g.wait_ge(S["pe"].h, guard)
            g.dma_start(out=dst_ap, in_=src_ap).then_inc(S["wl"].h, 16)
        em.add("gpsimd", f)
        wl_ready[key] = S["wl"].inc(16)

    def win_slot(i):
        return winb[:, (i % 2) * 1024:(i % 2 + 1) * 1024]

    def wbc_slot(i):
        return wbcb[:, (i % 4) * 2048:(i % 4 + 1) * 2048]

    def wout_slot(i):
        return woutb[:, (i % 2) * 2048:(i % 2 + 1) * 2048]

    bank_state = {4: None, 5: None}
    ln_prev = [None]
    ln_apply = {}
    stage_end = {}
    bc_end = {}
    yg_done_t = {}
    wout_end = {}

    def emit_stage(tc, after_u=None):
        par = tc % 2
        lnv = ln_apply[tc]
        silu_prev = [None]
        for j in range(16):
            jb = j % 2
            waits = [("ve", lnv), ("wl", wl_ready[("win", j, tc)])]
            if bank_state[4 + jb]:
                waits.append(bank_state[4 + jb])
            pairs = [(win_slot(j)[:, c * 128:(c + 1) * 128],
                      xlnT[par][:, c * NTl:(c + 1) * NTl]) for c in range(8)]
            pv = mm_group(ctx, em, ctx.bank[4 + jb][:, 0:NTl], pairs, waits=waits)
            if j + 2 < 16:
                wdma(("win", j + 2, tc), ctx.dram["win16"][j + 2, :, :], win_slot(j), pv)
            else:
                wdma(("wgate", j + 2 - 16, tc), ctx.dram["win16"][16 + j + 2 - 16, :, :],
                     win_slot(j), pv)

            wj = [ctx.wconv[:, 4 * j + i:4 * j + i + 1] for i in range(4)]
            tlr = ctx.tails[:, (tc % 2) * 48 + 3 * j:(tc % 2) * 48 + 3 * j + 3]
            tlw = ctx.tails[:, (1 - tc % 2) * 48 + 3 * j:(1 - tc % 2) * 48 + 3 * j + 3]
            bank = ctx.bank[4 + jb]

            def conv(v, j=j, tc=tc, pv=pv, wj=wj, tlr=tlr, tlw=tlw, bank=bank, sp=silu_prev[0]):
                v.wait_ge(S["pe"].h, pv)
                if tc == 0 and j == 0:
                    v.wait_ge(S["ve"].h, init_done)
                if sp is not None:
                    v.wait_ge(S["ac"].h, sp)
                v.tensor_scalar_mul(cacc[:], bank[:, 0:NTl], wj[3])
                v.scalar_tensor_tensor(cacc[:, 1:NTl], bank[:, 0:NTl - 1], wj[2],
                                       cacc[:, 1:NTl], op0=ALU.mult, op1=ALU.add)
                v.scalar_tensor_tensor(cacc[:, 2:NTl], bank[:, 0:NTl - 2], wj[1],
                                       cacc[:, 2:NTl], op0=ALU.mult, op1=ALU.add)
                v.scalar_tensor_tensor(cacc[:, 3:NTl], bank[:, 0:NTl - 3], wj[0],
                                       cacc[:, 3:NTl], op0=ALU.mult, op1=ALU.add)
                v.scalar_tensor_tensor(cacc[:, 0:3], tlr, wj[0], cacc[:, 0:3],
                                       op0=ALU.mult, op1=ALU.add)
                v.drain()
                v.scalar_tensor_tensor(cacc[:, 0:2], tlr[:, 1:3], wj[1], cacc[:, 0:2],
                                       op0=ALU.mult, op1=ALU.add)
                v.drain()
                v.scalar_tensor_tensor(cacc[:, 0:1], tlr[:, 2:3], wj[2], cacc[:, 0:1],
                                       op0=ALU.mult, op1=ALU.add)
                v.tensor_copy(tlw, bank[:, NTl - 3:NTl])
                v.drain().then_inc(S["ve"].h, 1)
            em.add("vector", conv)
            cv = S["ve"].inc()
            bank_state[4 + jb] = ("ve", cv)

            def silu(s, j=j, tc=tc, cv=cv, bce=bc_end.get(tc - 1)):
                s.wait_ge(S["ve"].h, cv)
                if j == 0 and bce is not None:
                    s.wait_ge(S["pe"].h, bce)
                s.activation(uc[:, j * NTl:(j + 1) * NTl], cacc[:], AF.Silu,
                             bias=sbias[:, j:j + 1]).then_inc(S["ac"].h, 1)
            em.add("scalar", silu)
            silu_prev[0] = S["ac"].inc()
        uc_done_ac = S["ac"].v
        if after_u is not None:
            after_u()

        for j in range(16):
            jb = j % 2
            waits = [("ve", lnv), ("wl", wl_ready[("wgate", j, tc)])]
            if bank_state[4 + jb]:
                waits.append(bank_state[4 + jb])
            pairs = [(win_slot(j)[:, c * 128:(c + 1) * 128],
                      xlnT[par][:, c * NTl:(c + 1) * NTl]) for c in range(8)]
            pv = mm_group(ctx, em, ctx.bank[4 + jb][:, 0:NTl], pairs, waits=waits)
            if j + 2 < 16:
                wdma(("wgate", j + 2, tc), ctx.dram["win16"][16 + j + 2, :, :], win_slot(j), pv)
            elif j == 14:
                wdma(("wbc_b", 0, tc), ctx.dram["wbc16"][0, :, :], wbc_slot(0), pv)
                wdma(("wbc_c", 0, tc), ctx.dram["wbc16"][16, :, :], wbc_slot(1), pv)
            elif j == 15:
                wdma(("wbc_b", 1, tc), ctx.dram["wbc16"][1, :, :], wbc_slot(2), pv)
                wdma(("wbc_c", 1, tc), ctx.dram["wbc16"][17, :, :], wbc_slot(3), pv)

            def gev(s, j=j, tc=tc, jb=jb, pv=pv, ygd=yg_done_t.get(tc - 1)):
                s.wait_ge(S["pe"].h, pv)
                if j == 0 and ygd is not None:
                    s.wait_ge(S["ve"].h, ygd)
                s.activation(sg[:, j * NTl:(j + 1) * NTl], ctx.bank[4 + jb][:, 0:NTl],
                             AF.Sigmoid, bias=bing[:, j:j + 1]).then_inc(S["ac"].h, 1)
            em.add("scalar", gev)
            bank_state[4 + jb] = ("ac", S["ac"].inc())

        scan_prev = [None]
        for j in range(16):
            waits = [("ac", uc_done_ac), ("wl", wl_ready[("wbc_b", j, tc)])]
            if bank_state[4]:
                waits.append(bank_state[4])
            pairs = [(wbc_slot(2 * (j % 2))[:, ci * 128:(ci + 1) * 128],
                      uc[:, ci * NTl:(ci + 1) * NTl]) for ci in range(16)]
            pvb = mm_group(ctx, em, ctx.bank[4][:, 0:NTl], pairs, waits=waits)
            if j + 2 < 16:
                wdma(("wbc_b", j + 2, tc), ctx.dram["wbc16"][j + 2, :, :],
                     wbc_slot(2 * (j % 2)), pvb)

            def bev(s, j=j, pvb=pvb, sp=scan_prev[0]):
                s.wait_ge(S["pe"].h, pvb)
                if sp is not None:
                    s.wait_ge(S["ve"].h, sp)
                s.activation(bsb[:], ctx.bank[4][:, 0:NTl], AF.Identity,
                             bias=bbcb[:, j:j + 1]).then_inc(S["ac"].h, 1)
            em.add("scalar", bev)
            b_ready = S["ac"].inc()
            bank_state[4] = ("ac", b_ready)

            waits = [("ac", uc_done_ac), ("wl", wl_ready[("wbc_c", j, tc)])]
            if bank_state[5]:
                waits.append(bank_state[5])
            pairs = [(wbc_slot(2 * (j % 2) + 1)[:, ci * 128:(ci + 1) * 128],
                      uc[:, ci * NTl:(ci + 1) * NTl]) for ci in range(16)]
            pvc = mm_group(ctx, em, ctx.bank[5][:, 0:NTl], pairs, waits=waits)
            if j + 2 < 16:
                wdma(("wbc_c", j + 2, tc), ctx.dram["wbc16"][16 + j + 2, :, :],
                     wbc_slot(2 * (j % 2) + 1), pvc)

            def scan(v, j=j, tc=tc, b_ready=b_ready, pvc=pvc, we=wout_end.get(tc - 1)):
                v.wait_ge(S["ac"].h, b_ready)
                v.wait_ge(S["pe"].h, pvc)
                if j == 0 and we is not None:
                    v.wait_ge(S["pe"].h, we)
                a_b = ctx.adecay[:, j:j + 1].broadcast_to([128, NTl])
                v.tensor_tensor_scan(ssb[:], a_b, bsb[:], ctx.carry[:, j:j + 1],
                                     op0=ALU.mult, op1=ALU.add)
                v.tensor_tensor(ysb[:], ssb[:], ctx.bank[5][:, 0:NTl], ALU.mult)
                v.scalar_tensor_tensor(ysb[:], ssb[:], bbcc[:, j:j + 1], ysb[:],
                                       op0=ALU.mult, op1=ALU.add)
                v.tensor_tensor(yg[:, j * NTl:(j + 1) * NTl], ysb[:],
                                sg[:, j * NTl:(j + 1) * NTl], ALU.mult)
                v.tensor_copy(ctx.carry[:, j:j + 1], ssb[:, NTl - 1:NTl])
                v.drain().then_inc(S["ve"].h, 1)
            em.add("vector", scan)
            scan_prev[0] = S["ve"].inc()
            bank_state[5] = ("ve", scan_prev[0])
        yg_done = S["ve"].v
        yg_done_t[tc] = yg_done
        bc_end[tc] = S["pe"].v

        rup_prev = [None]
        for m in range(8):
            mb = m % 2
            waits = [("ve", yg_done), ("wl", wl_ready[("wout", m, tc)])]
            if bank_state[4 + mb]:
                waits.append(bank_state[4 + mb])
            pairs = [(wout_slot(m)[:, ci * 128:(ci + 1) * 128],
                      yg[:, ci * NTl:(ci + 1) * NTl]) for ci in range(16)]
            pv = mm_group(ctx, em, ctx.bank[4 + mb][:, 0:NTl], pairs, waits=waits)
            if m + 2 < 8:
                wdma(("wout", m + 2, tc), ctx.dram["wout16"][m + 2, :, :], wout_slot(m), pv)
            elif tc + 1 < ntl:
                if m == 6:
                    wdma(("win", 0, tc + 1), ctx.dram["win16"][0, :, :], win_slot(0), pv)
                    wdma(("win", 1, tc + 1), ctx.dram["win16"][1, :, :], win_slot(1), pv)
                else:
                    wdma(("wout", 0, tc + 1), ctx.dram["wout16"][0, :, :], wout_slot(0), pv)
                    wdma(("wout", 1, tc + 1), ctx.dram["wout16"][1, :, :], wout_slot(1), pv)

            def oev(s, m=m, mb=mb, pv=pv, rp=rup_prev[0]):
                s.wait_ge(S["pe"].h, pv)
                if rp is not None:
                    s.wait_ge(S["ve"].h, rp)
                s.activation(otm[:], ctx.bank[4 + mb][:, 0:NTl], AF.Identity,
                             bias=bout[:, m:m + 1]).then_inc(S["ac"].h, 1)
            em.add("scalar", oev)
            a1 = S["ac"].inc()
            bank_state[4 + mb] = ("ac", a1)

            def rup(v, m=m, tc=tc, a1=a1):
                v.wait_ge(S["ac"].h, a1)
                rv = ctx.resid[:, m * T + tc * NTl: m * T + (tc + 1) * NTl]
                v.tensor_tensor(rv, rv, otm[:], ALU.add).then_inc(S["ve"].h, 1)
            em.add("vector", rup)
            rup_prev[0] = S["ve"].inc()
        stage_end[tc] = S["pe"].v
        wout_end[tc] = S["pe"].v

    wdma(("win", 0, 0), ctx.dram["win16"][0, :, :], win_slot(0), None)
    wdma(("win", 1, 0), ctx.dram["win16"][1, :, :], win_slot(1), None)
    wdma(("wout", 0, 0), ctx.dram["wout16"][0, :, :], wout_slot(0), None)
    wdma(("wout", 1, 0), ctx.dram["wout16"][1, :, :], wout_slot(1), None)

    def src_of(k):
        return lambda c, k=k: ctx.resid[:, c * T + k * NTl: c * T + (k + 1) * NTl]

    cast_vals = {0: emit_ln_cast(ctx, em, NTl, src_of(0), src_ready=ctx.resid_ready)}
    fins = {}
    ln_res = {}

    def run_fin(k):
        fins[k]()
        ln_apply[k] = ln_res[k]["apply"]

    for k in range(ntl):
        lnp = ln_prev[0]
        aw = [("pe", stage_end[k - 2])] if (k - 2) in stage_end else None
        res, fin = emit_ln_tile(
            ctx, em, k, NTl,
            src_fn=src_of(k),
            mode="bf16",
            dst_fn=lambda c, k=k: xlnT[k % 2][:, c * NTl:(c + 1) * NTl],
            row_off=(k % 2) * 512,
            bank_sxx_free=(("ve", lnp["var"]) if lnp else None),
            bank_ab_free=(("ve", lnp["apply"]) if lnp else None),
            src_ready=ctx.resid_ready, precast=cast_vals[k], apply_waits=aw,
            defer_tail=True)
        ln_prev[0] = res
        ln_res[k] = res
        fins[k] = fin
        if k + 1 < ntl:
            cast_vals[k + 1] = emit_ln_cast(ctx, em, NTl, src_of(k + 1),
                                            src_ready=ctx.resid_ready,
                                            cast_guard=[("pe", res["stats"])])
        if k == 0:
            run_fin(0)
        else:
            emit_stage(k - 1, after_u=(lambda k=k: run_fin(k)))
    emit_stage(ntl - 1)
    em.flush()
    ctx.resid_ready = [("ve", S["ve"].v)]
    ctx.resid_chunk_ready = None
    ctx.resid_tile_ready = None


EMITTERS.update({"l0ffn": emit_l0ffn, "l1ffn": emit_l1ffn, "l0attn": emit_l0attn, "l1attn": emit_l1attn})


# version 31
# speedup vs baseline: 1.0932x; 1.0001x over previous
import numpy as np
import ml_dtypes
from contextlib import ExitStack

import concourse.bass as bass
import concourse.mybir as mybir
from concourse.bass_utils import run_bass_kernel_spmd

F32 = mybir.dt.float32
BF16 = mybir.dt.bfloat16
AF = mybir.ActivationFunctionType
ALU = mybir.AluOpType
AX = mybir.AxisListType

D = 1024
T = 2048
C8 = 8
C16 = 16
NT = 512
NTILES = T // NT
EPS = 1e-5
PAD = 4
XW = T + PAD

BF = ml_dtypes.bfloat16


def _prep_lhsT(w):
    IN, OUT = w.shape
    return np.ascontiguousarray(w.reshape(IN // 128, 128, OUT).transpose(1, 0, 2).reshape(128, -1))


def _prep_out_blocked(w, blk=128):
    IN, OUT = w.shape
    return np.ascontiguousarray(
        w.reshape(IN // 128, 128, OUT // blk, blk).transpose(2, 1, 0, 3).reshape(OUT // blk, 128, -1))


def _cols(v):
    return np.ascontiguousarray(v.reshape(-1, 128).T)


def host_prep(I):
    p = {}
    p["wr"] = _prep_lhsT(np.ascontiguousarray(I["L0_ffn_W_r"].T))
    p["wk"] = _prep_lhsT(np.ascontiguousarray(I["L0_ffn_W_k"].T))
    p["wv"] = _prep_lhsT(np.ascontiguousarray(I["L0_ffn_W_v"].T))
    p["wq"] = _prep_out_blocked(I["L0_attn_wq"])
    p["wkk"] = _prep_out_blocked(I["L0_attn_wk"])
    p["wvv"] = _prep_out_blocked(I["L0_attn_wv"])
    p["wo"] = _prep_lhsT(I["L0_attn_wo"])
    p["peT"] = np.ascontiguousarray(I["L0_attn_pe"][0, :T, :].T)
    p["win"] = _prep_out_blocked(I["L1_attn_W_in"])
    p["wbc"] = _prep_out_blocked(I["L1_attn_W_bc"])
    p["wout"] = _prep_out_blocked(I["L1_attn_W_out"])
    p["wconv"] = np.ascontiguousarray(
        I["L1_attn_conv_w"][:, 0, :].reshape(16, 128, 4).transpose(1, 0, 2).reshape(128, 64))
    p["bw"] = _prep_lhsT(np.ascontiguousarray(I["L1_ffn_base_w"].T))
    p["rm"] = _prep_lhsT(np.ascontiguousarray(
        np.concatenate([I["L1_ffn_router_w"].T, I["L1_ffn_lora_A"]], axis=1)))
    p["bflat"] = np.ascontiguousarray(I["L1_ffn_lora_B"].reshape(64, 1024))
    cols = [
        _cols(I["L0_ffn_mu_r"]),
        _cols(I["L0_ffn_mu_k"]),
        _cols(I["L0_attn_bq"]),
        _cols(I["L0_attn_bk"]),
        _cols(I["L0_attn_bo"]),
        _cols(I["L0_attn_gamma"]),
        _cols(I["L0_attn_beta"]),
        np.ascontiguousarray(I["L1_attn_b_in"][:2048].reshape(16, 128).T),
        np.ascontiguousarray(I["L1_attn_b_in"][2048:].reshape(16, 128).T),
        np.ascontiguousarray(I["L1_attn_conv_b"].reshape(16, 128).T),
        np.ascontiguousarray(I["L1_attn_log_decay"].reshape(16, 128).T),
        np.ascontiguousarray(I["L1_attn_b_bc"][:2048].reshape(16, 128).T),
        np.ascontiguousarray(I["L1_attn_b_bc"][2048:].reshape(16, 128).T),
        _cols(I["L1_attn_b_out"]),
        _cols(I["L1_ffn_base_b"]),
    ]
    p["smalls"] = np.ascontiguousarray(np.concatenate(cols, axis=1))
    rows = [
        I["L0_attn_bv"],
        np.concatenate([I["L1_ffn_router_b"], np.zeros(8, np.float32)]),
    ]
    p["rows"] = np.ascontiguousarray(np.concatenate(rows)[None, :])
    p["ident"] = np.eye(128, dtype=np.float32)
    return p


SM = dict(mu_r=(0, 8), mu_k=(8, 16), bq=(16, 24), bk=(24, 32), bo=(32, 40),
          gamma=(40, 48), beta=(48, 56), b_in_u=(56, 72), b_in_g=(72, 88),
          conv_b=(88, 104), log_decay=(104, 120), b_bc_b=(120, 136),
          b_bc_c=(136, 152), b_out=(152, 160), base_b=(160, 168))
NSMALL = 168


class Sem:
    def __init__(self, h):
        self.h = h
        self.v = 0

    def inc(self, n=1):
        self.v += n
        return self.v


class Ctx:

    def __init__(self, nc, es):
        self.nc = nc
        self.es = es
        self.sems = {}
        for name in ("wl", "xl", "pe", "ac", "ve", "gp", "st", "pc"):
            self.sems[name] = Sem(es.enter_context(nc.semaphore(name)))
        self.psA = es.enter_context(nc.psum_tensor("psA", [128, 2048], F32))
        self.psB = es.enter_context(nc.psum_tensor("psB", [128, 1536], F32))
        self.bank = [self.psA[:, 512 * i:512 * (i + 1)] for i in range(4)] + \
                    [self.psB[:, 512 * i:512 * (i + 1)] for i in range(3)]
        self.bankb = es.enter_context(nc.psum_tensor("bankb", [128, 1024], BF16))

    def sb(self, name, shape, dtype):
        return self.es.enter_context(self.nc.sbuf_tensor(name, shape, dtype))


def ceil_div(a, b):
    return -(-a // b)


class E:

    ENGINES = ("tensor", "vector", "scalar", "sync", "gpsimd")

    def __init__(self, ctx, block):
        self.ctx = ctx
        self.block = block
        self.q = {e: [] for e in self.ENGINES}

    def add(self, eng, f):
        self.q[eng].append(f)

    def flush(self):
        for eng in self.ENGINES:
            steps = self.q[eng]
            if not steps:
                continue
            reg = getattr(self.block, eng)

            def body(engine, steps=steps):
                for s in steps:
                    s(engine)
            reg(body)
        self.q = {e: [] for e in self.ENGINES}


def _wait(S, waits):
    def f(engine, waits=tuple(waits)):
        for sname, val in waits:
            engine.wait_ge(S[sname].h, val)
    return f


def mm_group(ctx, em, bank_ap, pairs, waits=(), sem="pe", n_inc=1):
    S = ctx.sems

    def f(t, bank_ap=bank_ap, pairs=tuple(pairs), waits=tuple(waits)):
        for sname, val in waits:
            t.wait_ge(S[sname].h, val)
        n = len(pairs)
        inst = None
        for i, (l, r) in enumerate(pairs):
            inst = t.matmul(bank_ap, l, r, start=(i == 0), stop=(i == n - 1))
        inst.then_inc(S[sem].h, n_inc)
    em.add("tensor", f)
    return S[sem].inc(n_inc)


def emit_ln_cast(ctx, em, NTl, src_fn, src_ready=None, cast_guard=None):
    S = ctx.sems
    xc = ctx.xcastT

    def cast(v):
        for sname, val in (src_ready or ()):
            v.wait_ge(S[sname].h, val)
        for sname, val in (cast_guard or ()):
            v.wait_ge(S[sname].h, val)
        inst = None
        for c in range(8):
            inst = v.tensor_copy(xc[:, c * 512:c * 512 + NTl], src_fn(c))
        inst.then_inc(S["ve"].h, 1)
    em.add("vector", cast)
    return S["ve"].inc()


def emit_ln_tile(ctx, em, k, NTl, src_fn, mode, dst_fn=None, extra_fn=None,
                 gb=None, row_off=0, bank_sxx_free=None, bank_ab_free=None,
                 src_ready=None, rowsA=None, rowsB=None, apply_waits=None,
                 extra_ready=None, cast_eng="vector", cast_guard=None,
                 precast=None, defer_tail=False):
    S = ctx.sems
    nc = ctx.nc
    B0, B1, B2, B3 = ctx.bank[0], ctx.bank[1], ctx.bank[2], ctx.bank[3]
    xc, xq = ctx.xcastT, ctx.xsqT
    stat = ctx.statS

    csem = "ve" if cast_eng == "vector" else "gp"
    if precast is not None:
        csem, v_cast = "ve", precast
    else:
        def cast(v, k=k):
            for sname, val in (src_ready or ()):
                v.wait_ge(S[sname].h, val)
            for sname, val in (cast_guard or ()):
                v.wait_ge(S[sname].h, val)
            inst = None
            for c in range(8):
                inst = v.tensor_copy(xc[:, c * 512:c * 512 + NTl], src_fn(c))
            inst.then_inc(S[csem].h, 1)
        em.add(cast_eng, cast)
        v_cast = S[csem].inc()

    def sq(s, k=k):
        for sname, val in (src_ready or ()):
            s.wait_ge(S[sname].h, val)
        inst = None
        for c in range(8):
            inst = s.activation(xq[:, c * 512:c * 512 + NTl], src_fn(c), AF.Square)
        inst.then_inc(S["ac"].h, 1)
    em.add("scalar", sq)
    a_sq = S["ac"].inc()

    waits = [(csem, v_cast), ("ac", a_sq)]
    if bank_sxx_free is not None:
        waits.append(bank_sxx_free)
    pairs0 = [(ctx.ones_pcol_bf[:], xc[:, c * 512:c * 512 + NTl]) for c in range(8)]
    pairs1 = [(ctx.ones_pcol_bf[:], xq[:, c * 512:c * 512 + NTl]) for c in range(8)]
    mm_group(ctx, em, B0[0:1, 0:NTl], pairs0, waits=waits)
    p_stats = mm_group(ctx, em, B1[0:1, 0:NTl], pairs1)

    s0 = stat[0:1, 0:NTl]
    s1 = stat[0:1, 512:512 + NTl]
    s2 = stat[0:1, 1024:1024 + NTl]
    Arow = rowsA if rowsA is not None else ctx.rowA[0:1, row_off:row_off + NTl]

    def statmath_v(v):
        v.wait_ge(S["pe"].h, p_stats)
        v.tensor_scalar_mul(s0, B0[0:1, 0:NTl], -1.0 / 1024.0)
        v.tensor_scalar_mul(s1, B1[0:1, 0:NTl], 1.0 / 1024.0)
        v.tensor_tensor(s2, s0, s0, ALU.mult)
        v.tensor_sub(s2, s1, s2).then_inc(S["ve"].h, 1)
    em.add("vector", statmath_v)
    v_var = S["ve"].inc()

    def statmath_s(s):
        s.wait_ge(S["ve"].h, v_var)
        s.activation(s1, s2, AF.Ln, bias=ctx.epscol[0:1, 0:1])
        s.activation(Arow, s1, AF.Exp, scale=-0.5).then_inc(S["ac"].h, 1)
    em.add("scalar", statmath_s)
    a_A = S["ac"].inc()

    res = {"stats": p_stats, "cast": v_cast, "sq": a_sq, "var": v_var}

    def finish():
        waits = [("ve", v_var)]
        if bank_ab_free is not None:
            waits.append(bank_ab_free)
        mm_group(ctx, em, B3[:, 0:NTl], [(ctx.ones_row_f[:], s0)], waits=waits)
        p_bcast = mm_group(ctx, em, B2[:, 0:NTl], [(ctx.ones_row_f[:], Arow)],
                           waits=[("ac", a_A)])

        tmp = ctx.lnTmp

        def apply(v):
            v.wait_ge(S["pe"].h, p_bcast)
            for sname, val in (apply_waits or ()):
                v.wait_ge(S[sname].h, val)
            for sname, val in (extra_ready or ()):
                v.wait_ge(S[sname].h, val)
            inst = None
            for c in range(8):
                v.tensor_tensor(tmp[:, 0:NTl], src_fn(c), B3[:, 0:NTl], ALU.add)
                if mode == "bf16":
                    inst = v.tensor_tensor(dst_fn(c), tmp[:, 0:NTl], B2[:, 0:NTl], ALU.mult)
                elif mode == "pe":
                    v.tensor_tensor(tmp[:, 0:NTl], tmp[:, 0:NTl], B2[:, 0:NTl], ALU.mult)
                    inst = v.tensor_tensor(dst_fn(c), tmp[:, 0:NTl], extra_fn(c), ALU.add)
                elif mode == "gb_resid":
                    v.tensor_tensor(tmp[:, 0:NTl], tmp[:, 0:NTl], B2[:, 0:NTl], ALU.mult)
                    gcol, bcol = gb
                    v.tensor_scalar(tmp[:, 0:NTl], tmp[:, 0:NTl], gcol(c), bcol(c),
                                    op0=ALU.mult, op1=ALU.add)
                    inst = v.tensor_tensor(dst_fn(c), dst_fn(c), tmp[:, 0:NTl], ALU.add)
            inst.then_inc(S["ve"].h, 1)
        em.add("vector", apply)
        res["bcast"] = p_bcast
        res["apply"] = S["ve"].inc()
        return res

    if defer_tail:
        return res, finish
    finish()
    return res


LN_CAST_ENG = {}


def emit_l0ffn(ctx, block, scope, uid):
    nc, S = ctx.nc, ctx.sems
    NTl, ntl = 512, 4
    XS = NTl + 4
    sb = lambda n, sh, dt: scope.enter_context(nc.sbuf_tensor(f"{n}_{uid}", sh, dt))
    xlnT = [sb("xln0", [128, 8 * XS], BF16)]
    xlnT.append(xlnT[0])
    sv = sb("sv", [128, 8], BF16)
    wr = sb("wr", [128, 8 * 1024], BF16)
    wk = sb("wk", [128, 8 * 1024], BF16)
    wv = sb("wv", [128, 8 * 1024], BF16)
    xr = [sb("xr0", [128, 8 * NTl], BF16), sb("xr1", [128, 8 * NTl], BF16)]
    xk = [sb("xk0", [128, 8 * NTl], BF16), sb("xk1", [128, 8 * NTl], BF16)]
    rbuf = sb("r0", [128, 8 * NTl], BF16)
    kact = sb("ka0", [128, 8 * NTl], BF16)
    mtmp = sb("mtmp", [128, NTl], F32)
    gtmp = sb("gtmp", [128, NTl], BF16)
    otmp = mtmp

    em = E(ctx, block)

    def wload(g):
        g.dma_start(out=wr[:], in_=ctx.dram["wr"][:, :]).then_inc(S["wl"].h, 16)
        g.dma_start(out=wk[:], in_=ctx.dram["wk"][:, :]).then_inc(S["wl"].h, 16)
        g.dma_start(out=wv[:], in_=ctx.dram["wv"][:, :]).then_inc(S["wl"].h, 16)
    em.add("gpsimd", wload)
    wl_done = S["wl"].inc(48)

    def emit_precast():
        rr = list(ctx.resid_ready)

        def precast(g, rr=rr):
            for sname, val in rr:
                g.wait_ge(S[sname].h, val)
            g.wait_ge(S["wl"].h, wl_done)
            for nm16, nm, n in (("win16", "win", 32), ("wbc16", "wbc", 32),
                                ("wout16", "wout", 8)):
                for i in range(n):
                    g.dma_start(out=ctx.dram[nm16][i, :, :],
                                in_=ctx.dram[nm][i, :, :]).then_inc(S["pc"].h, 16)
        em.add("gpsimd", precast)
        S["pc"].inc(72 * 16)

    mu_r = ctx.smalls[:, SM["mu_r"][0]:SM["mu_r"][1]]
    mu_k = ctx.smalls[:, SM["mu_k"][0]:SM["mu_k"][1]]
    omu_r = ctx.smalls2[:, 0:8]
    omu_k = ctx.smalls2[:, 8:16]

    bank_state = {4: None, 5: None}
    ln_prev = [None]
    bcopy_done = {}
    mix_done = {}
    stage_end = {}
    evac_end = {}

    def emit_mix(k):
        par = k % 2

        apv = ln_prev[0]["apply"]

        def bcopy(g, k=k, apv=apv):
            g.wait_ge(S["ve"].h, apv)
            if k == 0:
                g.memset(xlnT[0][:, 3:8 * XS:XS], 0.0)
            else:
                g.tensor_copy(xlnT[0][:, 3:8 * XS:XS], sv[:])
            g.drain().then_inc(S["gp"].h, 1)
        em.add("gpsimd", bcopy)
        bcopy_done[k] = S["gp"].inc()

        def mix(v, k=k, par=par, apv=apv):
            v.wait_ge(S["gp"].h, bcopy_done[k])
            if k >= 2 and stage_end.get(k - 2):
                v.wait_ge(S["pe"].h, stage_end[k - 2]["k"])
            inst = None
            for c in range(8):
                cur = xlnT[par][:, c * XS + 4: c * XS + 4 + NTl]
                prv = xlnT[par][:, c * XS + 3: c * XS + 3 + NTl]
                v.tensor_scalar_mul(gtmp[:], cur, mu_r[:, c:c + 1])
                v.scalar_tensor_tensor(xr[par][:, c * NTl:(c + 1) * NTl], prv,
                                       omu_r[:, c:c + 1], gtmp[:], op0=ALU.mult, op1=ALU.add)
                v.tensor_scalar_mul(gtmp[:], cur, mu_k[:, c:c + 1])
                inst = v.scalar_tensor_tensor(xk[par][:, c * NTl:(c + 1) * NTl], prv,
                                              omu_k[:, c:c + 1], gtmp[:], op0=ALU.mult, op1=ALU.add)
            v.tensor_copy(sv[:], xlnT[0][:, XS - 1:8 * XS:XS])
            inst = v.drain()
            inst.then_inc(S["ve"].h, 1)
        em.add("vector", mix)
        mix_done[k] = S["ve"].inc()

    def emit_stages(k, after_r=None):
        par = k % 2
        ends = {}
        for m in range(8):
            bidx = 4 + (m % 2)
            waits = [("ve", mix_done[k]), ("wl", wl_done)]
            if bank_state[bidx]:
                waits.append(bank_state[bidx])
            pairs = [(wr[:, c * 1024 + m * 128: c * 1024 + (m + 1) * 128],
                      xr[par][:, c * NTl:(c + 1) * NTl]) for c in range(8)]
            pv = mm_group(ctx, em, ctx.bank[bidx][:, 0:NTl], pairs, waits=waits)

            def ev(s, m=m, k=k, bidx=bidx, pv=pv):
                s.wait_ge(S["pe"].h, pv)
                if m == 0 and k >= 1 and (k - 1) in evac_end:
                    s.wait_ge(S["ve"].h, evac_end[k - 1])
                s.activation(rbuf[:, m * NTl:(m + 1) * NTl],
                             ctx.bank[bidx][:, 0:NTl], AF.Sigmoid).then_inc(S["ac"].h, 1)
            em.add("scalar", ev)
            bank_state[bidx] = ("ac", S["ac"].inc())
        ends["r"] = S["pe"].v
        if after_r is not None:
            after_r()
        for m in range(8):
            bidx = 4 + (m % 2)
            waits = [("ve", mix_done[k])]
            if bank_state[bidx]:
                waits.append(bank_state[bidx])
            pairs = [(wk[:, c * 1024 + m * 128: c * 1024 + (m + 1) * 128],
                      xk[par][:, c * NTl:(c + 1) * NTl]) for c in range(8)]
            pv = mm_group(ctx, em, ctx.bank[bidx][:, 0:NTl], pairs, waits=waits)

            def ev(s, m=m, bidx=bidx, pv=pv):
                s.wait_ge(S["pe"].h, pv)
                s.activation(ctx.bank[bidx][:, 0:NTl], ctx.bank[bidx][:, 0:NTl], AF.Relu)
                s.activation(kact[:, m * NTl:(m + 1) * NTl], ctx.bank[bidx][:, 0:NTl],
                             AF.Square).then_inc(S["ac"].h, 1)
            em.add("scalar", ev)
            bank_state[bidx] = ("ac", S["ac"].inc())
        ends["k"] = S["pe"].v
        ac_kdone = S["ac"].v
        for m in range(8):
            bidx = 4 + (m % 2)
            waits = [("ac", ac_kdone)]
            if bank_state[bidx]:
                waits.append(bank_state[bidx])
            pairs = [(wv[:, c * 1024 + m * 128: c * 1024 + (m + 1) * 128],
                      kact[:, c * NTl:(c + 1) * NTl]) for c in range(8)]
            pv = mm_group(ctx, em, ctx.bank[bidx][:, 0:NTl], pairs, waits=waits)

            def ev(v, m=m, k=k, bidx=bidx, pv=pv, par=par):
                v.wait_ge(S["pe"].h, pv)
                v.tensor_tensor(otmp[:], ctx.bank[bidx][:, 0:NTl],
                                rbuf[:, m * NTl:(m + 1) * NTl], ALU.mult)
                rv = ctx.resid[:, m * T + k * NTl: m * T + k * NTl + NTl]
                v.scalar_tensor_tensor(rv, otmp[:], 0.5, rv,
                                       op0=ALU.mult, op1=ALU.add).then_inc(S["ve"].h, 1)
            em.add("vector", ev)
            bank_state[bidx] = ("ve", S["ve"].inc())
        ends["v"] = S["pe"].v
        evac_end[k] = S["ve"].v
        stage_end[k] = ends

    def src_of(k):
        return lambda c, k=k: ctx.resid[:, c * T + k * NTl: c * T + (k + 1) * NTl]

    def src_ready_of(k):
        tr = getattr(ctx, "resid_tile_ready", None)
        return tr[k] if tr else ctx.resid_ready

    cast_vals = {0: emit_ln_cast(ctx, em, NTl, src_of(0), src_ready=src_ready_of(0))}
    fins = {}
    for k in range(ntl):
        par = k % 2
        lnp = ln_prev[0]
        res, fin = emit_ln_tile(
            ctx, em, k, NTl,
            src_fn=src_of(k),
            mode="bf16",
            dst_fn=lambda c, k=k, par=par: xlnT[par][:, c * XS + 4: c * XS + 4 + NTl],
            row_off=(k % 2) * 512,
            bank_sxx_free=(("ve", lnp["var"]) if lnp else None),
            bank_ab_free=(("ve", lnp["apply"]) if lnp else None),
            src_ready=src_ready_of(k), precast=cast_vals[k],
            apply_waits=([("ve", mix_done[k - 1])] if (k - 1) in mix_done else None),
            defer_tail=True)
        ln_prev[0] = res
        fins[k] = fin
        if uid == 0 and k == ntl - 1:
            emit_precast()
        if k + 1 < ntl:
            cast_vals[k + 1] = emit_ln_cast(ctx, em, NTl, src_of(k + 1),
                                            src_ready=src_ready_of(k + 1),
                                            cast_guard=[("pe", res["stats"])])
        if k == 0:
            fins[0]()
            emit_mix(0)
        else:
            emit_stages(k - 1, after_r=(lambda k=k: (fins[k](), emit_mix(k))))
    emit_stages(ntl - 1)
    em.flush()
    ctx.resid_ready = [("ve", S["ve"].v)]
    ctx.resid_chunk_ready = None
    ctx.resid_tile_ready = None


DRAM_DECLS = [
    ("wr", [128, 8 * 1024]), ("wk", [128, 8 * 1024]), ("wv", [128, 8 * 1024]),
    ("wq", [8, 128, 8 * 128]), ("wkk", [8, 128, 8 * 128]), ("wvv", [8, 128, 8 * 128]),
    ("wo", [128, 8 * 1024]), ("peT", [1024, 2048]),
    ("win", [32, 128, 8 * 128]), ("wbc", [32, 128, 16 * 128]), ("wout", [8, 128, 16 * 128]),
    ("wconv", [128, 64]),
    ("bw", [128, 8 * 1024]), ("rm", [128, 8 * 16]), ("bflat", [64, 1024]),
    ("smalls", [128, NSMALL]), ("rows", [1, 1040]), ("ident", [128, 128]),
]


def emit_setup(ctx, block):
    nc, S = ctx.nc, ctx.sems
    em = E(ctx, block)

    def loads(sy):
        sy.dma_start(out=ctx.smalls[:], in_=ctx.dram["smalls"][:, :]).then_inc(S["xl"].h, 16)
        sy.dma_start(out=ctx.wconv[:], in_=ctx.dram["wconv"][:, :]).then_inc(S["xl"].h, 16)
    em.add("sync", loads)
    xl_v = S["xl"].inc(32)

    wl_tgt = S["wl"].v + 32

    def gloads(g, wl_tgt=wl_tgt):
        g.dma_start(out=ctx.rows_bf[:], in_=ctx.dram["rows"][:, :]).then_inc(S["wl"].h, 16)
        g.dma_start(out=ctx.idn[:], in_=ctx.dram["ident"][:, :]).then_inc(S["wl"].h, 16)
        g.memset(ctx.ones_pcol_bf[:], 1.0)
        g.memset(ctx.ones_row_f[:], 1.0)
        g.memset(ctx.ones_row_bf[:], 1.0)
        g.wait_ge(S["wl"].h, wl_tgt)
    em.add("gpsimd", gloads)
    S["wl"].inc(32)

    def vsetup(v, xl_v=xl_v):
        v.wait_ge(S["xl"].h, xl_v)
        mu_r = ctx.smalls[:, SM["mu_r"][0]:SM["mu_r"][1]]
        mu_k = ctx.smalls[:, SM["mu_k"][0]:SM["mu_k"][1]]
        v.tensor_scalar(ctx.smalls2[:, 0:8], mu_r, -1.0, 1.0, op0=ALU.mult, op1=ALU.add)
        v.tensor_scalar(ctx.smalls2[:, 8:16], mu_k, -1.0, 1.0, op0=ALU.mult, op1=ALU.add)
        v.memset(ctx.epscol[:], EPS)
        for j in range(16):
            v.reduce_sum(ctx.smalls2[:, 16 + j:17 + j], ctx.wconv[:, 4 * j:4 * j + 4], axis=AX.X)
        v.drain()
        binu_ = ctx.smalls[:, SM["b_in_u"][0]:SM["b_in_u"][1]]
        cvb_ = ctx.smalls[:, SM["conv_b"][0]:SM["conv_b"][1]]
        for j in range(16):
            v.scalar_tensor_tensor(ctx.smalls2[:, 32 + j:33 + j], binu_[:, j:j + 1],
                                   ctx.smalls2[:, 16 + j:17 + j], cvb_[:, j:j + 1],
                                   op0=ALU.mult, op1=ALU.add)
        v.memset(ctx.zcol[:], 0.0)
        v.memset(ctx.ln8col[:], -2.0794415416798357)
    em.add("vector", vsetup)

    def ssetup(s, xl_v=xl_v):
        s.wait_ge(S["xl"].h, xl_v)
        ld = ctx.smalls[:, SM["log_decay"][0]:SM["log_decay"][1]]
        s.activation(ctx.adecay[:], ld, AF.Sigmoid, scale=-1.0)
    em.add("scalar", ssetup)
    em.flush()


def emit_load(ctx, block, seq):
    S = ctx.sems
    em = E(ctx, block)
    st_base = S["st"].v - 128 if S["st"].v else 0
    first = S["st"].v == 0

    def loads(sy, st_base=st_base, first=first):
        for kt in range(4):
            for c in range(8):
                if not first and kt == 0:
                    sy.wait_ge(S["st"].h, st_base + 16 * (c + 1))
                sy.dma_start(
                    out=ctx.resid[:, c * T + kt * 512: c * T + (kt + 1) * 512],
                    in_=ctx.dram["xT"][seq, c * 128:(c + 1) * 128,
                                       kt * 512:(kt + 1) * 512]).then_inc(S["xl"].h, 16)
    em.add("sync", loads)
    xl0 = S["xl"].v
    ctx.resid_tile_ready = [[("xl", xl0 + 128 * (kt + 1))] for kt in range(4)]
    ctx.resid_ready = [("xl", S["xl"].inc(512))]
    ctx.resid_chunk_ready = None
    em.flush()


def emit_store(ctx, block, seq, last):
    S = ctx.sems
    em = E(ctx, block)
    waits = list(ctx.resid_ready)
    chunk_ready = getattr(ctx, "resid_chunk_ready", None)
    st_final = S["st"].v + 128

    def stores(sy, st_final=st_final, chunk_ready=chunk_ready):
        for c in range(8):
            if chunk_ready is not None:
                sname, val = chunk_ready[c]
                sy.wait_ge(S[sname].h, val)
            elif c == 0:
                for sname, val in waits:
                    sy.wait_ge(S[sname].h, val)
            sy.dma_start(out=ctx.dram["outT"][seq, c * 128:(c + 1) * 128, :],
                         in_=ctx.resid[:, c * T:(c + 1) * T]).then_inc(S["st"].h, 16)
        if last:
            sy.wait_ge(S["st"].h, st_final)
    em.add("sync", stores)
    S["st"].inc(128)
    em.flush()


def build_program(subs):
    nc = bass.Bass("TRN2", target_bir_lowering=False)
    dram = {"xT": nc.dram_tensor("xT", [2, 1024, 2048], F32, kind="ExternalInput").ap(),
            "outT": nc.dram_tensor("outT", [2, 1024, 2048], F32, kind="ExternalOutput").ap()}
    for name, shape in DRAM_DECLS:
        dram[name] = nc.dram_tensor(name, shape, F32, kind="ExternalInput").ap()
    for name, shape in (("win16", [32, 128, 8 * 128]), ("wbc16", [32, 128, 16 * 128]),
                        ("wout16", [8, 128, 16 * 128])):
        dram[name] = nc.dram_tensor(name, shape, BF16, kind="Internal").ap()
    with ExitStack() as es:
        ctx = Ctx(nc, es)
        ctx.dram = dram
        ctx.resid = ctx.sb("resid", [128, 8 * T], F32)
        ctx.xcastT = ctx.sb("xcastT", [128, 8 * 512], BF16)
        ctx.xsqT = ctx.sb("xsqT", [128, 8 * 512], BF16)
        ctx.statS = ctx.sb("statS", [1, 1536], F32)
        ctx.rowA = ctx.sb("rowA", [1, 1024], F32)
        ctx.rowB = ctx.sb("rowB", [1, 1024], F32)
        ctx.lnTmp = ctx.sb("lnTmp", [128, 512], F32)
        ctx.smalls = ctx.sb("sb_smalls", [128, NSMALL], F32)
        ctx.smalls2 = ctx.sb("smalls2", [128, 48], F32)
        ctx.rows_bf = ctx.sb("rows_bf", [1, 1040], BF16)
        ctx.ones_pcol_bf = ctx.sb("ones_pcol_bf", [128, 1], BF16)
        ctx.ones_row_f = ctx.sb("ones_row_f", [1, 128], F32)
        ctx.ones_row_bf = ctx.sb("ones_row_bf", [1, 128], BF16)
        ctx.idn = ctx.sb("idn", [128, 128], BF16)
        ctx.wconv = ctx.sb("sb_wconv", [128, 64], F32)
        ctx.adecay = ctx.sb("adecay", [128, 16], F32)
        ctx.epscol = ctx.sb("epscol", [128, 1], F32)
        ctx.zcol = ctx.sb("zcol", [128, 1], F32)
        ctx.ln8col = ctx.sb("ln8col", [128, 1], F32)
        ctx.carry = ctx.sb("carry", [128, 16], F32)
        ctx.tails = ctx.sb("tails", [128, 96], F32)

        with nc.Block("setup") as b:
            emit_setup(ctx, b)
        uid = 0
        for seq in range(2):
            with nc.Block(f"load{seq}") as b:
                emit_load(ctx, b, seq)
            for sname in subs:
                with nc.Block(f"{sname}_{uid}") as b:
                    with ExitStack() as scope:
                        EMITTERS[sname](ctx, b, scope, uid)
                    uid += 1
            with nc.Block(f"store{seq}") as b:
                emit_store(ctx, b, seq, last=(seq == 1))
    return nc


EMITTERS = {}

SUBS = ["l0ffn", "l0attn", "l0ffn", "l1ffn", "l1attn", "l1ffn"]
FUSED = True


def kernel(**inputs):
    prep = host_prep(inputs)
    x = inputs["L0_attn_x"]
    xTs = [np.ascontiguousarray(x[2 * c:2 * c + 2].transpose(0, 2, 1)) for c in range(8)]
    groups = [SUBS] if FUSED else [[s] for s in SUBS]
    cur = xTs
    for g in groups:
        nc = build_program(g)
        in_maps = [dict(prep, xT=cur[c]) for c in range(8)]
        res = run_bass_kernel_spmd(nc, in_maps, core_ids=list(range(8)))
        cur = [np.asarray(res.results[c]["outT"]) for c in range(8)]
    out = np.empty((16, 2048, 1024), np.float32)
    for c in range(8):
        out[2 * c:2 * c + 2] = cur[c].transpose(0, 2, 1)
    return out


def emit_l1ffn(ctx, block, scope, uid):
    nc, S = ctx.nc, ctx.sems
    NTl, ntl = 512, 4
    sb = lambda n, sh, dt: scope.enter_context(nc.sbuf_tensor(f"{n}_{uid}", sh, dt))
    xlnT = [sb("xln0", [128, 8 * NTl], BF16), sb("xln1", [128, 8 * NTl], BF16)]
    bw = sb("bw", [128, 8 * 1024], BF16)
    rm = sb("rm", [128, 8 * 16], BF16)
    bflat = sb("bflat", [64, 1024], BF16)
    rmf = sb("rmf", [128, 64], F32)
    mx = sb("mx", [128, 16], F32)
    gat = sb("gat", [128, 32], F32)
    gbf = sb("gbf", [128, 32], BF16)
    gatf = sb("gatf", [128, 32], F32)
    G = sb("G", [128, 64], BF16)
    gt = sb("gt", [64, 512], BF16)
    lora = sb("lora", [128, 8 * NTl], F32)
    bbuf = sb("bbuf", [128, 8 * NTl], F32)

    em = E(ctx, block)

    def wload(g):
        g.dma_start(out=bw[:], in_=ctx.dram["bw"][:, :]).then_inc(S["wl"].h, 16)
        g.dma_start(out=rm[:], in_=ctx.dram["rm"][:, :]).then_inc(S["wl"].h, 16)
        g.dma_start(out=bflat[:], in_=ctx.dram["bflat"][:, :]).then_inc(S["wl"].h, 16)
    em.add("gpsimd", wload)
    wl_done = S["wl"].inc(48)

    bb = ctx.smalls[:, SM["base_b"][0]:SM["base_b"][1]]
    B6 = ctx.bank[6]
    ln_prev = [None]
    stage_end = {}
    ve_end = {}

    def emit_stage(k, after_base=None):
        par = k % 2
        lnv = ln_apply[k]
        v1s = {}
        for tt in range(4):
            bidx = 4 + (tt % 2)
            waits = [("ve", lnv), ("wl", wl_done)]
            if bank_state[bidx]:
                waits.append(bank_state[bidx])
            pairs = [(xlnT[par][:, c * NTl + tt * 128: c * NTl + (tt + 1) * 128],
                      rm[:, c * 16:(c + 1) * 16]) for c in range(8)]
            pairs.append((ctx.ones_row_bf[:], ctx.rows_bf[0:1, 1024:1040]))
            pv = mm_group(ctx, em, ctx.bank[bidx][:, 0:16], pairs, waits=waits)

            def small(v, tt=tt, bidx=bidx, pv=pv):
                v.wait_ge(S["pe"].h, pv)
                v.tensor_copy(rmf[:, 16 * tt:16 * tt + 16], ctx.bank[bidx][:, 0:16])
                v.reduce_max(mx[:, 4 * tt:4 * tt + 1], ctx.bank[bidx][:, 0:8], axis=AX.X)
                v.drain()
                v.tensor_scalar_mul(mx[:, 4 * tt + 1:4 * tt + 2],
                                    mx[:, 4 * tt:4 * tt + 1], -1.0)
                v.drain().then_inc(S["ve"].h, 1)
            em.add("vector", small)
            v1s[tt] = S["ve"].inc()
            bank_state[bidx] = ("ve", v1s[tt])

        a2s = {}
        for m in range(8):
            bidx = 4 + (m % 2)
            waits = [("ve", lnv)]
            if bank_state[bidx]:
                waits.append(bank_state[bidx])
            pairs = [(bw[:, c * 1024 + m * 128: c * 1024 + (m + 1) * 128],
                      xlnT[par][:, c * NTl:(c + 1) * NTl]) for c in range(8)]
            pv = mm_group(ctx, em, ctx.bank[bidx][:, 0:NTl], pairs, waits=waits)

            def bev(s, m=m, bidx=bidx, pv=pv, rp=r1_of_m.get(m)):
                s.wait_ge(S["pe"].h, pv)
                if rp is not None:
                    s.wait_ge(S["ve"].h, rp)
                s.activation(bbuf[:, m * NTl:(m + 1) * NTl], ctx.bank[bidx][:, 0:NTl],
                             AF.Identity, bias=bb[:, m:m + 1]).then_inc(S["ac"].h, 1)
            em.add("scalar", bev)
            a2s[m] = S["ac"].inc()
            bank_state[bidx] = ("ac", a2s[m])

        def sexp(s):
            for tt in range(4):
                s.wait_ge(S["ve"].h, v1s[tt])
            inst = None
            for tt in range(4):
                inst = s.activation(gat[:, 8 * tt:8 * tt + 8],
                                    rmf[:, 16 * tt:16 * tt + 8], AF.Exp,
                                    bias=mx[:, 4 * tt + 1:4 * tt + 2])
            inst.then_inc(S["ac"].h, 1)
        em.add("scalar", sexp)
        a1 = S["ac"].inc()

        def vsum(v, a1=a1):
            v.wait_ge(S["ac"].h, a1)
            for tt in range(4):
                v.reduce_sum(mx[:, 4 * tt + 2:4 * tt + 3],
                             gat[:, 8 * tt:8 * tt + 8], axis=AX.X)
            v.drain()
            v.reciprocal(mx[:, 3:16:4], mx[:, 2:16:4])
            v.drain().then_inc(S["ve"].h, 1)
        em.add("vector", vsum)
        vsum_v = S["ve"].inc()

        gt_ready = []
        for tt in range(4):
            def small2(v, tt=tt, vsum_v=vsum_v):
                v.wait_ge(S["ve"].h, vsum_v)
                v.tensor_scalar_mul(gbf[:, 8 * tt:8 * tt + 8], gat[:, 8 * tt:8 * tt + 8],
                                    mx[:, 4 * tt + 3:4 * tt + 4])
                v.drain()
                v.tensor_copy(gatf[:, 8 * tt:8 * tt + 8], gbf[:, 8 * tt:8 * tt + 8])
                v.drain()
                inst = None
                for e in range(8):
                    inst = v.tensor_scalar_mul(G[:, 8 * e:8 * e + 8],
                                               rmf[:, 16 * tt + 8:16 * tt + 16],
                                               gatf[:, 8 * tt + e:8 * tt + e + 1])
                v.drain().then_inc(S["ve"].h, 1)
            em.add("vector", small2)
            v2 = S["ve"].inc()

            bf_guard = tuple(bankb_free[0]) if (tt == 0 and bankb_free[0]) else None

            def tp(t, tt=tt, v2=v2, bf_guard=bf_guard):
                t.wait_ge(S["ve"].h, v2)
                if bf_guard:
                    t.wait_ge(S[bf_guard[0]].h, bf_guard[1])
                t.transpose(ctx.bankb[0:64, tt * 128:(tt + 1) * 128], G[:], ctx.idn[:]
                            ).then_inc(S["pe"].h, 1)
            em.add("tensor", tp)
            p2 = S["pe"].inc()

            def gtc(v, tt=tt, p2=p2):
                v.wait_ge(S["pe"].h, p2)
                v.tensor_copy(gt[:, tt * 128:(tt + 1) * 128],
                              ctx.bankb[0:64, tt * 128:(tt + 1) * 128]).then_inc(S["ve"].h, 1)
            em.add("vector", gtc)
            gt_ready.append(S["ve"].inc())
        bankb_free[0] = ("ve", gt_ready[-1])
        if after_base is not None:
            after_base()

        lora_ready = []
        rg = rups_done.get(k - 1)
        for m in range(8):
            waits = [("ve", gt_ready[-1])]
            if bank_state[6]:
                waits.append(bank_state[6])
            pv = mm_group(ctx, em, B6[:, 0:NTl],
                          [(bflat[0:64, m * 128:(m + 1) * 128], gt[0:64, 0:NTl])],
                          waits=waits)

            def lev(v, m=m, pv=pv, rg=rg):
                v.wait_ge(S["pe"].h, pv)
                if m == 0 and rg is not None:
                    v.wait_ge(S["gp"].h, rg)
                v.tensor_copy(lora[:, m * NTl:(m + 1) * NTl], B6[:, 0:NTl]).then_inc(S["ve"].h, 1)
            em.add("vector", lev)
            bank_state[6] = ("ve", S["ve"].inc())
            lora_ready.append(bank_state[6][1])

        for m in range(8):
            def rup1(v, m=m, a2=a2s[m], lr=lora_ready[m]):
                v.wait_ge(S["ac"].h, a2)
                v.wait_ge(S["ve"].h, lr)
                lm = lora[:, m * NTl:(m + 1) * NTl]
                v.scalar_tensor_tensor(lm, bbuf[:, m * NTl:(m + 1) * NTl], 0.5, lm,
                                       op0=ALU.mult, op1=ALU.add).then_inc(S["ve"].h, 1)
            em.add("vector", rup1)
            r1 = S["ve"].inc()
            r1_of_m[m] = r1

            def rup(g, m=m, k=k, r1=r1):
                g.wait_ge(S["ve"].h, r1)
                lm = lora[:, m * NTl:(m + 1) * NTl]
                rv = ctx.resid[:, m * T + k * NTl: m * T + (k + 1) * NTl]
                g.tensor_tensor(rv, rv, lm, ALU.add).then_inc(S["gp"].h, 1)
            em.add("gpsimd", rup)
            S["gp"].inc()
            if k == ntl - 1:
                chunk_gp[m] = S["gp"].v
        rups_done[k] = S["gp"].v
        stage_end[k] = S["pe"].v

    bank_state = {4: None, 5: None, 6: None}
    bankb_free = [None]
    rups_done = {}
    r1_of_m = {}
    chunk_gp = {}
    ln_apply = {}
    def src_of(k):
        return lambda c, k=k: ctx.resid[:, c * T + k * NTl: c * T + (k + 1) * NTl]

    cast_vals = {0: emit_ln_cast(ctx, em, NTl, src_of(0), src_ready=ctx.resid_ready)}
    fins = {}

    def run_fin(k):
        fins[k]()
        ln_apply[k] = ln_res[k]["apply"]

    ln_res = {}
    for k in range(ntl):
        lnp = ln_prev[0]
        aw = [("pe", stage_end[k - 2])] if (k - 2) in stage_end else None
        res, fin = emit_ln_tile(
            ctx, em, k, NTl,
            src_fn=src_of(k),
            mode="bf16",
            dst_fn=lambda c, k=k: xlnT[k % 2][:, c * NTl:(c + 1) * NTl],
            row_off=(k % 2) * 512,
            bank_sxx_free=(("ve", lnp["var"]) if lnp else None),
            bank_ab_free=(("ve", lnp["apply"]) if lnp else None),
            src_ready=ctx.resid_ready, precast=cast_vals[k], apply_waits=aw,
            defer_tail=True)
        ln_prev[0] = res
        ln_res[k] = res
        fins[k] = fin
        if k + 1 < ntl:
            cast_vals[k + 1] = emit_ln_cast(ctx, em, NTl, src_of(k + 1),
                                            src_ready=ctx.resid_ready,
                                            cast_guard=[("pe", res["stats"])])
        if k == 0:
            run_fin(0)
        else:
            emit_stage(k - 1, after_base=(lambda k=k: run_fin(k)))
    emit_stage(ntl - 1)
    em.flush()
    ctx.resid_ready = [("gp", S["gp"].v)]
    ctx.resid_chunk_ready = [("gp", chunk_gp[m]) for m in range(8)]
    ctx.resid_tile_ready = None


def emit_l0attn(ctx, block, scope, uid):
    nc, S = ctx.nc, ctx.sems
    NTl = 512
    sb = lambda n, sh, dt: scope.enter_context(nc.sbuf_tensor(f"{n}_{uid}", sh, dt))
    xattn = sb("xattn", [128, 8 * 2048], BF16)
    xpe = sb("xpe", [128, 8 * 2048], BF16)
    em = E(ctx, block)
    bq = ctx.smalls[:, SM["bq"][0]:SM["bq"][1]]
    bk = ctx.smalls[:, SM["bk"][0]:SM["bk"][1]]
    bo = ctx.smalls[:, SM["bo"][0]:SM["bo"][1]]
    gam = ctx.smalls[:, SM["gamma"][0]:SM["gamma"][1]]
    bet = ctx.smalls[:, SM["beta"][0]:SM["beta"][1]]
    B0, B1, B2, B3, B6 = ctx.bank[0], ctx.bank[1], ctx.bank[2], ctx.bank[3], ctx.bank[6]

    with ExitStack() as sA:
        sba = lambda n, sh, dt: sA.enter_context(nc.sbuf_tensor(f"{n}_{uid}", sh, dt))
        wqb = sba("wqb", [128, 1024], BF16)
        wkb = sba("wkb", [128, 1024], BF16)
        wvb = sba("wvb", [128, 1024], BF16)
        ksb = [sba("ksb0", [128, 2048], BF16), sba("ksb1", [128, 2048], BF16)]
        vsb = [sba("vsb0", [128, 16 * 130], BF16), sba("vsb1", [128, 16 * 130], BF16)]

        B4, B5, B6 = ctx.bank[4], ctx.bank[5], ctx.bank[6]
        streg = [ctx.psA[:, 0:1024], ctx.psA[:, 1024:2048]]
        bank_state = {6: None}
        streg_state = {0: None, 1: None}
        pv_free = {0: None, 1: None}
        wl_ready = {}
        proj_end = {}
        qkv_done = {}
        mb_done = {}

        def wload(p):
            guard = proj_end.get(p - 1)

            def f(g, p=p, guard=guard):
                if guard is not None:
                    g.wait_ge(S["pe"].h, guard)
                for nm, buf in (("wq", wqb), ("wkk", wkb), ("wvv", wvb)):
                    g.dma_start(out=buf[:, 0:1024],
                                in_=ctx.dram[nm][p, :, :]).then_inc(S["wl"].h, 16)
            em.add("gpsimd", f)
            wl_ready[p] = S["wl"].inc(48)

        def vones(v):
            for par in range(2):
                v.memset(vsb[par][:, 64:16 * 130:130], 1.0)
                v.memset(vsb[par][:, 129:16 * 130:130], 1.0)
        em.add("vector", vones)
        wload(0)

        def qk_step0(wbuf, dst, bias, qt, wait_v):
            waits = [("ve", wait_v), ("wl", wl_ready[0])]
            if bank_state[6]:
                waits.append(bank_state[6])
            pairs = [(wbuf[:, c * 128:(c + 1) * 128],
                      xpe[:, c * 2048 + qt * NTl: c * 2048 + (qt + 1) * NTl])
                     for c in range(8)]
            pv = mm_group(ctx, em, B6[:, 0:NTl], pairs, waits=waits)
            proj_end[0] = pv

            def ev(v, dst=dst, bias=bias, qt=qt, pv=pv):
                v.wait_ge(S["pe"].h, pv)
                v.tensor_scalar_add(dst[:, qt * NTl:(qt + 1) * NTl],
                                    B6[:, 0:NTl], bias[:, 0:1]).then_inc(S["ve"].h, 1)
            em.add("vector", ev)
            bank_state[6] = ("ve", S["ve"].inc())
            qkv_done[0] = bank_state[6][1]

        def v_step0(tt, wait_v):
            waits = [("ve", wait_v), ("wl", wl_ready[0])]
            if bank_state[6]:
                waits.append(bank_state[6])
            pairs = [(xpe[:, c * 2048 + tt * 128: c * 2048 + (tt + 1) * 128],
                      wvb[:, c * 128:(c + 1) * 128])
                     for c in range(8)]
            pairs.append((ctx.ones_row_bf[:], ctx.rows_bf[0:1, 0:128]))
            pv = mm_group(ctx, em, B6[:, 0:128], pairs, waits=waits)
            proj_end[0] = pv

            def vev(v, tt=tt, pv=pv):
                v.wait_ge(S["pe"].h, pv)
                v.tensor_copy(vsb[0][:, tt * 130:tt * 130 + 64], B6[:, 0:64])
                v.tensor_copy(vsb[0][:, tt * 130 + 65:tt * 130 + 129],
                              B6[:, 64:128]).then_inc(S["ve"].h, 1)
            em.add("vector", vev)
            bank_state[6] = ("ve", S["ve"].inc())
            qkv_done[0] = bank_state[6][1]

        ln_prev = [None]
        pe_ready = {}
        with ExitStack() as sA1:
            pet = sA1.enter_context(nc.sbuf_tensor(f"pet_{uid}", [128, 8 * NTl], BF16))
            for k in range(4):
                ap_prev = ln_prev[0]["apply"] if ln_prev[0] else None

                def peload(g, k=k, ap_prev=ap_prev):
                    if ap_prev is not None:
                        g.wait_ge(S["ve"].h, ap_prev)
                    for c in range(8):
                        g.dma_start(out=pet[:, c * NTl:(c + 1) * NTl],
                                    in_=ctx.dram["peT"][c * 128:(c + 1) * 128,
                                                        k * NTl:(k + 1) * NTl]).then_inc(S["wl"].h, 16)
                em.add("gpsimd", peload)
                pe_ready[k] = S["wl"].inc(128)
                lnp = ln_prev[0]
                res = emit_ln_tile(
                    ctx, em, k, NTl,
                    src_fn=lambda c, k=k: ctx.resid[:, c * T + k * NTl: c * T + (k + 1) * NTl],
                    mode="pe",
                    dst_fn=lambda c, k=k: xpe[:, c * 2048 + k * NTl: c * 2048 + (k + 1) * NTl],
                    extra_fn=lambda c: pet[:, c * NTl:(c + 1) * NTl],
                    row_off=(k % 2) * 512,
                    bank_sxx_free=(("ve", lnp["var"]) if lnp else None),
                    bank_ab_free=(("ve", lnp["apply"]) if lnp else None),
                    src_ready=ctx.resid_ready,
                    extra_ready=[("wl", pe_ready[k])], cast_eng="vector",
                    cast_guard=([("pe", lnp["stats"])] if lnp else None))
                ln_prev[0] = res
                qk_step0(wkb, ksb[0], bk, k, res["apply"])
                for tt in range(4 * k, 4 * k + 4):
                    v_step0(tt, res["apply"])
        xpe_done = ln_prev[0]["apply"]
        qsb = [sba("qsb0", [128, 2048], BF16), sba("qsb1", [128, 2048], BF16)]
        pch = sba("pch", [128, 4 * 1024], BF16)
        nrm = sba("nrm", [1, 64], F32)
        mbias = sba("mbias", [128, 4], F32)
        out65x = sba("o65a", [65, 512], F32)
        out65 = [out65x, out65x]
        for qt in range(4):
            qk_step0(wqb, qsb[0], bq, qt, xpe_done)

        def make_proj_steps(p):
            par = p % 2
            steps = []
            for wbuf, dst, bias in ((wqb, qsb[par], bq), (wkb, ksb[par], bk)):
                for qt in range(4):
                    def step(wbuf=wbuf, dst=dst, bias=bias, qt=qt, p=p, par=par):
                        waits = [("ve", xpe_done), ("wl", wl_ready[p])]
                        if bank_state[6]:
                            waits.append(bank_state[6])
                        pairs = [(wbuf[:, c * 128:(c + 1) * 128],
                                  xpe[:, c * 2048 + qt * NTl: c * 2048 + (qt + 1) * NTl])
                                 for c in range(8)]
                        pv = mm_group(ctx, em, B6[:, 0:NTl], pairs, waits=waits)
                        proj_end[p] = pv

                        def ev(v, dst=dst, bias=bias, p=p, qt=qt, pv=pv):
                            v.wait_ge(S["pe"].h, pv)
                            v.tensor_scalar_add(dst[:, qt * NTl:(qt + 1) * NTl],
                                                B6[:, 0:NTl],
                                                bias[:, p:p + 1]).then_inc(S["ve"].h, 1)
                        em.add("vector", ev)
                        bank_state[6] = ("ve", S["ve"].inc())
                        qkv_done[p] = bank_state[6][1]
                    steps.append(step)
            for tt in range(16):
                def step(tt=tt, p=p, par=par):
                    waits = [("ve", xpe_done), ("wl", wl_ready[p])]
                    if bank_state[6]:
                        waits.append(bank_state[6])
                    pairs = [(xpe[:, c * 2048 + tt * 128: c * 2048 + (tt + 1) * 128],
                              wvb[:, c * 128:(c + 1) * 128])
                             for c in range(8)]
                    pairs.append((ctx.ones_row_bf[:], ctx.rows_bf[0:1, p * 128:(p + 1) * 128]))
                    pv = mm_group(ctx, em, B6[:, 0:128], pairs, waits=waits)
                    proj_end[p] = pv

                    def vev(v, tt=tt, par=par, pv=pv):
                        v.wait_ge(S["pe"].h, pv)
                        v.tensor_copy(vsb[par][:, tt * 130:tt * 130 + 64], B6[:, 0:64])
                        v.tensor_copy(vsb[par][:, tt * 130 + 65:tt * 130 + 129],
                                      B6[:, 64:128]).then_inc(S["ve"].h, 1)
                    em.add("vector", vev)
                    bank_state[6] = ("ve", S["ve"].inc())
                    qkv_done[p] = bank_state[6][1]
                steps.append(step)
            return steps

        def emit_norm(p):
            par = p % 2

            def sq(v, p=p, par=par, qd=qkv_done[p]):
                v.wait_ge(S["ve"].h, qd)
                v.tensor_tensor(ctx.xcastT[:, 0:2048], qsb[par][:], qsb[par][:], ALU.mult)
                v.tensor_tensor(ctx.xcastT[:, 2048:4096], ksb[par][:], ksb[par][:],
                                ALU.mult).then_inc(S["ve"].h, 1)
            em.add("vector", sq)
            sq_done = S["ve"].inc()
            for h2 in range(2):
                for i, base in enumerate((0, 2048)):
                    for qt in range(4):
                        waits = [("ve", sq_done)]
                        if bank_state[6]:
                            waits.append(bank_state[6])
                        pv = mm_group(ctx, em, B6[0:1, 0:NTl],
                                      [(ctx.ones_pcol_bf[64 * h2:64 * h2 + 64, :],
                                        ctx.xcastT[64 * h2:64 * h2 + 64,
                                                   base + qt * NTl: base + (qt + 1) * NTl])],
                                      waits=waits)

                        def nrd(v, h2=h2, i=i, qt=qt, pv=pv):
                            v.wait_ge(S["pe"].h, pv)
                            v.reduce_max(nrm[0:1, 8 * h2 + 4 * i + qt: 8 * h2 + 4 * i + qt + 1],
                                         B6[0:1, 0:NTl], axis=AX.X).then_inc(S["ve"].h, 1)
                        em.add("vector", nrd)
                        bank_state[6] = ("ve", S["ve"].inc())

            def mcomb_v(v):
                v.drain()
                for h2 in range(2):
                    v.reduce_max(nrm[0:1, 16 + h2:17 + h2], nrm[0:1, 8 * h2:8 * h2 + 4], axis=AX.X)
                    v.reduce_max(nrm[0:1, 18 + h2:19 + h2], nrm[0:1, 8 * h2 + 4:8 * h2 + 8], axis=AX.X)
                v.drain()
                for h2 in range(2):
                    v.tensor_tensor(nrm[0:1, 20 + h2:21 + h2], nrm[0:1, 16 + h2:17 + h2],
                                    nrm[0:1, 18 + h2:19 + h2], ALU.mult)
                v.drain().then_inc(S["ve"].h, 1)
            em.add("vector", mcomb_v)
            vq = S["ve"].inc()

            def mcomb_s(s, vq=vq):
                s.wait_ge(S["ve"].h, vq)
                s.activation(nrm[0:1, 22:24], nrm[0:1, 20:22], AF.Ln, bias=ctx.zcol[0:1, :])
                s.drain()
                s.activation(nrm[0:1, 24:26], nrm[0:1, 22:24], AF.Exp, scale=0.5,
                             bias=ctx.ln8col[0:1, :])
                s.drain().then_inc(S["ac"].h, 1)
            em.add("scalar", mcomb_s)
            aM = S["ac"].inc()

            def mneg(v, aM=aM):
                v.wait_ge(S["ac"].h, aM)
                v.tensor_scalar_mul(nrm[0:1, 26:28], nrm[0:1, 24:26], -1.0)
                v.drain().then_inc(S["ve"].h, 1)
            em.add("vector", mneg)
            vneg = S["ve"].inc()
            for h2 in range(2):
                waits = [("ve", vneg)]
                if bank_state[6]:
                    waits.append(bank_state[6])
                pv = mm_group(ctx, em, B6[:, 0:1],
                              [(ctx.ones_row_f[:], nrm[0:1, 26 + h2:27 + h2])], waits=waits)

                def mbc(v, h2=h2, par=par, pv=pv):
                    v.wait_ge(S["pe"].h, pv)
                    v.tensor_copy(mbias[:, par * 2 + h2:par * 2 + h2 + 1],
                                  B6[:, 0:1]).then_inc(S["ve"].h, 1)
                em.add("vector", mbc)
                bank_state[6] = ("ve", S["ve"].inc())
            mb_done[p] = S["ve"].v

        def emit_inner(p, fillers):
            par = p % 2
            for qt in range(4):
                expv = {}
                pvv = {}

                def emit_pv(kt, qt=qt, par=par, expv=expv, pvv=pvv):
                    for h2, pvbank in ((0, B4), (1, B5)):
                        g = pv_free[h2] if kt == 0 else None

                        def pvmm(t, kt=kt, h2=h2, pvbank=pvbank, exv=expv[(h2, kt // 2)],
                                 g=g, par=par):
                            t.wait_ge(S["ac"].h, exv)
                            if g:
                                t.wait_ge(S[g[0]].h, g[1])
                            t.matmul(pvbank[0:65, 0:NTl],
                                     vsb[par][:, kt * 130 + 65 * h2: kt * 130 + 65 * h2 + 65],
                                     pch[:, ((h2 * 2 + kt // 2 % 2) * 1024 + (kt % 2) * 512):
                                          ((h2 * 2 + kt // 2 % 2) * 1024 + (kt % 2) * 512) + NTl],
                                     start=(kt == 0), stop=(kt == 15)).then_inc(S["pe"].h, 1)
                        em.add("tensor", pvmm)
                        pvv[(h2, kt)] = S["pe"].inc()

                for kp in range(8):
                    if fillers:
                        fillers.pop(0)()
                    stv = {}
                    for h2 in range(2):
                        waits = [("ve", qkv_done[p])]
                        if streg_state[h2]:
                            waits.append(streg_state[h2])
                        pairs = []
                        stv[h2] = None
                        for half in range(2):
                            kt = 2 * kp + half
                            pr = [(ksb[par][64 * h2:64 * h2 + 64, kt * 128:(kt + 1) * 128],
                                   qsb[par][64 * h2:64 * h2 + 64, qt * NTl:(qt + 1) * NTl])]
                            w = waits if half == 0 else ()
                            stv[h2] = mm_group(ctx, em,
                                               streg[h2][:, half * 512:half * 512 + NTl],
                                               pr, waits=w)

                        def ex(s, kp=kp, h2=h2, par=par, stv=stv[h2],
                               mbd=(mb_done[p] if kp == 0 else None),
                               pvg=(pvv.get((h2, 2 * kp - 3)))):
                            s.wait_ge(S["pe"].h, stv)
                            if mbd is not None:
                                s.wait_ge(S["ve"].h, mbd)
                            if pvg is not None:
                                s.wait_ge(S["pe"].h, pvg)
                            s.activation(pch[:, (h2 * 2 + kp % 2) * 1024:
                                              (h2 * 2 + kp % 2) * 1024 + 1024],
                                         streg[h2][:], AF.Exp,
                                         bias=mbias[:, par * 2 + h2:par * 2 + h2 + 1],
                                         scale=0.125).then_inc(S["ac"].h, 1)
                        em.add("scalar", ex)
                        exv = S["ac"].inc()
                        expv[(h2, kp)] = exv
                        streg_state[h2] = ("ac", exv)
                    if kp >= 1:
                        emit_pv(2 * kp - 2)
                        emit_pv(2 * kp - 1)
                emit_pv(14)
                emit_pv(15)
                for h2, pvbank in ((0, B4), (1, B5)):
                    pv_done = pvv[(h2, 15)]

                    def cp65(v, pv_done=pv_done, pvbank=pvbank, h2=h2):
                        v.wait_ge(S["pe"].h, pv_done)
                        v.tensor_copy(out65[h2][0:64, 0:NTl], pvbank[0:64, 0:NTl])
                        v.tensor_copy(ctx.statS[0:1, 0:NTl], pvbank[64:65, 0:NTl])
                        v.drain().then_inc(S["ve"].h, 1)
                    em.add("vector", cp65)
                    vrec = S["ve"].inc()
                    pv_free[h2] = ("ve", vrec)
                    waits = [("ve", vrec)]
                    if bank_state[6]:
                        waits.append(bank_state[6])
                    pbc = mm_group(ctx, em, B6[0:64, 0:NTl],
                                   [(ctx.ones_row_f[0:1, 0:64], ctx.statS[0:1, 0:NTl])],
                                   waits=waits)

                    def srec(s, pbc=pbc):
                        s.wait_ge(S["pe"].h, pbc)
                        s.activation(ctx.lnTmp[0:64, 0:NTl], B6[0:64, 0:NTl],
                                     AF.Ln, bias=ctx.zcol[0:64, 0:1])
                        s.activation(B6[0:64, 0:NTl], ctx.lnTmp[0:64, 0:NTl],
                                     AF.Exp, scale=-1.0).then_inc(S["ac"].h, 1)
                    em.add("scalar", srec)
                    a_rec = S["ac"].inc()

                    def fin(v, p=p, h2=h2, qt=qt, a_rec=a_rec):
                        v.wait_ge(S["ac"].h, a_rec)
                        v.tensor_tensor(
                            xattn[64 * h2:64 * h2 + 64,
                                  p * 2048 + qt * NTl: p * 2048 + (qt + 1) * NTl],
                            out65[h2][0:64, 0:NTl], B6[0:64, 0:NTl],
                            ALU.mult).then_inc(S["ve"].h, 2)
                    em.add("vector", fin)
                    S["ve"].inc(2)
                    bank_state[6] = ("ve", S["ve"].v)
                if qt == 2 and p + 1 < 8:
                    while fillers:
                        fillers.pop(0)()
                    emit_norm(p + 1)

        emit_norm(0)
        for p in range(8):
            fillers = []
            if p + 1 < 8:
                wload(p + 1)
                fillers = make_proj_steps(p + 1)
            emit_inner(p, fillers)
        xattn_done = S["ve"].v

    wo = sb("wo", [128, 8 * 1024], BF16)
    h_sb = sb("h_sb", [128, 8 * NTl], F32)

    def wol(g, xattn_done=xattn_done):
        g.wait_ge(S["ve"].h, xattn_done)
        g.dma_start(out=wo[:], in_=ctx.dram["wo"][:, :]).then_inc(S["wl"].h, 16)
    em.add("gpsimd", wol)
    wo_ready = S["wl"].inc(16)

    bank_state = {4: None, 5: None}
    ln_prev2 = [None]
    for kt in range(4):
        h_ready = []
        hguard = ln_prev2[0]["apply"] if ln_prev2[0] else None
        for m in range(8):
            bidx = 4 + (m % 2)
            waits = [("ve", xattn_done), ("wl", wo_ready)]
            if bank_state[bidx]:
                waits.append(bank_state[bidx])
            pairs = [(wo[:, c * 1024 + m * 128: c * 1024 + (m + 1) * 128],
                      xattn[:, c * 2048 + kt * NTl: c * 2048 + (kt + 1) * NTl])
                     for c in range(8)]
            pv = mm_group(ctx, em, ctx.bank[bidx][:, 0:NTl], pairs, waits=waits)

            def hev(s, m=m, bidx=bidx, pv=pv, hguard=hguard):
                s.wait_ge(S["pe"].h, pv)
                if m == 0 and hguard is not None:
                    s.wait_ge(S["ve"].h, hguard)
                s.activation(h_sb[:, m * NTl:(m + 1) * NTl], ctx.bank[bidx][:, 0:NTl],
                             AF.Identity, bias=bo[:, m:m + 1]).then_inc(S["ac"].h, 1)
            em.add("scalar", hev)
            a1 = S["ac"].inc()
            bank_state[bidx] = ("ac", a1)

            def hadd(v, m=m, kt=kt, a1=a1):
                v.wait_ge(S["ac"].h, a1)
                hv = h_sb[:, m * NTl:(m + 1) * NTl]
                v.tensor_tensor(hv, hv, xpe[:, m * 2048 + kt * NTl: m * 2048 + (kt + 1) * NTl],
                                ALU.add).then_inc(S["ve"].h, 1)
            em.add("vector", hadd)
            h_ready.append(S["ve"].inc())
        lnp = ln_prev2[0]
        res = emit_ln_tile(
            ctx, em, kt, NTl,
            src_fn=lambda c: h_sb[:, c * NTl:(c + 1) * NTl],
            mode="gb_resid",
            dst_fn=lambda c, kt=kt: ctx.resid[:, c * T + kt * NTl: c * T + (kt + 1) * NTl],
            gb=(lambda c: gam[:, c:c + 1], lambda c: bet[:, c:c + 1]),
            row_off=(kt % 2) * 512,
            bank_sxx_free=(("ve", lnp["var"]) if lnp else None),
            bank_ab_free=(("ve", lnp["apply"]) if lnp else None),
            src_ready=[("ve", h_ready[-1])])
        ln_prev2[0] = res
    em.flush()
    ctx.resid_ready = [("gp", S["gp"].v)]
    ctx.resid_chunk_ready = None
    ctx.resid_tile_ready = None



def emit_l1attn(ctx, block, scope, uid):
    nc, S = ctx.nc, ctx.sems
    NTl, ntl = 512, 4
    sb = lambda n, sh, dt: scope.enter_context(nc.sbuf_tensor(f"{n}_{uid}", sh, dt))
    xlnT = [sb("xln0", [128, 8 * NTl], BF16), sb("xln1", [128, 8 * NTl], BF16)]
    uc = sb("uc", [128, 16 * NTl], BF16)
    sg = sb("sg", [128, 16 * NTl], BF16)
    yg = sb("yg", [128, 16 * NTl], BF16)
    winb = sb("winb", [128, 2 * 1024], BF16)
    wbcb = sb("wbcb", [128, 4 * 2048], BF16)
    woutb = sb("woutb", [128, 2 * 2048], BF16)
    cacc = sb("cacc", [128, NTl], F32)
    ssb = sb("ssb", [128, NTl], F32)
    ysb = sb("ysb", [128, NTl], F32)
    bsb = sb("bsb", [128, NTl], F32)
    otm = sb("otm", [128, NTl], F32)

    em = E(ctx, block)
    bing = ctx.smalls[:, SM["b_in_g"][0]:SM["b_in_g"][1]]
    bbcb = ctx.smalls[:, SM["b_bc_b"][0]:SM["b_bc_b"][1]]
    bbcc = ctx.smalls[:, SM["b_bc_c"][0]:SM["b_bc_c"][1]]
    bout = ctx.smalls[:, SM["b_out"][0]:SM["b_out"][1]]
    sbias = ctx.smalls2[:, 32:48]

    def vinit(v):
        v.memset(ctx.carry[:], 0.0)
        v.memset(ctx.tails[:], 0.0).then_inc(S["ve"].h, 1)
    em.add("vector", vinit)
    init_done = S["ve"].inc()

    wl_ready = {}

    pc_tgt = S["pc"].v

    def wdma(key, src_ap, dst_ap, guard):
        def f(g, src_ap=src_ap, dst_ap=dst_ap, guard=guard):
            if pc_tgt:
                g.wait_ge(S["pc"].h, pc_tgt)
            if guard is not None:
                g.wait_ge(S["pe"].h, guard)
            g.dma_start(out=dst_ap, in_=src_ap).then_inc(S["wl"].h, 16)
        em.add("gpsimd", f)
        wl_ready[key] = S["wl"].inc(16)

    def win_slot(i):
        return winb[:, (i % 2) * 1024:(i % 2 + 1) * 1024]

    def wbc_slot(i):
        return wbcb[:, (i % 4) * 2048:(i % 4 + 1) * 2048]

    def wout_slot(i):
        return woutb[:, (i % 2) * 2048:(i % 2 + 1) * 2048]

    bank_state = {4: None, 5: None}
    ln_prev = [None]
    ln_apply = {}
    stage_end = {}
    bc_end = {}
    yg_done_t = {}
    wout_end = {}

    def emit_stage(tc, after_u=None):
        par = tc % 2
        lnv = ln_apply[tc]
        silu_prev = [None]
        for j in range(16):
            jb = j % 2
            waits = [("ve", lnv), ("wl", wl_ready[("win", j, tc)])]
            if bank_state[4 + jb]:
                waits.append(bank_state[4 + jb])
            pairs = [(win_slot(j)[:, c * 128:(c + 1) * 128],
                      xlnT[par][:, c * NTl:(c + 1) * NTl]) for c in range(8)]
            pv = mm_group(ctx, em, ctx.bank[4 + jb][:, 0:NTl], pairs, waits=waits)
            if j + 2 < 16:
                wdma(("win", j + 2, tc), ctx.dram["win16"][j + 2, :, :], win_slot(j), pv)
            else:
                wdma(("wgate", j + 2 - 16, tc), ctx.dram["win16"][16 + j + 2 - 16, :, :],
                     win_slot(j), pv)

            wj = [ctx.wconv[:, 4 * j + i:4 * j + i + 1] for i in range(4)]
            tlr = ctx.tails[:, (tc % 2) * 48 + 3 * j:(tc % 2) * 48 + 3 * j + 3]
            tlw = ctx.tails[:, (1 - tc % 2) * 48 + 3 * j:(1 - tc % 2) * 48 + 3 * j + 3]
            bank = ctx.bank[4 + jb]

            def conv(v, j=j, tc=tc, pv=pv, wj=wj, tlr=tlr, tlw=tlw, bank=bank, sp=silu_prev[0]):
                v.wait_ge(S["pe"].h, pv)
                if tc == 0 and j == 0:
                    v.wait_ge(S["ve"].h, init_done)
                if sp is not None:
                    v.wait_ge(S["ac"].h, sp)
                v.tensor_scalar_mul(cacc[:], bank[:, 0:NTl], wj[3])
                v.scalar_tensor_tensor(cacc[:, 1:NTl], bank[:, 0:NTl - 1], wj[2],
                                       cacc[:, 1:NTl], op0=ALU.mult, op1=ALU.add)
                v.scalar_tensor_tensor(cacc[:, 2:NTl], bank[:, 0:NTl - 2], wj[1],
                                       cacc[:, 2:NTl], op0=ALU.mult, op1=ALU.add)
                v.scalar_tensor_tensor(cacc[:, 3:NTl], bank[:, 0:NTl - 3], wj[0],
                                       cacc[:, 3:NTl], op0=ALU.mult, op1=ALU.add)
                v.scalar_tensor_tensor(cacc[:, 0:3], tlr, wj[0], cacc[:, 0:3],
                                       op0=ALU.mult, op1=ALU.add)
                v.drain()
                v.scalar_tensor_tensor(cacc[:, 0:2], tlr[:, 1:3], wj[1], cacc[:, 0:2],
                                       op0=ALU.mult, op1=ALU.add)
                v.drain()
                v.scalar_tensor_tensor(cacc[:, 0:1], tlr[:, 2:3], wj[2], cacc[:, 0:1],
                                       op0=ALU.mult, op1=ALU.add)
                v.tensor_copy(tlw, bank[:, NTl - 3:NTl])
                v.drain().then_inc(S["ve"].h, 1)
            em.add("vector", conv)
            cv = S["ve"].inc()
            bank_state[4 + jb] = ("ve", cv)

            def silu(s, j=j, tc=tc, cv=cv, bce=bc_end.get(tc - 1)):
                s.wait_ge(S["ve"].h, cv)
                if j == 0 and bce is not None:
                    s.wait_ge(S["pe"].h, bce)
                s.activation(uc[:, j * NTl:(j + 1) * NTl], cacc[:], AF.Silu,
                             bias=sbias[:, j:j + 1]).then_inc(S["ac"].h, 1)
            em.add("scalar", silu)
            silu_prev[0] = S["ac"].inc()
        uc_done_ac = S["ac"].v
        if after_u is not None:
            after_u()

        for j in range(16):
            jb = j % 2
            waits = [("ve", lnv), ("wl", wl_ready[("wgate", j, tc)])]
            if bank_state[4 + jb]:
                waits.append(bank_state[4 + jb])
            pairs = [(win_slot(j)[:, c * 128:(c + 1) * 128],
                      xlnT[par][:, c * NTl:(c + 1) * NTl]) for c in range(8)]
            pv = mm_group(ctx, em, ctx.bank[4 + jb][:, 0:NTl], pairs, waits=waits)
            if j + 2 < 16:
                wdma(("wgate", j + 2, tc), ctx.dram["win16"][16 + j + 2, :, :], win_slot(j), pv)
            elif j == 14:
                wdma(("wbc_b", 0, tc), ctx.dram["wbc16"][0, :, :], wbc_slot(0), pv)
                wdma(("wbc_c", 0, tc), ctx.dram["wbc16"][16, :, :], wbc_slot(1), pv)
            elif j == 15:
                wdma(("wbc_b", 1, tc), ctx.dram["wbc16"][1, :, :], wbc_slot(2), pv)
                wdma(("wbc_c", 1, tc), ctx.dram["wbc16"][17, :, :], wbc_slot(3), pv)

            def gev(s, j=j, tc=tc, jb=jb, pv=pv, ygd=yg_done_t.get(tc - 1)):
                s.wait_ge(S["pe"].h, pv)
                if j == 0 and ygd is not None:
                    s.wait_ge(S["ve"].h, ygd)
                s.activation(sg[:, j * NTl:(j + 1) * NTl], ctx.bank[4 + jb][:, 0:NTl],
                             AF.Sigmoid, bias=bing[:, j:j + 1]).then_inc(S["ac"].h, 1)
            em.add("scalar", gev)
            bank_state[4 + jb] = ("ac", S["ac"].inc())

        scan_prev = [None]
        for j in range(16):
            waits = [("ac", uc_done_ac), ("wl", wl_ready[("wbc_b", j, tc)])]
            if bank_state[4]:
                waits.append(bank_state[4])
            pairs = [(wbc_slot(2 * (j % 2))[:, ci * 128:(ci + 1) * 128],
                      uc[:, ci * NTl:(ci + 1) * NTl]) for ci in range(16)]
            pvb = mm_group(ctx, em, ctx.bank[4][:, 0:NTl], pairs, waits=waits)
            if j + 2 < 16:
                wdma(("wbc_b", j + 2, tc), ctx.dram["wbc16"][j + 2, :, :],
                     wbc_slot(2 * (j % 2)), pvb)

            def bev(s, j=j, pvb=pvb, sp=scan_prev[0]):
                s.wait_ge(S["pe"].h, pvb)
                if sp is not None:
                    s.wait_ge(S["ve"].h, sp)
                s.activation(bsb[:], ctx.bank[4][:, 0:NTl], AF.Identity,
                             bias=bbcb[:, j:j + 1]).then_inc(S["ac"].h, 1)
            em.add("scalar", bev)
            b_ready = S["ac"].inc()
            bank_state[4] = ("ac", b_ready)

            waits = [("ac", uc_done_ac), ("wl", wl_ready[("wbc_c", j, tc)])]
            if bank_state[5]:
                waits.append(bank_state[5])
            pairs = [(wbc_slot(2 * (j % 2) + 1)[:, ci * 128:(ci + 1) * 128],
                      uc[:, ci * NTl:(ci + 1) * NTl]) for ci in range(16)]
            pvc = mm_group(ctx, em, ctx.bank[5][:, 0:NTl], pairs, waits=waits)
            if j + 2 < 16:
                wdma(("wbc_c", j + 2, tc), ctx.dram["wbc16"][16 + j + 2, :, :],
                     wbc_slot(2 * (j % 2) + 1), pvc)

            def scan(v, j=j, tc=tc, b_ready=b_ready, pvc=pvc, we=wout_end.get(tc - 1)):
                v.wait_ge(S["ac"].h, b_ready)
                v.wait_ge(S["pe"].h, pvc)
                if j == 0 and we is not None:
                    v.wait_ge(S["pe"].h, we)
                a_b = ctx.adecay[:, j:j + 1].broadcast_to([128, NTl])
                v.tensor_tensor_scan(ssb[:], a_b, bsb[:], ctx.carry[:, j:j + 1],
                                     op0=ALU.mult, op1=ALU.add)
                v.tensor_tensor(ysb[:], ssb[:], ctx.bank[5][:, 0:NTl], ALU.mult)
                v.scalar_tensor_tensor(ysb[:], ssb[:], bbcc[:, j:j + 1], ysb[:],
                                       op0=ALU.mult, op1=ALU.add)
                v.tensor_tensor(yg[:, j * NTl:(j + 1) * NTl], ysb[:],
                                sg[:, j * NTl:(j + 1) * NTl], ALU.mult)
                v.tensor_copy(ctx.carry[:, j:j + 1], ssb[:, NTl - 1:NTl])
                v.drain().then_inc(S["ve"].h, 1)
            em.add("vector", scan)
            scan_prev[0] = S["ve"].inc()
            bank_state[5] = ("ve", scan_prev[0])
        yg_done = S["ve"].v
        yg_done_t[tc] = yg_done
        bc_end[tc] = S["pe"].v

        rup_prev = [None]
        for m in range(8):
            mb = m % 2
            waits = [("ve", yg_done), ("wl", wl_ready[("wout", m, tc)])]
            if bank_state[4 + mb]:
                waits.append(bank_state[4 + mb])
            pairs = [(wout_slot(m)[:, ci * 128:(ci + 1) * 128],
                      yg[:, ci * NTl:(ci + 1) * NTl]) for ci in range(16)]
            pv = mm_group(ctx, em, ctx.bank[4 + mb][:, 0:NTl], pairs, waits=waits)
            if m + 2 < 8:
                wdma(("wout", m + 2, tc), ctx.dram["wout16"][m + 2, :, :], wout_slot(m), pv)
            elif tc + 1 < ntl:
                if m == 6:
                    wdma(("win", 0, tc + 1), ctx.dram["win16"][0, :, :], win_slot(0), pv)
                    wdma(("win", 1, tc + 1), ctx.dram["win16"][1, :, :], win_slot(1), pv)
                else:
                    wdma(("wout", 0, tc + 1), ctx.dram["wout16"][0, :, :], wout_slot(0), pv)
                    wdma(("wout", 1, tc + 1), ctx.dram["wout16"][1, :, :], wout_slot(1), pv)

            def oev(s, m=m, mb=mb, pv=pv, rp=rup_prev[0]):
                s.wait_ge(S["pe"].h, pv)
                if rp is not None:
                    s.wait_ge(S["ve"].h, rp)
                s.activation(otm[:], ctx.bank[4 + mb][:, 0:NTl], AF.Identity,
                             bias=bout[:, m:m + 1]).then_inc(S["ac"].h, 1)
            em.add("scalar", oev)
            a1 = S["ac"].inc()
            bank_state[4 + mb] = ("ac", a1)

            def rup(v, m=m, tc=tc, a1=a1):
                v.wait_ge(S["ac"].h, a1)
                rv = ctx.resid[:, m * T + tc * NTl: m * T + (tc + 1) * NTl]
                v.tensor_tensor(rv, rv, otm[:], ALU.add).then_inc(S["ve"].h, 1)
            em.add("vector", rup)
            rup_prev[0] = S["ve"].inc()
        stage_end[tc] = S["pe"].v
        wout_end[tc] = S["pe"].v

    wdma(("win", 0, 0), ctx.dram["win16"][0, :, :], win_slot(0), None)
    wdma(("win", 1, 0), ctx.dram["win16"][1, :, :], win_slot(1), None)
    wdma(("wout", 0, 0), ctx.dram["wout16"][0, :, :], wout_slot(0), None)
    wdma(("wout", 1, 0), ctx.dram["wout16"][1, :, :], wout_slot(1), None)

    def src_of(k):
        return lambda c, k=k: ctx.resid[:, c * T + k * NTl: c * T + (k + 1) * NTl]

    cast_vals = {0: emit_ln_cast(ctx, em, NTl, src_of(0), src_ready=ctx.resid_ready)}
    fins = {}
    ln_res = {}

    def run_fin(k):
        fins[k]()
        ln_apply[k] = ln_res[k]["apply"]

    for k in range(ntl):
        lnp = ln_prev[0]
        aw = [("pe", stage_end[k - 2])] if (k - 2) in stage_end else None
        res, fin = emit_ln_tile(
            ctx, em, k, NTl,
            src_fn=src_of(k),
            mode="bf16",
            dst_fn=lambda c, k=k: xlnT[k % 2][:, c * NTl:(c + 1) * NTl],
            row_off=(k % 2) * 512,
            bank_sxx_free=(("ve", lnp["var"]) if lnp else None),
            bank_ab_free=(("ve", lnp["apply"]) if lnp else None),
            src_ready=ctx.resid_ready, precast=cast_vals[k], apply_waits=aw,
            defer_tail=True)
        ln_prev[0] = res
        ln_res[k] = res
        fins[k] = fin
        if k + 1 < ntl:
            cast_vals[k + 1] = emit_ln_cast(ctx, em, NTl, src_of(k + 1),
                                            src_ready=ctx.resid_ready,
                                            cast_guard=[("pe", res["stats"])])
        if k == 0:
            run_fin(0)
        else:
            emit_stage(k - 1, after_u=(lambda k=k: run_fin(k)))
    emit_stage(ntl - 1)
    em.flush()
    ctx.resid_ready = [("ve", S["ve"].v)]
    ctx.resid_chunk_ready = None
    ctx.resid_tile_ready = None


EMITTERS.update({"l0ffn": emit_l0ffn, "l1ffn": emit_l1ffn, "l0attn": emit_l0attn, "l1attn": emit_l1attn})
